# version 06
# baseline (speedup 1.0000x reference)
import numpy as np
import ml_dtypes
import contextlib
import concourse.bass as bass
import concourse.mybir as mybir
from concourse.bass_utils import run_bass_kernel_spmd

F32 = mybir.dt.float32
BF16 = mybir.dt.bfloat16
I32 = mybir.dt.int32
AF = mybir.ActivationFunctionType
ALU = mybir.AluOpType
AX = mybir.AxisListType
BFD = ml_dtypes.bfloat16

NCORE = 8
TOKC = 1024
NSLAB = 8
D = 2048
NFC = 16
INTER = 1408
NIC = 11
NEXP = 18
EPS = 1e-5
SPARSE = True
CCAP = 256
GROWS = 16 * CCAP
CUT = None


class _CutDone(Exception):
    pass
SEM_ROLES = ("ldx", "ldw", "st", "dbgst", "cw", "t", "v", "s", "p")


class Em:
    ENGS = ("sync", "gpsimd", "tensor", "vector", "scalar")

    def __init__(self):
        self.q = {e: [] for e in self.ENGS}
        self.cnt = {r: 0 for r in SEM_ROLES}
        self.last_wait = {}

    def ev(self, role):
        return (role, self.cnt[role])

    def op(self, eng, fn, waits=(), inc=None, by=1):
        w = []
        for wt in waits:
            if wt is None:
                continue
            role, val = wt
            if val <= 0:
                continue
            key = (eng, role)
            if self.last_wait.get(key, 0) < val:
                self.last_wait[key] = val
                w.append((role, val))
        ev = None
        if inc is not None:
            self.cnt[inc] += by
            ev = (inc, self.cnt[inc])
        self.q[eng].append((fn, w, inc, by))
        return ev

    def dma(self, eng, fn, waits=(), inc=None):
        return self.op(eng, fn, waits, inc, by=16)


def host_layout(inputs):
    g = {}
    x = np.ascontiguousarray(np.asarray(inputs["L0_attn_x"], dtype=np.float32).reshape(8192, D))
    g["x_shards"] = [np.ascontiguousarray(x[c * TOKC:(c + 1) * TOKC]) for c in range(NCORE)]

    Wqkv = np.asarray(inputs["L0_attn_Wqkv"], dtype=np.float32)
    bqkv = np.asarray(inputs["L0_attn_bqkv"], dtype=np.float32)
    wq = np.zeros((16, 128, 128), np.float32)
    wk = np.zeros((32, 128, 128), np.float32)
    wv = np.zeros((16, 128, 128), np.float32)
    for j in range(16):
        wq[j, 0:64, 0:64] = Wqkv[2 * j, :, 0:64]
        wq[j, 64:128, 64:128] = Wqkv[2 * j + 1, :, 0:64]
        wv[j, 0:64, 0:64] = Wqkv[2 * j, :, 128:192]
        wv[j, 64:128, 64:128] = Wqkv[2 * j + 1, :, 128:192]
    for m in range(32):
        h = (m % 2) * 64
        wk[m, h:h + 64, 0:64] = Wqkv[m, :, 64:128]
        wk[m, h:h + 64, 64:128] = Wqkv[m, :, 64:128]
    g["wq"], g["wk"], g["wv"] = wq, wk, wv
    bqc = np.zeros((128, 16), np.float32)
    bkc = np.zeros((128, 32), np.float32)
    bvc = np.zeros((128, 16), np.float32)
    for j in range(16):
        bqc[0:64, j] = bqkv[2 * j, 0:64]
        bqc[64:128, j] = bqkv[2 * j + 1, 0:64]
        bvc[0:64, j] = bqkv[2 * j, 128:192]
        bvc[64:128, j] = bqkv[2 * j + 1, 128:192]
    for m in range(32):
        bkc[0:64, m] = bqkv[m, 64:128]
        bkc[64:128, m] = bqkv[m, 64:128]
    g["bqc"], g["bkc"], g["bvc"] = bqc, bkc, bvc

    wout_ = np.asarray(inputs["L0_attn_Wout"], dtype=np.float32)
    g["wout"] = np.ascontiguousarray(np.concatenate([wout_, wout_], axis=0))
    bout = np.asarray(inputs["L0_attn_bout"], dtype=np.float32)
    g["bout2"] = np.ascontiguousarray(np.concatenate([bout, bout])[:, None])

    gate_w = np.asarray(inputs["L0_ffn_gate_w"], dtype=np.float32)
    g["gate"] = np.ascontiguousarray(gate_w.T.reshape(NFC, 128, 16).transpose(1, 0, 2))

    def lay_w13(w):
        nic = w.shape[0] // 128
        return w.reshape(nic, 128, NFC, 128).transpose(0, 3, 2, 1)

    def lay_w2(w2):
        nic = w2.shape[1] // 128
        return w2.T.reshape(nic, 128, 4, 512).transpose(2, 0, 1, 3)

    w1 = np.asarray(inputs["L0_ffn_w1"], dtype=np.float32)
    w3 = np.asarray(inputs["L0_ffn_w3"], dtype=np.float32)
    w2 = np.asarray(inputs["L0_ffn_w2"], dtype=np.float32)
    sw1 = np.asarray(inputs["L0_ffn_sw1"], dtype=np.float32)
    sw3 = np.asarray(inputs["L0_ffn_sw3"], dtype=np.float32)
    sw2 = np.asarray(inputs["L0_ffn_sw2"], dtype=np.float32)

    w1_all = np.empty((NEXP, NIC, 128, NFC, 128), np.float32)
    w3_all = np.empty((NEXP, NIC, 128, NFC, 128), np.float32)
    w2_all = np.empty((NEXP, 4, NIC, 128, 512), np.float32)
    w1_all[0] = lay_w13(sw1[0:INTER]);      w1_all[1] = lay_w13(sw1[INTER:])
    w3_all[0] = lay_w13(sw3[0:INTER]);      w3_all[1] = lay_w13(sw3[INTER:])
    w2_all[0] = lay_w2(sw2[:, 0:INTER]);    w2_all[1] = lay_w2(sw2[:, INTER:])
    for e in range(16):
        w1_all[2 + e] = lay_w13(w1[e])
        w3_all[2 + e] = lay_w13(w3[e])
        w2_all[2 + e] = lay_w2(w2[e])
    g["w1_all"] = np.ascontiguousarray(w1_all)
    g["w3_all"] = np.ascontiguousarray(w3_all)
    g["w2_all"] = np.ascontiguousarray(w2_all)
    return g


SB_BASE = 16768
SB_TOP = 229344

def _mk_map():
    M = {}
    off = SB_BASE
    def A(name, nbytes):
        nonlocal off
        off = (off + 31) // 32 * 32
        M[name] = off
        off += nbytes
    A("idf32", 512); A("idb16", 256); A("id64x2", 128)
    A("wout", 128); A("gate", 512)
    A("bqc", 64); A("bkc", 128); A("bvc", 64); A("bout2", 32)
    A("wall", 512); A("stats", 256); A("gsc", 256); A("cst", 32)
    A("rt_rts", 384); A("rt_rts16", 32); A("rt_rtsi", 8); A("rt_metg", 32)
    A("rt_dsti", 8); A("rt_b0row", 64); A("rt_cntr", 64); A("rt_ecb", 64)
    A("rt_iocol", 32); A("rt_lt", 256); A("rt_o1r", 512); A("rt_o1c", 32)
    A("scr", 8192)
    A("xslab", 16384); A("oslab", 16384)
    perm_end = off
    off = perm_end
    A("wqkv", 16384)
    A("tfmA", 8192)
    A("Qb", 16384); A("Kb", 32768); A("Vb", 16384)
    A("Sb", 16384); A("Eb", 16384); A("Etb", 16384)
    A("zf", 512); A("rz", 512); A("ub", 512); A("u16", 256)
    A("ctx16", 4096); A("afm", 8192); A("atok", 8192)
    attn_end = off
    off = perm_end
    A("tfm", 32768)
    A("hst", 22528)
    A("w13", 16384)
    A("w2q", 22528)
    A("yac", 65536)
    A("sil", 4096)
    ffn_end = off
    assert max(attn_end, ffn_end) <= SB_TOP, (attn_end, ffn_end)
    return M

MM = _mk_map()


def build_nc(dbg=False, phases=4):
    nc = bass.Bass("TRN2", target_bir_lowering=False, detect_race_conditions=False)

    d_x = nc.dram_tensor("x_in", [TOKC, D], F32, kind="ExternalInput").ap()
    d_wq = nc.dram_tensor("wq", [16, 128, 128], F32, kind="ExternalInput").ap()
    d_wk = nc.dram_tensor("wk", [32, 128, 128], F32, kind="ExternalInput").ap()
    d_wv = nc.dram_tensor("wv", [16, 128, 128], F32, kind="ExternalInput").ap()
    d_bqc = nc.dram_tensor("bqc", [128, 16], F32, kind="ExternalInput").ap()
    d_bkc = nc.dram_tensor("bkc", [128, 32], F32, kind="ExternalInput").ap()
    d_bvc = nc.dram_tensor("bvc", [128, 16], F32, kind="ExternalInput").ap()
    d_wout = nc.dram_tensor("wout", [128, 64], F32, kind="ExternalInput").ap()
    d_bout2 = nc.dram_tensor("bout2", [128, 1], F32, kind="ExternalInput").ap()
    d_gate = nc.dram_tensor("gate", [128, NFC, 16], F32, kind="ExternalInput").ap()
    d_w1 = nc.dram_tensor("w1_all", [NEXP, NIC, 128, NFC, 128], F32, kind="ExternalInput").ap()
    d_w3 = nc.dram_tensor("w3_all", [NEXP, NIC, 128, NFC, 128], F32, kind="ExternalInput").ap()
    d_w2 = nc.dram_tensor("w2_all", [NEXP, 4, NIC, 128, 512], F32, kind="ExternalInput").ap()
    d_w1b = nc.dram_tensor("w1b", [NEXP, NIC, 128, NFC, 128], BF16, kind="Internal").ap()
    d_w3b = nc.dram_tensor("w3b", [NEXP, NIC, 128, NFC, 128], BF16, kind="Internal").ap()
    d_w2b = nc.dram_tensor("w2b", [NEXP, 4, NIC, 128, 512], BF16, kind="Internal").ap()
    d_out = nc.dram_tensor("out", [TOKC, D], F32, kind="ExternalOutput").ap()
    d_x2 = nc.dram_tensor("x2spill", [TOKC, D], F32, kind="Internal").ap()
    d_gb = nc.dram_tensor("gbuf", [GROWS, D], BF16, kind="Internal").ap()
    d_yb = nc.dram_tensor("ybuf", [3 * TOKC, D], F32, kind="Internal").ap()
    d_md = nc.dram_tensor("mbuf", [GROWS, 4], F32, kind="Internal").ap()

    c_idf32 = nc.inline_tensor(np.eye(128, dtype=np.float32), name="c_idf32").ap()
    c_idb16 = nc.inline_tensor(np.eye(128, dtype=BFD), name="c_idb16").ap()
    c_id64x2 = nc.inline_tensor(
        np.concatenate([np.eye(64), np.eye(64)], axis=0).astype(BFD), name="c_id64x2").ap()
    c_lt = nc.inline_tensor(np.triu(np.ones((128, 128))).astype(BFD), name="c_lt").ap()
    c_o1r = nc.inline_tensor(np.ones((1, 128), np.float32), name="c_o1r").ap()
    c_o1c = nc.inline_tensor(np.ones((128, 1), dtype=BFD), name="c_o1c").ap()
    c_ecb = nc.inline_tensor(np.tile(np.arange(16, dtype=np.float32) * CCAP, (128, 1)),
                             name="c_ecb").ap()
    c_iot = nc.inline_tensor(np.arange(128, dtype=np.float32)[:, None], name="c_iot").ap()
    c_z16 = nc.inline_tensor(np.zeros((128, D), dtype=BFD), name="c_z16").ap()
    c_z32 = nc.inline_tensor(np.zeros((128, D), np.float32), name="c_z32").ap()
    minit_np = np.zeros((GROWS, 4), np.float32); minit_np[:, 1] = 1e5
    c_mi = nc.inline_tensor(minit_np, name="c_mi").ap()
    CONSTS = [EPS, -0.5, 0.125]
    c_cst = nc.inline_tensor(
        np.tile(np.array([CONSTS], dtype=np.float32), (128, 1)), name="c_cst").ap()

    T = {}
    def sb(name, shape, dtype):
        T[name] = nc.alloc_sbuf_tensor_at(name, shape, dtype, offset=MM[name])
        return T[name]

    sb("idf32", [128, 128], F32); sb("idb16", [128, 128], BF16)
    T["lt"] = nc.alloc_sbuf_tensor_at("lt", [128, 128], BF16, offset=MM["rt_lt"])
    T["o1r"] = nc.alloc_sbuf_tensor_at("o1r", [1, 128], F32, offset=MM["rt_o1r"])
    T["o1c"] = nc.alloc_sbuf_tensor_at("o1c", [128, 1], BF16, offset=MM["rt_o1c"])
    sb("id64x2", [128, 64], BF16)
    sb("wout", [128, 64], BF16); sb("gate", [128, NFC * 16], BF16)
    sb("bqc", [128, 16], F32); sb("bkc", [128, 32], F32); sb("bvc", [128, 16], F32)
    sb("bout2", [128, 1], F32)
    sb("wall", [128, 128], F32)
    sb("cst", [128, 3], F32)
    for ci_, cv_ in enumerate([EPS, -0.5, 0.125]):
        nc.const_aps.register(F32, cv_, T["cst"][:, ci_:ci_ + 1])
    sb("stats", [128, 64], F32)
    sb("gsc", [128, 64], F32)
    sb("scr", [128, 2048], F32)
    sb("xslab", [128, 2 * 2048], F32)
    sb("oslab", [128, 2 * 2048], F32)
    sb("wqkv", [128, 64 * 128], BF16)
    sb("tfmA", [128, 2 * 2048], BF16)
    sb("Qb", [128, 4096], BF16); sb("Kb", [128, 8192], BF16); sb("Vb", [128, 4096], F32)
    sb("Sb", [128, 4096], BF16)
    sb("Eb", [128, 2 * 4096], BF16); sb("Etb", [128, 2 * 4096], BF16)
    sb("zf", [128, 128], F32); sb("rz", [128, 128], F32)
    sb("ub", [128, 128], F32); sb("u16", [128, 128], BF16)
    sb("ctx16", [128, 2048], BF16)
    sb("afm", [128, 2048], F32); sb("atok", [128, 2048], F32)
    T["ctx16b"] = nc.alloc_sbuf_tensor_at("ctx16b", [64, 2048], BF16, offset=MM["atok"])
    sb("tfm", [128, NFC * 1024], BF16)
    sb("hst", [128, NIC * 1024], BF16)
    sb("w13", [128, 2 * 2 * 2048], BF16)
    sb("w2q", [128, 2 * NIC * 512], BF16)
    sb("yac", [128, NSLAB * 2048], F32)
    sb("sil", [128, 2 * 512], F32)
    R = MM["yac"]
    def sbat(name, off, shape, dtype):
        T[name] = nc.alloc_sbuf_tensor_at(name, shape, dtype, offset=off)
    sbat("t16tok", R, [128, 2 * 2048], BF16)
    sbat("metasrc", R + 8192, [128, 8], F32)
    sbat("tfs", R, [128, 2 * 4096], BF16)
    sbat("gsl", R + 16384, [128, 2 * 2048], F32)
    sbat("ysc", R + 32768, [128, 2 * 2048], F32)
    sbat("yfull", R + 49152, [128, 2 * 2048], F32)
    for nm, shp, dt in [("rts", [128, 96], F32), ("rts16", [128, 16], BF16),
                        ("rtsi", [128, 2], I32), ("metg", [128, 8], F32),
                        ("dsti", [128, 2], I32), ("b0row", [1, 16], F32),
                        ("cntr", [1, 16], F32), ("ecb", [128, 16], F32),
                        ("iocol", [128, 1], F32)]:
        T[nm] = nc.alloc_sbuf_tensor_at(nm, shp, dt, offset=MM["rt_" + nm])

    pb = [nc.alloc_psum_tensor(f"pb{i}", [128, 512], F32) for i in range(8)]

    em = Em()
    BCREG = {}
    dbg_outs = {}
    if dbg:
        for name, shape in [("dbg_tfmA", [128, 2048]), ("dbg_Qb", [128, 4096]),
                            ("dbg_Kb", [128, 8192]), ("dbg_Vb", [128, 4096]),
                            ("dbg_ctx16", [128, 2048]),
                            ("dbg_atok", [128, 2048]), ("dbg_x2", [TOKC, D]),
                            ("dbg_tfm", [128, NFC * 1024]), ("dbg_hst", [128, NIC * 1024]),
                            ("dbg_yac0", [128, 2048]), ("dbg_wall", [128, 128])]:
            dbg_outs[name] = nc.dram_tensor(name, shape, F32, kind="ExternalOutput").ap()

    def dbg_cap(name, src_ap, waits):
        if not dbg:
            return None
        return em.dma("gpsimd", lambda g, s=src_ap, o=dbg_outs[name]:
                      g.dma_start(out=o, in_=s), waits=waits, inc="dbgst")

    ev_consts = None
    for fn in [
        lambda g: g.dma_start(out=T["cst"][:, :], in_=c_cst),
        lambda g: g.dma_start(out=T["lt"][:, :], in_=c_lt),
        lambda g: g.dma_start(out=T["o1r"][:, :], in_=c_o1r),
        lambda g: g.dma_start(out=T["o1c"][:, :], in_=c_o1c),
        lambda g: g.dma_start(out=T["ecb"][:, :], in_=c_ecb),
        lambda g: g.dma_start(out=T["iocol"][:, :], in_=c_iot),
        lambda g: g.dma_start(out=T["idf32"][:, :], in_=c_idf32),
        lambda g: g.dma_start(out=T["idb16"][:, :], in_=c_idb16),
        lambda g: g.dma_start(out=T["id64x2"][:, :], in_=c_id64x2),
        lambda g: g.dma_start(out=T["wout"][:, :], in_=d_wout),
        lambda g: g.dma_start(out=T["gate"][:, :], in_=d_gate.rearrange("p a e -> p (a e)")),
        lambda g: g.dma_start(out=T["bqc"][:, :], in_=d_bqc),
        lambda g: g.dma_start(out=T["bkc"][:, :], in_=d_bkc),
        lambda g: g.dma_start(out=T["bvc"][:, :], in_=d_bvc),
        lambda g: g.dma_start(out=T["bout2"][:, :], in_=d_bout2),
        lambda g: g.dma_start(out=T["wqkv"][:, 0:2048].rearrange("p (a q) -> p a q", a=16),
                              in_=d_wq.rearrange("a p q -> p a q")),
        lambda g: g.dma_start(out=T["wqkv"][:, 2048:6144].rearrange("p (a q) -> p a q", a=32),
                              in_=d_wk.rearrange("a p q -> p a q")),
        lambda g: g.dma_start(out=T["wqkv"][:, 6144:8192].rearrange("p (a q) -> p a q", a=16),
                              in_=d_wv.rearrange("a p q -> p a q")),
    ]:
        ev_consts = em.dma("gpsimd", fn, inc="ldw")

    ev_xload = [None] * NSLAB
    ev_xfree = [None] * NSLAB
    ev_ofree = [None] * NSLAB

    def load_x(s):
        par = s % 2
        w = [ev_xfree[s - 2]] if s >= 2 else []
        ev_xload[s] = em.dma(
            "sync", lambda e, s=s, par=par: e.dma_start(
                out=T["xslab"][:, par * 2048:(par + 1) * 2048],
                in_=d_x[s * 128:(s + 1) * 128, :]), waits=w, inc="ldx")

    ev_gbz = None
    ev_ybz = None
    ev_mdz = None
    if SPARSE:
        for r0 in range(0, GROWS, 128):
            ev_gbz = em.dma("sync", lambda e_, r0=r0: e_.dma_start(
                out=d_gb[r0:r0 + 128, :], in_=c_z16), inc="st")
        for r0 in range(0, 2 * TOKC, 128):
            ev_ybz = em.dma("sync", lambda e_, r0=r0: e_.dma_start(
                out=d_yb[r0:r0 + 128, :], in_=c_z32), inc="st")
        ev_mdz = em.dma("sync", lambda e_: e_.dma_start(out=d_md, in_=c_mi), inc="st")

    load_x(0); load_x(1)

    SC = {"sx": 0, "ssq": 1, "mean": 2, "msq": 3, "var": 4, "lnv": 5, "rstd": 6, "t7": 7,
          "gmax": 8, "gneg": 9, "gsum": 10, "grs": 11, "gm1": 12, "gm2": 13}

    def stat(k):
        return T["stats"][:, SC[k]:SC[k] + 1]

    scr_guard = [[]]

    def ln_to_scr(X_ap, waits_in):
        wl = list(waits_in)
        ev_sq = em.op("scalar", lambda s, X=X_ap: s.activation(
            out=T["scr"][:, :], in_=X, func=AF.Square),
            waits=wl + scr_guard[0], inc="s")
        em.op("vector", lambda v, X=X_ap: v.tensor_reduce(
            out=stat("sx"), in_=X, axis=AX.X, op=ALU.add), waits=wl)
        ev_red = em.op("vector", lambda v: v.tensor_reduce(
            out=stat("ssq"), in_=T["scr"][:, :], axis=AX.X, op=ALU.add),
            waits=[ev_sq], inc="v")
        em.op("gpsimd", lambda g: g.tensor_scalar_mul(
            out=stat("mean"), in0=stat("sx"), scalar1=1.0 / D), waits=[ev_red])
        em.op("gpsimd", lambda g: g.tensor_scalar_mul(
            out=stat("msq"), in0=stat("mean"), scalar1=stat("mean")))
        em.op("gpsimd", lambda g: g.tensor_scalar_mul(
            out=stat("t7"), in0=stat("ssq"), scalar1=1.0 / D))
        ev_var = em.op("gpsimd", lambda g: g.tensor_sub(
            out=stat("var"), in0=stat("t7"), in1=stat("msq")), inc="p")
        em.op("scalar", lambda s: s.activation(
            out=stat("lnv"), in_=stat("var"), func=AF.Ln, bias=EPS), waits=[ev_var])
        ev_rstd = em.op("scalar", lambda s: s.activation(
            out=stat("rstd"), in_=stat("lnv"), func=AF.Exp, scale=-0.5), inc="s")
        ev_tf = em.op("vector", lambda v, X=X_ap: v.tensor_scalar(
            out=T["scr"][:, :], in0=X, scalar1=stat("mean"), scalar2=stat("rstd"),
            op0=ALU.subtract, op1=ALU.mult), waits=[ev_rstd] + scr_guard[0], inc="v")
        return ev_tf

    bank_w = {}
    bank_r = {}

    def transpose16(dest_ap_fn, src_tensor, waits_src, extra_evac_waits=()):
        ev = None
        for grpi in range(4):
            bank = grpi % 2
            j0 = grpi * 4
            for jj in range(4):
                j = j0 + jj
                em.op("tensor", lambda t, j=j, bank=bank, jj=jj, src=src_tensor:
                      t.transpose(pb[bank][:, jj * 128:(jj + 1) * 128],
                                  src[:, j * 128:(j + 1) * 128], T["idf32"][:, :]),
                      waits=(list(waits_src) if j == 0 else []) +
                            ([bank_r.get(bank)] if jj == 0 else []),
                      inc="t" if jj == 3 else None)
            bank_w[bank] = em.ev("t")
            ev = em.op("scalar", lambda s_, bank=bank, j0=j0:
                       s_.activation(out=dest_ap_fn(j0), in_=pb[bank][:, :].rearrange(
                           "p (a b) -> p a b", a=4), func=AF.Copy),
                       waits=[bank_w[bank]] + list(extra_evac_waits), inc="s")
            bank_r[bank] = ev
        return ev, em.ev("t")

    def cut(name):
        if CUT == name:
            raise _CutDone()

    ev_cw1 = {}; ev_cw3 = {}; ev_cw2 = {}
    cast_next = [0]

    def emit_casts(n):
        for _ in range(n):
            e = cast_next[0]
            if e >= NEXP:
                return
            cast_next[0] = e + 1
            ev_cw1[e] = em.dma("gpsimd", lambda g, e=e: g.dma_start(
                out=d_w1b[e].rearrange("i p a q -> (i p) (a q)"),
                in_=d_w1[e].rearrange("i p a q -> (i p) (a q)")), inc="cw")
            ev_cw3[e] = em.dma("gpsimd", lambda g, e=e: g.dma_start(
                out=d_w3b[e].rearrange("i p a q -> (i p) (a q)"),
                in_=d_w3[e].rearrange("i p a q -> (i p) (a q)")), inc="cw")
            ev_cw2[e] = em.dma("gpsimd", lambda g, e=e: g.dma_start(
                out=d_w2b[e].rearrange("a i p d -> (a i p) d"),
                in_=d_w2[e].rearrange("a i p d -> (a i p) d")), inc="cw")

    emit_casts(3)

    ev_spill = [None] * NSLAB
    ev_prev_inner_v = None
    ev_qkv_lastmm = None
    roll_exp = [None, None]
    roll_zmm = {}
    roll_emul = {}
    roll_emul_p = {}
    roll_ctx = {}
    roll_fold = None
    prev_ln3 = None

    try:
      for grp in range(NSLAB // 2):
        s0, s1 = 2 * grp, 2 * grp + 1
        ev_tfmA = [None, None]
        for s in (s0, s1):
            par = s % 2
            X = T["xslab"][:, par * 2048:(par + 1) * 2048]
            ev_tf = ln_to_scr(X, [ev_xload[s], ev_consts])
            dest = lambda j0, par=par: T["tfmA"][
                :, par * 2048 + j0 * 128: par * 2048 + (j0 + 4) * 128].rearrange(
                "p (a b) -> p a b", a=4)
            ev_tfmA[par], lastT = transpose16(dest, T["scr"], [ev_tf],
                                              extra_evac_waits=[ev_qkv_lastmm])
            scr_guard[0] = [("t", lastT[1])]
            cut("trans0")
        tfmA2 = T["tfmA"][:, :].rearrange("p (h j t) -> p h j t", h=2, j=16)
        ev_qkvevac = None
        qi = 0
        for kind, n_tiles, wbase, bias_t in (("q", 16, 0, "bqc"), ("k", 32, 16, "bkc"),
                                             ("v", 16, 48, "bvc")):
            for idx in range(n_tiles):
                bank = 2 + (qi % 2)
                j = idx // 2 if kind == "k" else idx
                lhs = T["wqkv"][:, (wbase + idx) * 128:(wbase + idx + 1) * 128]
                rhs = tfmA2[:, :, j, :]
                ev_mm = em.op("tensor", lambda t, bank=bank, lhs=lhs, rhs=rhs:
                              t.matmul(pb[bank][:, 0:256], lhs, rhs, start=True, stop=True),
                              waits=[ev_tfmA[0], ev_tfmA[1], bank_r.get(bank),
                                     ev_prev_inner_v], inc="t")
                bank_w[bank] = ev_mm
                if kind == "q":
                    dst = T["Qb"][:, idx * 256:(idx + 1) * 256]
                    src = pb[bank][:, 0:256]
                elif kind == "v":
                    dst = T["Vb"][:, idx * 256:(idx + 1) * 256]
                    src = pb[bank][:, 0:256]
                else:
                    dst = T["Kb"][:, :].rearrange("p (h r) -> p h r", h=2)[
                        :, :, idx * 128:(idx + 1) * 128]
                    src = pb[bank][:, 0:256].rearrange("p (h t) -> p h t", h=2)
                bias = T[bias_t][:, idx:idx + 1]
                ev_cp = em.op("scalar", lambda s_, src=src, dst=dst:
                              s_.activation(out=dst, in_=src, func=AF.Copy),
                              waits=[ev_mm], inc="s")
                ev_e = em.op("gpsimd", lambda g_, dst=dst, bias=bias:
                             g_.tensor_scalar_add(out=dst, in0=dst, scalar1=bias),
                             waits=[ev_cp], inc="p")
                bank_r[bank] = ev_cp
                ev_qkvevac = ev_e
                qi += 1
        ev_qkv_lastmm = em.ev("t")
        cut("qkv")

        for s in (s0, s1):
            par = s % 2
            kb_slab = T["Kb"][:, par * 4096:(par + 1) * 4096].rearrange(
                "p (m t) -> p m t", m=32)
            ev_ctx15 = None
            for cp in range(16):
                ep = cp % 2
                qsl = T["Qb"][:, cp * 256 + par * 128: cp * 256 + par * 128 + 128]
                ev_smul_h = [None, None]
                ev_exp_h = [None, None]
                for h in range(2):
                    kb_h = kb_slab[:, h * 16:(h + 1) * 16, :]
                    ev_smul_h[h] = em.op(
                        "vector", lambda v, qsl=qsl, kb_h=kb_h, h=h: v.tensor_tensor(
                            out=T["Sb"][:, h * 2048:(h + 1) * 2048].rearrange(
                                "p (m t) -> p m t", m=16),
                            in0=kb_h, in1=qsl.unsqueeze(1).broadcast_to((128, 16, 128)),
                            op=ALU.mult),
                        waits=[ev_qkvevac, roll_exp[h]], inc="v")
                    ev_exp_h[h] = em.op(
                        "scalar", lambda s_, ep=ep, h=h: s_.activation(
                            out=T["Eb"][:, ep * 4096 + h * 2048:
                                        ep * 4096 + (h + 1) * 2048],
                            in_=T["Sb"][:, h * 2048:(h + 1) * 2048],
                            func=AF.Exp, scale=0.125),
                        waits=[ev_smul_h[h], roll_zmm.get(ep),
                               (roll_emul_p if h == 0 else roll_emul).get(ep)], inc="s")
                    roll_exp[h] = ev_exp_h[h]
                ev_exp = ev_exp_h[1]
                cut("exp0")
                ev_zmm = None
                for j8 in range(8):
                    ev_zmm = em.op(
                        "tensor", lambda t, ep=ep, j8=j8: t.matmul(
                            pb[3][:, :], T["idb16"][:, :],
                            T["Eb"][:, ep * 4096 + j8 * 512: ep * 4096 + (j8 + 1) * 512],
                            start=(j8 == 0), stop=(j8 == 7)),
                        waits=([ev_exp_h[0], roll_fold, bank_r.get(3)] if j8 == 0 else
                               ([ev_exp_h[1]] if j8 == 4 else [])),
                        inc="t" if j8 == 7 else None)
                roll_zmm[ep] = ev_zmm
                bank_w[3] = ev_zmm
                cut("zmm0")
                ev_fold = em.op(
                    "vector", lambda v: v.tensor_reduce(
                        out=T["zf"][:, :], in_=pb[3][:, :].rearrange("p (q t) -> p t q", q=4),
                        axis=AX.X, op=ALU.add),
                    waits=[ev_zmm], inc="v")
                roll_fold = ev_fold
                bank_r[3] = ev_fold
                cut("fold0")
                em.op("vector", lambda v: v.reciprocal(out=T["rz"][:, :], in_=T["zf"][:, :]))
                vsl = T["Vb"][:, cp * 256 + par * 128: cp * 256 + par * 128 + 128]
                ev_u16 = em.op("vector", lambda v, vsl=vsl: v.tensor_mul(
                    out=T["u16"][:, :], in0=vsl, in1=T["rz"][:, :]), inc="v")
                ev_emul_p = em.op(
                    "gpsimd", lambda g, ep=ep: g.tensor_tensor(
                        out=T["Etb"][:, ep * 4096:ep * 4096 + 2048].rearrange(
                            "p (m t) -> p m t", m=16),
                        in0=T["Eb"][:, ep * 4096:ep * 4096 + 2048].rearrange(
                            "p (m t) -> p m t", m=16),
                        in1=T["u16"][:, :].unsqueeze(1).broadcast_to((128, 16, 128)),
                        op=ALU.mult),
                    waits=[ev_u16, ev_exp_h[0], roll_ctx.get(ep)], inc="p")
                roll_emul_p[ep] = ev_emul_p
                ev_emul = em.op(
                    "vector", lambda v, ep=ep: v.tensor_tensor(
                        out=T["Etb"][:, ep * 4096 + 2048:(ep + 1) * 4096].rearrange(
                            "p (m t) -> p m t", m=16),
                        in0=T["Eb"][:, ep * 4096 + 2048:(ep + 1) * 4096].rearrange(
                            "p (m t) -> p m t", m=16),
                        in1=T["u16"][:, :].unsqueeze(1).broadcast_to((128, 16, 128)),
                        op=ALU.mult),
                    waits=[ev_exp, roll_ctx.get(ep)], inc="v")
                roll_emul[ep] = ev_emul
                cut("emul0")
                ev_ctx = None
                for q8 in range(8):
                    bank = 4 + q8 // 2
                    hp = (q8 % 2) * 64
                    ev_ctx = em.op(
                        "tensor", lambda t, ep=ep, q8=q8, bank=bank, hp=hp, cp=cp: t.matmul(
                            pb[bank][hp:hp + 64, :], T["id64x2"][:, :],
                            T["Etb"][:, ep * 4096 + q8 * 512: ep * 4096 + (q8 + 1) * 512],
                            start=(cp == 0), stop=(cp == 15),
                            skip_group_check=True,
                            tile_position=(0, hp) if hp else None),
                        waits=([ev_emul_p] if q8 == 0 else
                               ([ev_emul] if q8 == 4 else [])) +
                              ([bank_r.get(bank)] if cp == 0 else []),
                        inc="t" if q8 == 7 else None)
                roll_ctx[ep] = ev_ctx
                cut("ctx0")
                cut("cp1" if cp >= 1 else "_never")
                if cp == 15:
                    ev_ctx15 = ev_ctx
                    for b in range(4, 8):
                        bank_w[b] = ev_ctx15
            cut("inner")
            ev_prev_inner_v = roll_emul[1]
            ev_ctxe = None
            for b in range(4):
                ev_ctxe = em.op("vector", lambda v, b=b: v.tensor_copy(
                    out=T["ctx16"][:, b * 512:(b + 1) * 512], in_=pb[4 + b][:, :]),
                    waits=[ev_ctx15], inc="v")
                bank_r[4 + b] = ev_ctxe
            ev_cb = em.dma("sync", lambda e_: e_.dma_start(
                out=T["ctx16b"][:, :], in_=T["ctx16"][64:128, :]),
                waits=[ev_ctxe, prev_ln3], inc="ldx")
            ev_opmm = None
            for m in range(32):
                q = m // 4
                c = m % 4
                jpair = m // 2
                bank = 4 + jpair // 4
                cc = jpair % 4
                hp = (m % 2) * 64
                cols = slice((q // 2) * 512 + c * 128, (q // 2) * 512 + (c + 1) * 128)
                if q % 2 == 0:
                    rhs = T["ctx16"][0:64, cols]
                else:
                    rhs = T["ctx16b"][0:64, cols]
                ev_opmm = em.op(
                    "tensor", lambda t, bank=bank, hp=hp, cc=cc, rhs=rhs: t.matmul(
                        pb[bank][hp:hp + 64, cc * 128:(cc + 1) * 128],
                        T["wout"][0:64, :], rhs, start=True, stop=True,
                        tile_position=(0, hp)),
                    waits=[ev_ctxe, ev_cb] + ([bank_r.get(bank)] if m < 8 else []), inc="t")
            for b in range(4, 8):
                bank_w[b] = ev_opmm
            ev_afm = None
            for b in range(4):
                ev_afm = em.op("vector", lambda v_, b=b: v_.tensor_scalar_add(
                    out=T["afm"][:, b * 512:(b + 1) * 512], in0=pb[4 + b][:, :],
                    scalar1=T["bout2"][:, :]),
                    waits=[ev_opmm], inc="v")
                bank_r[4 + b] = ev_afm
            cut("afm")
            dest = lambda j0: T["atok"][:, j0 * 128:(j0 + 4) * 128].rearrange(
                "p (a b) -> p a b", a=4)
            ev_atok, lastT = transpose16(dest, T["afm"], [ev_afm])
            cut("atok")
            ev_ln3 = ln_to_scr(T["atok"][:, :], [ev_atok])
            prev_ln3 = ev_ln3
            scr_guard[0] = []
            X = T["xslab"][:, par * 2048:(par + 1) * 2048]
            O = T["oslab"][:, par * 2048:(par + 1) * 2048]
            ev_x2 = em.op("vector", lambda v, X=X, O=O: v.tensor_add(
                out=O, in0=X, in1=T["scr"][:, :]),
                waits=[ev_ln3, ev_ofree[s - 2] if s >= 2 else None], inc="v")
            scr_guard[0] = [ev_x2]
            ev_xfree[s] = ev_x2
            cut("resid")
            ev_spill[s] = em.dma("sync", lambda e, s=s, O=O: e.dma_start(
                out=d_x2[s * 128:(s + 1) * 128, :], in_=O), waits=[ev_x2], inc="st")
            emit_casts(2)
            cut("spill")
            ev_ofree[s] = ev_spill[s]
            if s + 2 < NSLAB:
                load_x(s + 2)

        if dbg and grp == 0:
            ev_c1 = dbg_cap("dbg_tfmA", T["tfmA"][:, 0:2048], [ev_tfmA[0]])
            ev_c2 = dbg_cap("dbg_Qb", T["Qb"][:, :], [ev_qkvevac])
            ev_c3 = dbg_cap("dbg_Kb", T["Kb"][:, :], [ev_qkvevac])
            ev_c4 = dbg_cap("dbg_Vb", T["Vb"][:, :], [ev_qkvevac])
            ev_c5 = dbg_cap("dbg_ctx16", T["ctx16"][:, :], [em.ev("v")])
            ev_c6 = dbg_cap("dbg_atok", T["atok"][:, :], [em.ev("v")])
            scr_guard[0] = [("dbgst", ev_c6[1])]

    except _CutDone:
        em.dma("sync", lambda e_: e_.dma_start(out=d_out[0:128, :],
                                               in_=T["xslab"][:, 0:2048]), inc="st")
    if phases == 1:
        for s in range(NSLAB):
            if CUT is not None:
                break
            em.dma("sync", lambda e_, s=s: e_.dma_start(
                out=d_out[s * 128:(s + 1) * 128, :],
                in_=d_x2[s * 128:(s + 1) * 128, :]),
                waits=[ev_spill[s]], inc="st")
    if phases >= 2:
      ev_gate = [None] * NSLAB
      prev_ev3 = None
      p2_xfree = [None] * NSLAB
      ev_tfm_done = None
      t16free = [None, None]
      metafree = None
      rtsifree = None
      b0_ev = None
      ev_scat = None
      if SPARSE:
          em.op("gpsimd", lambda g: g.memset(T["metasrc"][:, :], 0.0))
          b0_ev = em.op("gpsimd", lambda g: g.memset(T["b0row"][:, :], 0.0), inc="p")
      for s in range(NSLAB):
          par = s % 2
          wfree = p2_xfree[s - 2] if s >= 2 else ev_xfree[NSLAB - 2 + par]
          ev_x2load = em.dma("sync", lambda e, s=s, par=par: e.dma_start(
              out=T["xslab"][:, par * 2048:(par + 1) * 2048],
              in_=d_x2[s * 128:(s + 1) * 128, :]),
              waits=[ev_spill[s], wfree], inc="ldx")
          X = T["xslab"][:, par * 2048:(par + 1) * 2048]
          ev_tf = ln_to_scr(X, [ev_x2load])
          p2_xfree[s] = ev_tf
          ev_t16cp = None
          if SPARSE:
              ev_t16cp = em.op("scalar", lambda s_, par=par: s_.activation(
                  out=T["t16tok"][:, par * 2048:(par + 1) * 2048], in_=T["scr"][:, :],
                  func=AF.Copy), waits=[ev_tf, t16free[par]], inc="s")
          dest = lambda j0, s=s: T["tfm"][:, :].rearrange(
              "p (j t) -> p j t", j=16)[:, j0:j0 + 4, s * 128:(s + 1) * 128]
          ev_tfm_done, lastT = transpose16(dest, T["scr"], [ev_tf])
          scr_guard[0] = [("t", lastT[1])] + ([ev_t16cp] if SPARSE else [])
          ev_gmm = None
          for fc in range(NFC):
              lhs = T["tfm"][:, fc * 1024 + s * 128: fc * 1024 + (s + 1) * 128]
              ev_gmm = em.op("tensor", lambda t, fc=fc, lhs=lhs: t.matmul(
                  pb[2][:, 0:16], lhs, T["gate"][:, fc * 16:(fc + 1) * 16],
                  start=(fc == 0), stop=(fc == NFC - 1)),
                  waits=([ev_tfm_done, bank_r.get(2)] if fc == 0 else []),
                  inc="t" if fc == NFC - 1 else None)
          bank_w[2] = ev_gmm
          gl = T["gsc"][:, 0:16]; ge_ = T["gsc"][:, 16:32]; msk = T["gsc"][:, 32:48]
          eqm = T["gsc"][:, 48:64]
          wsl = T["wall"][:, s * 16:(s + 1) * 16]
          ev1 = em.op("vector", lambda v, gl=gl: v.tensor_copy(out=gl, in_=pb[2][:, 0:16]),
                      waits=[ev_gmm, prev_ev3], inc="v")
          bank_r[2] = ev1
          evv1 = em.op("vector", lambda v, gl=gl: v.tensor_reduce(
              out=stat("gmax"), in_=gl, axis=AX.X, op=ALU.max), inc="v")
          evp1 = em.op("gpsimd", lambda g: g.tensor_scalar_mul(
              out=stat("gneg"), in0=stat("gmax"), scalar1=-1.0),
              waits=[evv1, prev_ev3], inc="p")
          ev3 = em.op("scalar", lambda s_, gl=gl, ge_=ge_: s_.activation(
              out=ge_, in_=gl, func=AF.Exp, bias=stat("gneg")), waits=[evp1], inc="s")
          prev_ev3 = ev3
          evv2 = em.op("vector", lambda v, ge_=ge_: v.tensor_reduce(
              out=stat("gsum"), in_=ge_, axis=AX.X, op=ALU.add), waits=[ev3])
          evv2 = em.op("vector", lambda v, ge_=ge_: v.tensor_reduce(
              out=stat("gm1"), in_=ge_, axis=AX.X, op=ALU.max), inc="v")
          em.op("gpsimd", lambda g, ge_=ge_, eqm=eqm: g.tensor_single_scalar(
              out=eqm, in_=ge_, scalar=stat("gm1"), op=ALU.is_equal), waits=[evv2])
          em.op("gpsimd", lambda g, eqm=eqm: g.tensor_scalar_mul(
              out=eqm, in0=eqm, scalar1=-1e30))
          em.op("gpsimd", lambda g: g.tensor_copy(out=stat("t7"), in_=stat("gsum")))
          evp2 = em.op("gpsimd", lambda g, ge_=ge_, eqm=eqm, msk=msk: g.tensor_add(
              out=msk, in0=eqm, in1=ge_), inc="p")
          evv3 = em.op("vector", lambda v, msk=msk: v.tensor_reduce(
              out=stat("gm2"), in_=msk, axis=AX.X, op=ALU.max), waits=[evp2], inc="v")
          em.op("gpsimd", lambda g, ge_=ge_, eqm=eqm: g.tensor_single_scalar(
              out=eqm, in_=ge_, scalar=stat("gm2"), op=ALU.is_ge), waits=[evv3])
          evp3 = em.op("gpsimd", lambda g, ge_=ge_, eqm=eqm, msk=msk: g.tensor_mul(
              out=msk, in0=ge_, in1=eqm), inc="p")
          evv4 = em.op("vector", lambda v: v.reciprocal(
              out=stat("grs"), in_=stat("t7")), waits=[evp3], inc="v")
          ev_gate[s] = em.op("gpsimd", lambda g, msk=msk, wsl=wsl: g.tensor_scalar_mul(
              out=wsl, in0=msk, scalar1=stat("grs")), waits=[evv4], inc="p")

          if SPARSE:
              r_ = T["rts"]
              m1w = r_[:, 32:33]; tot = r_[:, 33:34]; wB = r_[:, 34:35]
              msk0 = r_[:, 0:16]; eq1 = r_[:, 16:32]; srow0 = r_[:, 48:64]
              slotf = r_[:, 64:80]; tmpz = r_[:, 80:96]
              evm = em.op("vector", lambda v, wsl=wsl: v.tensor_reduce(
                  out=m1w, in_=wsl, axis=AX.X, op=ALU.max), waits=[ev_gate[s]])
              evm = em.op("vector", lambda v, wsl=wsl: v.tensor_reduce(
                  out=tot, in_=wsl, axis=AX.X, op=ALU.add), inc="v")
              em.op("gpsimd", lambda g, wsl=wsl: g.tensor_single_scalar(
                  out=msk0, in_=wsl, scalar=0.0, op=ALU.is_gt), waits=[evm])
              em.op("gpsimd", lambda g, wsl=wsl: g.tensor_single_scalar(
                  out=eq1, in_=wsl, scalar=m1w, op=ALU.is_equal))
              em.op("gpsimd", lambda g: g.tensor_sub(out=wB, in0=tot, in1=m1w))
              evmk = em.op("gpsimd", lambda g: g.tensor_copy(
                  out=T["rts16"][:, :], in_=msk0), inc="p")
              em.op("tensor", lambda t: t.matmul(
                  pb[3][:, 0:16], T["lt"][:, :], T["rts16"][:, :],
                  start=True, stop=False),
                  waits=[evmk, bank_r.get(3)], inc="t")
              em.op("tensor", lambda t: t.matmul(
                  pb[3][:, 0:16], T["o1r"][:, :], T["b0row"][:, :],
                  start=False, stop=True), waits=[b0_ev])
              evsb = em.op("tensor", lambda t: t.matmul(
                  pb[3][0:1, 32:48], T["o1c"][:, :], T["rts16"][:, :],
                  start=True, stop=True), inc="t")
              bank_w[3] = evsb
              em.op("vector", lambda v: v.tensor_copy(out=srow0, in_=pb[3][:, 0:16]),
                    waits=[evsb])
              evec = em.op("vector", lambda v: v.tensor_copy(
                  out=T["cntr"][:, :], in_=pb[3][0:1, 32:48]), inc="v")
              bank_r[3] = evec
              b0_ev = em.op("gpsimd", lambda g: g.tensor_add(
                  out=T["b0row"][:, :], in0=T["b0row"][:, :], in1=T["cntr"][:, :]),
                  waits=[evec], inc="p")
              em.op("gpsimd", lambda g: g.tensor_add(out=slotf, in0=srow0, in1=T["ecb"][:, :]))
              em.op("gpsimd", lambda g: g.tensor_scalar_add(out=slotf, in0=slotf, scalar1=-1.0))
              em.op("gpsimd", lambda g: g.tensor_single_scalar(
                  out=tmpz, in_=srow0, scalar=float(CCAP), op=ALU.is_gt))
              em.op("gpsimd", lambda g: g.tensor_scalar_mul(out=tmpz, in0=tmpz, scalar1=1e5))
              em.op("gpsimd", lambda g: g.tensor_add(out=slotf, in0=slotf, in1=tmpz))
              evpa = em.op("gpsimd", lambda g: g.tensor_mul(out=tmpz, in0=eq1, in1=slotf),
                           waits=[rtsifree], inc="p")
              evsa = em.op("vector", lambda v: v.tensor_reduce(
                  out=r_[:, 36:37], in_=tmpz, axis=AX.X, op=ALU.add), waits=[evpa], inc="v")
              em.op("gpsimd", lambda g: g.tensor_sub(out=eq1, in0=msk0, in1=eq1))
              evpb = em.op("gpsimd", lambda g: g.tensor_mul(out=tmpz, in0=eq1, in1=slotf),
                           waits=[evsa], inc="p")
              evsbf = em.op("vector", lambda v: v.tensor_reduce(
                  out=r_[:, 37:38], in_=tmpz, axis=AX.X, op=ALU.add), waits=[evpb], inc="v")
              em.op("gpsimd", lambda g: g.tensor_copy(
                  out=T["rtsi"][:, 0:1], in_=r_[:, 36:37]), waits=[evsbf])
              em.op("gpsimd", lambda g: g.tensor_copy(
                  out=T["rtsi"][:, 1:2], in_=r_[:, 37:38]))
              em.op("gpsimd", lambda g, s=s: g.tensor_single_scalar(
                  out=r_[:, 38:39], in_=T["iocol"][:, :], scalar=float(s * 128), op=ALU.add))
              em.op("gpsimd", lambda g, s=s: g.tensor_single_scalar(
                  out=r_[:, 39:40], in_=T["iocol"][:, :], scalar=float(1024 + s * 128),
                  op=ALU.add))
              em.op("gpsimd", lambda g: g.tensor_copy(
                  out=T["metasrc"][:, 0:1], in_=m1w), waits=[metafree])
              em.op("gpsimd", lambda g: g.tensor_copy(
                  out=T["metasrc"][:, 1:2], in_=r_[:, 38:39]))
              em.op("gpsimd", lambda g: g.tensor_copy(
                  out=T["metasrc"][:, 4:5], in_=wB))
              evmeta = em.op("gpsimd", lambda g: g.tensor_copy(
                  out=T["metasrc"][:, 5:6], in_=r_[:, 39:40]), inc="p")
              t16 = T["t16tok"][:, par * 2048:(par + 1) * 2048]
              em.dma("gpsimd", lambda g, t16=t16: g.indirect_dma_start(
                  out=d_gb, out_offset=bass.IndirectOffsetOnAxis(T["rtsi"][:, 0:1], 0),
                  in_=t16, in_offset=None, bounds_check=BCREG["g"], oob_is_err=False),
                  waits=[ev_gbz, ev_t16cp, evmeta], inc="ldw")
              em.dma("gpsimd", lambda g: g.indirect_dma_start(
                  out=d_md, out_offset=bass.IndirectOffsetOnAxis(T["rtsi"][:, 0:1], 0),
                  in_=T["metasrc"][:, 0:4], in_offset=None,
                  bounds_check=BCREG["g"], oob_is_err=False),
                  waits=[ev_mdz], inc="ldw")
              ev_sc3 = em.dma("gpsimd", lambda g, t16=t16: g.indirect_dma_start(
                  out=d_gb, out_offset=bass.IndirectOffsetOnAxis(T["rtsi"][:, 1:2], 0),
                  in_=t16, in_offset=None, bounds_check=BCREG["g"], oob_is_err=False),
                  inc="ldw")
              ev_scat = em.dma("gpsimd", lambda g: g.indirect_dma_start(
                  out=d_md, out_offset=bass.IndirectOffsetOnAxis(T["rtsi"][:, 1:2], 0),
                  in_=T["metasrc"][:, 4:8], in_offset=None,
                  bounds_check=BCREG["g"], oob_is_err=False),
                  inc="ldw")
              t16free[par] = ev_sc3
              metafree = ev_scat
              rtsifree = ev_scat

      if dbg:
          dbg_cap("dbg_tfm", T["tfm"][:, :], [ev_tfm_done])
          dbg_cap("dbg_wall", T["wall"][:, :], [ev_gate[NSLAB - 1]])

      ev_w13 = {}
      ev_w2 = {}
      ev_w13_free = {}
      ev_w2_free = {}
      ev_hpsum_free = {}
      ev_yevac = {}
      ev_hst_readers = None
      dbg_y_hold = None

      def w13_par(e, i):
          return (e * NIC + i) % 2

      def w2_par(e, dq):
          return (e * 4 + dq) % 2

      def load_w13(e, i):
          par = w13_par(e, i)
          em.dma("gpsimd", lambda g, e=e, i=i, par=par: g.dma_start(
              out=T["w13"][:, par * 4096: par * 4096 + 2048],
              in_=d_w1b[e, i].rearrange("p a q -> p (a q)")),
              waits=[ev_w13_free.get(par), ev_cw1[e]], inc="ldw")
          ev_w13[(e, i)] = em.dma("gpsimd", lambda g, e=e, i=i, par=par: g.dma_start(
              out=T["w13"][:, par * 4096 + 2048: par * 4096 + 4096],
              in_=d_w3b[e, i].rearrange("p a q -> p (a q)")),
              waits=[ev_cw3[e]], inc="ldw")

      def load_w2(e, dq):
          par = w2_par(e, dq)
          ev_w2[(e, dq)] = em.dma("gpsimd", lambda g, e=e, dq=dq, par=par: g.dma_start(
              out=T["w2q"][:, par * 5632:(par + 1) * 5632].rearrange("p (i d) -> p i d", i=NIC),
              in_=d_w2b[e, dq].rearrange("i p d -> p i d")),
              waits=[ev_w2_free.get(par), ev_cw2[e]], inc="ldw")

      ev_ysp = [None] * NSLAB
      ylast = {}
      tfs_evac = {}
      gsl_free = {}
      ysc_free = {}
      metg_free = {}
      ev_yscat = None
      n_dense = NEXP if not SPARSE else 2

      for e in range(NEXP):
          if e == 0:
              load_w13(0, 0)
              load_w13(0, 1)
              load_w2(0, 0)
              load_w2(0, 1)
          dense = e < n_dense
          epar = e % 2
          nslot = 1024 if dense else 2 * CCAP

          if not dense:
              ep = e - 2
              ev_gath = [None, None]
              ev_metg = [None, None]
              for k in range(2):
                  ev_gath[k] = em.dma("gpsimd", lambda g, ep=ep, k=k: g.dma_start(
                      out=T["gsl"][:, k * 2048:(k + 1) * 2048],
                      in_=d_gb[ep * CCAP + k * 128: ep * CCAP + (k + 1) * 128, :]),
                      waits=[ev_scat, gsl_free.get(k),
                             ev_ysp[NSLAB - 1] if e == 2 else None], inc="ldw")
                  ev_metg[k] = em.dma("sync", lambda e_, ep=ep, k=k: e_.dma_start(
                      out=T["metg"][:, k * 4:(k + 1) * 4],
                      in_=d_md[ep * CCAP + k * 128: ep * CCAP + (k + 1) * 128, :]),
                      waits=[ev_scat, metg_free.get(k)], inc="ldx")
                  em.op("gpsimd", lambda g, k=k: g.tensor_copy(
                      out=T["dsti"][:, k:k + 1], in_=T["metg"][:, k * 4 + 1:k * 4 + 2]),
                      waits=[ev_metg[k]], inc="p")
              for k in range(2):
                  lastTk = None
                  for grpi in range(4):
                      bank = grpi % 2
                      j0 = grpi * 4
                      for jj in range(4):
                          j = j0 + jj
                          em.op("tensor", lambda t, j=j, bank=bank, jj=jj, k=k:
                                t.transpose(pb[bank][:, jj * 128:(jj + 1) * 128],
                                            T["gsl"][:, k * 2048 + j * 128:
                                                     k * 2048 + (j + 1) * 128],
                                            T["idf32"][:, :]),
                                waits=([ev_gath[k],
                                        ev_ysp[NSLAB - 1] if e == 2 else None]
                                       if j == 0 else []) +
                                      ([bank_r.get(bank)] if jj == 0 else []),
                                inc="t" if jj == 3 else None)
                      bank_w[bank] = em.ev("t")
                      dest = T["tfs"][:, epar * 4096:(epar + 1) * 4096].rearrange(
                          "p (f c) -> p f c", f=16)[:, j0:j0 + 4, k * 128:(k + 1) * 128]
                      ev_te = em.op("vector", lambda v, bank=bank, dest=dest:
                                    v.tensor_copy(out=dest, in_=pb[bank][:, :].rearrange(
                                        "p (a b) -> p a b", a=4)),
                                    waits=[bank_w[bank],
                                           ev_ysp[NSLAB - 1] if e == 2 else None], inc="v")
                      bank_r[bank] = ev_te
                  gsl_free[k] = em.ev("t")
              tfs_evac[epar] = em.ev("v")

          ev_last_hmul = None
          nfree = 512 if dense else 256
          for i in range(NIC):
              par = w13_par(e, i)
              for half in range(2 if dense else 1):
                  sel = (2 * i + half) % 2
                  b1 = 4 + sel
                  b3 = 6 + sel
                  for fc in range(NFC):
                      lhs1 = T["w13"][:, par * 4096 + fc * 128: par * 4096 + (fc + 1) * 128]
                      if dense:
                          rhs = T["tfm"][:, fc * 1024 + half * 512: fc * 1024 + (half + 1) * 512]
                      else:
                          rhs = T["tfs"][:, epar * 4096 + fc * 256: epar * 4096 + (fc + 1) * 256]
                      em.op("tensor", lambda t, b1=b1, lhs1=lhs1, rhs=rhs, fc=fc, nfree=nfree:
                            t.matmul(pb[b1][:, 0:nfree], lhs1, rhs,
                                     start=(fc == 0), stop=(fc == NFC - 1)),
                            waits=([ev_w13[(e, i)], ev_gate[NSLAB - 1],
                                    None if dense else tfs_evac.get(epar),
                                    ev_hpsum_free.get(b1), bank_r.get(b1)] if fc == 0 else []),
                            inc="t" if fc == NFC - 1 else None)
                  ev_mm1 = em.ev("t")
                  bank_w[b1] = ev_mm1
                  for fc in range(NFC):
                      lhs3 = T["w13"][:, par * 4096 + 2048 + fc * 128:
                                      par * 4096 + 2048 + (fc + 1) * 128]
                      if dense:
                          rhs = T["tfm"][:, fc * 1024 + half * 512: fc * 1024 + (half + 1) * 512]
                      else:
                          rhs = T["tfs"][:, epar * 4096 + fc * 256: epar * 4096 + (fc + 1) * 256]
                      em.op("tensor", lambda t, b3=b3, lhs3=lhs3, rhs=rhs, fc=fc, nfree=nfree:
                            t.matmul(pb[b3][:, 0:nfree], lhs3, rhs,
                                     start=(fc == 0), stop=(fc == NFC - 1)),
                            waits=([ev_hpsum_free.get(b3), bank_r.get(b3)] if fc == 0 else []),
                            inc="t" if fc == NFC - 1 else None)
                  ev_mm3 = em.ev("t")
                  bank_w[b3] = ev_mm3
                  ev_sil = em.op("scalar", lambda s_, b1=b1, sel=sel, nfree=nfree:
                                 s_.activation(
                                     out=T["sil"][:, sel * 512: sel * 512 + nfree],
                                     in_=pb[b1][:, 0:nfree], func=AF.Silu),
                                 waits=[ev_mm1, ev_hpsum_free.get(("sil", sel))], inc="s")
                  dst = T["hst"][:, i * 1024 + half * 512: i * 1024 + half * 512 + nfree]
                  first_write = (i == 0 and half == 0)
                  ev_hm = em.op("vector", lambda v, b3=b3, sel=sel, dst=dst, nfree=nfree:
                                v.tensor_tensor(
                                    out=dst, in0=T["sil"][:, sel * 512: sel * 512 + nfree],
                                    in1=pb[b3][:, 0:nfree], op=ALU.mult),
                                waits=[ev_sil, ev_mm3,
                                       ev_hst_readers if first_write else None,
                                       dbg_y_hold if first_write else None],
                                inc="v")
                  ev_hpsum_free[b1] = ev_hm
                  ev_hpsum_free[b3] = ev_hm
                  ev_hpsum_free[("sil", sel)] = ev_hm
                  bank_r[b1] = ev_hm
                  bank_r[b3] = ev_hm
                  ev_last_hmul = ev_hm
              ev_w13_free[par] = em.ev("t")
              if i + 2 < NIC:
                  load_w13(e, i + 2)
              elif e + 1 < NEXP:
                  load_w13(e + 1, i + 2 - NIC)

          if dense:
              for dq in range(4):
                  par = w2_par(e, dq)
                  for s in range(NSLAB):
                      bank = s % 4
                      for i in range(NIC):
                          lhs = T["hst"][:, i * 1024 + s * 128: i * 1024 + (s + 1) * 128]
                          rhs = T["w2q"][:, par * 5632 + i * 512: par * 5632 + (i + 1) * 512]
                          em.op("tensor", lambda t, bank=bank, lhs=lhs, rhs=rhs, i=i: t.matmul(
                              pb[bank][:, :], lhs, rhs, start=(i == 0), stop=(i == NIC - 1)),
                              waits=([ev_last_hmul, ev_w2[(e, dq)],
                                      ev_yevac.get(bank), bank_r.get(bank)] if i == 0 else []),
                              inc="t" if i == NIC - 1 else None)
                      ev_bmm = em.ev("t")
                      bank_w[bank] = ev_bmm
                      ysl = T["yac"][:, s * 2048 + dq * 512: s * 2048 + (dq + 1) * 512]
                      if e == 0:
                          ev_ye = em.op("vector", lambda v, bank=bank, ysl=ysl:
                                        v.tensor_scalar_mul(
                                            out=ysl, in0=pb[bank][:, :], scalar1=1.0),
                                        waits=[ev_bmm, ev_scat], inc="v")
                      else:
                          ev_ye = em.op("vector", lambda v, bank=bank, ysl=ysl:
                                        v.scalar_tensor_tensor(
                                            out=ysl, in0=pb[bank][:, :], scalar=1.0, in1=ysl,
                                            op0=ALU.mult, op1=ALU.add),
                                        waits=[ev_bmm], inc="v")
                      ev_yevac[bank] = ev_ye
                      bank_r[bank] = ev_ye
                      if e == n_dense - 1 and dq == 3:
                          ylast[s] = ev_ye
                  ev_w2_free[par] = em.ev("t")
                  if dq + 2 < 4:
                      load_w2(e, dq + 2)
                  elif e + 1 < NEXP:
                      load_w2(e + 1, dq + 2 - 4)
              if SPARSE and e == n_dense - 1:
                  for s in range(NSLAB):
                      ev_ysp[s] = em.dma("sync", lambda e_, s=s: e_.dma_start(
                          out=d_yb[2 * TOKC + s * 128: 2 * TOKC + (s + 1) * 128, :],
                          in_=T["yac"][:, s * 2048:(s + 1) * 2048]),
                          waits=[ylast[s]], inc="st")
          else:
              for dq in range(4):
                  par = w2_par(e, dq)
                  for k in range(2):
                      bank = 2 + k
                      for i in range(NIC):
                          lhs = T["hst"][:, i * 1024 + k * 128: i * 1024 + (k + 1) * 128]
                          rhs = T["w2q"][:, par * 5632 + i * 512: par * 5632 + (i + 1) * 512]
                          em.op("tensor", lambda t, bank=bank, lhs=lhs, rhs=rhs, i=i: t.matmul(
                              pb[bank][:, :], lhs, rhs, start=(i == 0), stop=(i == NIC - 1)),
                              waits=([ev_last_hmul, ev_w2[(e, dq)],
                                      ev_yevac.get(bank), bank_r.get(bank)] if i == 0 else []),
                              inc="t" if i == NIC - 1 else None)
                      ev_bmm = em.ev("t")
                      bank_w[bank] = ev_bmm
                      wcol = T["metg"][:, k * 4: k * 4 + 1]
                      ysl = T["ysc"][:, k * 2048 + dq * 512: k * 2048 + (dq + 1) * 512]
                      ev_ye = em.op("vector", lambda v, bank=bank, ysl=ysl, wcol=wcol:
                                    v.tensor_scalar_mul(
                                        out=ysl, in0=pb[bank][:, :], scalar1=wcol),
                                    waits=[ev_bmm, ev_metg[k],
                                           ysc_free.get(k) if dq == 0 else None], inc="v")
                      ev_yevac[bank] = ev_ye
                      bank_r[bank] = ev_ye
                      if dq == 3:
                          metg_free[k] = ev_ye
                          ysc_free[k] = em.dma("gpsimd", lambda g, k=k: g.indirect_dma_start(
                              out=d_yb,
                              out_offset=bass.IndirectOffsetOnAxis(T["dsti"][:, k:k + 1], 0),
                              in_=T["ysc"][:, k * 2048:(k + 1) * 2048], in_offset=None,
                              bounds_check=BCREG["y"], oob_is_err=False),
                              waits=[ev_ye, ev_ybz], inc="ldw")
                          ev_yscat = ysc_free[k]
                  ev_w2_free[par] = em.ev("t")
                  if dq + 2 < 4:
                      load_w2(e, dq + 2)
                  elif e + 1 < NEXP:
                      load_w2(e + 1, dq + 2 - 4)
          ev_hst_readers = em.ev("t")

          if dbg and e == 0:
              c1 = dbg_cap("dbg_hst", T["hst"][:, :], [ev_last_hmul])
              c2 = dbg_cap("dbg_yac0", T["yac"][:, 0:2048], [ev_yevac[0], ev_yevac[1],
                                                             ev_yevac[2], ev_yevac[3]])
              dbg_y_hold = ("dbgst", max(c1[1], c2[1]))

      ev_store = [None] * NSLAB
      yf_free = [None, None]
      ga_free = None
      for s in range(NSLAB):
          par = s % 2
          if SPARSE:
              yf = T["yfull"][:, par * 2048:(par + 1) * 2048]
              l1 = em.dma("sync", lambda e_, s=s, yf=yf: e_.dma_start(
                  out=yf, in_=d_yb[2 * TOKC + s * 128: 2 * TOKC + (s + 1) * 128, :]),
                  waits=[ev_ysp[s], yf_free[par]], inc="ldx")
              l2 = em.dma("sync", lambda e_, s=s: e_.dma_start(
                  out=T["gsl"][:, 0:2048], in_=d_yb[s * 128:(s + 1) * 128, :]),
                  waits=[ev_yscat, ga_free, gsl_free.get(0), gsl_free.get(1)], inc="ldx")
              l3 = em.dma("sync", lambda e_, s=s: e_.dma_start(
                  out=T["gsl"][:, 2048:4096],
                  in_=d_yb[TOKC + s * 128: TOKC + (s + 1) * 128, :]), inc="ldx")
              ev_a1 = em.op("vector", lambda v, yf=yf: v.tensor_add(
                  out=yf, in0=yf, in1=T["gsl"][:, 0:2048]), waits=[l1, l2])
              ev_a2 = em.op("vector", lambda v, yf=yf: v.tensor_add(
                  out=yf, in0=yf, in1=T["gsl"][:, 2048:4096]), waits=[l3], inc="v")
              ga_free = ev_a2
              ev_ln5 = ln_to_scr(yf, [ev_a2])
              yf_free[par] = ev_ln5
          else:
              ysl = T["yac"][:, s * 2048:(s + 1) * 2048]
              ev_ln5 = ln_to_scr(ysl, [ev_yevac[0], ev_yevac[1], ev_yevac[2], ev_yevac[3]])
          ev_xl = em.dma("sync", lambda e_, s=s, par=par: e_.dma_start(
              out=T["xslab"][:, par * 2048:(par + 1) * 2048],
              in_=d_x2[s * 128:(s + 1) * 128, :]),
              waits=[ev_gate[NSLAB - 1], ev_store[s - 2] if s >= 2 else None], inc="ldx")
          X = T["xslab"][:, par * 2048:(par + 1) * 2048]
          O = T["oslab"][:, par * 2048:(par + 1) * 2048]
          ev_o = em.op("vector", lambda v, X=X, O=O: v.tensor_add(
              out=O, in0=X, in1=T["scr"][:, :]),
              waits=[ev_ln5, ev_xl,
                     ev_store[s - 2] if s >= 2 else ev_ofree[NSLAB - 2 + par]], inc="v")
          scr_guard[0] = [ev_o]
          ev_store[s] = em.dma("sync", lambda e_, s=s, O=O: e_.dma_start(
              out=d_out[s * 128:(s + 1) * 128, :], in_=O), waits=[ev_o], inc="st")

      if dbg:
          em.dma("gpsimd", lambda g: g.dma_start(out=dbg_outs["dbg_x2"], in_=d_x2),
                 waits=[ev_spill[NSLAB - 1]], inc="dbgst")
          em.op("gpsimd", lambda g: g.engine_nop(), waits=[("dbgst", em.cnt["dbgst"])])

    final_st = em.ev("st")

    with contextlib.ExitStack() as stk:
        sems = {r: stk.enter_context(nc.semaphore(r)) for r in SEM_ROLES}
        block = stk.enter_context(nc.Block())

        def replay(name):
            def body(eng):
                if name == "gpsimd" and SPARSE:
                    BCREG["g"] = eng.to_reg(GROWS - 1)
                    BCREG["y"] = eng.to_reg(2 * TOKC - 1)
                for (fn, w, inc, by) in em.q[name]:
                    for (role, val) in w:
                        eng.wait_ge(sems[role], val)
                    inst = fn(eng)
                    if inc is not None:
                        inst.then_inc(sems[inc], by)
                if name == "sync":
                    eng.wait_ge(sems["st"], final_st[1])
            return body

        nc._em = em
        block.sync(replay("sync"))
        block.gpsimd(replay("gpsimd"))
        block.tensor(replay("tensor"))
        block.vector(replay("vector"))
        block.scalar(replay("scalar"))

    return nc


def kernel(**inputs):
    dbg = bool(inputs.pop("_dbg", False))
    g = host_layout(inputs)
    nc = build_nc(dbg=dbg)
    shared = {k: g[k] for k in ["wq", "wk", "wv", "bqc", "bkc", "bvc", "wout", "bout2",
                                "gate", "w1_all", "w3_all", "w2_all"]}
    in_maps = []
    for c in range(NCORE):
        m = {"x_in": g["x_shards"][c]}
        m.update(shared)
        in_maps.append(m)
    res = run_bass_kernel_spmd(nc, in_maps, core_ids=list(range(NCORE)))
    out = np.concatenate([np.asarray(res.results[c]["out"]) for c in range(NCORE)], axis=0)
    if dbg:
        return out.reshape(1, 8192, D), res
    return out.reshape(1, 8192, D)


# version 07
# speedup vs baseline: 1.0466x; 1.0466x over previous
import numpy as np
import ml_dtypes
import contextlib
import concourse.bass as bass
import concourse.mybir as mybir
from concourse.bass_utils import run_bass_kernel_spmd

F32 = mybir.dt.float32
BF16 = mybir.dt.bfloat16
I32 = mybir.dt.int32
AF = mybir.ActivationFunctionType
ALU = mybir.AluOpType
AX = mybir.AxisListType
BFD = ml_dtypes.bfloat16

NCORE = 8
TOKC = 1024
NSLAB = 8
D = 2048
NFC = 16
INTER = 1408
NIC = 11
NEXP = 18
EPS = 1e-5
SPARSE = True
CCAP = 256
GROWS = 16 * CCAP
CUT = None


class _CutDone(Exception):
    pass
SEM_ROLES = ("ldx", "ldw", "st", "dbgst", "cw", "t", "v", "s", "p")


class Em:
    ENGS = ("sync", "gpsimd", "tensor", "vector", "scalar")

    def __init__(self):
        self.q = {e: [] for e in self.ENGS}
        self.cnt = {r: 0 for r in SEM_ROLES}
        self.last_wait = {}

    def ev(self, role):
        return (role, self.cnt[role])

    def op(self, eng, fn, waits=(), inc=None, by=1):
        w = []
        for wt in waits:
            if wt is None:
                continue
            role, val = wt
            if val <= 0:
                continue
            key = (eng, role)
            if self.last_wait.get(key, 0) < val:
                self.last_wait[key] = val
                w.append((role, val))
        ev = None
        if inc is not None:
            self.cnt[inc] += by
            ev = (inc, self.cnt[inc])
        self.q[eng].append((fn, w, inc, by))
        return ev

    def dma(self, eng, fn, waits=(), inc=None):
        return self.op(eng, fn, waits, inc, by=16)


def host_layout(inputs):
    g = {}
    x = np.ascontiguousarray(np.asarray(inputs["L0_attn_x"], dtype=np.float32).reshape(8192, D))
    g["x_shards"] = [np.ascontiguousarray(x[c * TOKC:(c + 1) * TOKC]) for c in range(NCORE)]

    Wqkv = np.asarray(inputs["L0_attn_Wqkv"], dtype=np.float32)
    bqkv = np.asarray(inputs["L0_attn_bqkv"], dtype=np.float32)
    wq = np.zeros((16, 128, 128), np.float32)
    wk = np.zeros((32, 128, 128), np.float32)
    wv = np.zeros((16, 128, 128), np.float32)
    for j in range(16):
        wq[j, 0:64, 0:64] = Wqkv[2 * j, :, 0:64]
        wq[j, 64:128, 64:128] = Wqkv[2 * j + 1, :, 0:64]
        wv[j, 0:64, 0:64] = Wqkv[2 * j, :, 128:192]
        wv[j, 64:128, 64:128] = Wqkv[2 * j + 1, :, 128:192]
    for m in range(32):
        h = (m % 2) * 64
        wk[m, h:h + 64, 0:64] = Wqkv[m, :, 64:128]
        wk[m, h:h + 64, 64:128] = Wqkv[m, :, 64:128]
    g["wq"], g["wk"], g["wv"] = wq, wk, wv
    bqc = np.zeros((128, 16), np.float32)
    bkc = np.zeros((128, 32), np.float32)
    bvc = np.zeros((128, 16), np.float32)
    for j in range(16):
        bqc[0:64, j] = bqkv[2 * j, 0:64]
        bqc[64:128, j] = bqkv[2 * j + 1, 0:64]
        bvc[0:64, j] = bqkv[2 * j, 128:192]
        bvc[64:128, j] = bqkv[2 * j + 1, 128:192]
    for m in range(32):
        bkc[0:64, m] = bqkv[m, 64:128]
        bkc[64:128, m] = bqkv[m, 64:128]
    g["bqc"], g["bkc"], g["bvc"] = bqc, bkc, bvc
    g["brow"] = np.ascontiguousarray(np.concatenate(
        [bqc.T.reshape(-1), bkc.T.reshape(-1), bvc.T.reshape(-1)])[None, :])

    wout_ = np.asarray(inputs["L0_attn_Wout"], dtype=np.float32)
    g["wout"] = np.ascontiguousarray(np.concatenate([wout_, wout_], axis=0))
    bout = np.asarray(inputs["L0_attn_bout"], dtype=np.float32)
    g["bout2"] = np.ascontiguousarray(np.concatenate([bout, bout])[:, None])

    gate_w = np.asarray(inputs["L0_ffn_gate_w"], dtype=np.float32)
    g["gate"] = np.ascontiguousarray(gate_w.T.reshape(NFC, 128, 16).transpose(1, 0, 2))

    def lay_w13(w):
        nic = w.shape[0] // 128
        return w.reshape(nic, 128, NFC, 128).transpose(0, 3, 2, 1)

    def lay_w2(w2):
        nic = w2.shape[1] // 128
        return w2.T.reshape(nic, 128, 4, 512).transpose(2, 0, 1, 3)

    w1 = np.asarray(inputs["L0_ffn_w1"], dtype=np.float32)
    w3 = np.asarray(inputs["L0_ffn_w3"], dtype=np.float32)
    w2 = np.asarray(inputs["L0_ffn_w2"], dtype=np.float32)
    sw1 = np.asarray(inputs["L0_ffn_sw1"], dtype=np.float32)
    sw3 = np.asarray(inputs["L0_ffn_sw3"], dtype=np.float32)
    sw2 = np.asarray(inputs["L0_ffn_sw2"], dtype=np.float32)

    w1_all = np.empty((NEXP, NIC, 128, NFC, 128), np.float32)
    w3_all = np.empty((NEXP, NIC, 128, NFC, 128), np.float32)
    w2_all = np.empty((NEXP, 4, NIC, 128, 512), np.float32)
    w1_all[0] = lay_w13(sw1[0:INTER]);      w1_all[1] = lay_w13(sw1[INTER:])
    w3_all[0] = lay_w13(sw3[0:INTER]);      w3_all[1] = lay_w13(sw3[INTER:])
    w2_all[0] = lay_w2(sw2[:, 0:INTER]);    w2_all[1] = lay_w2(sw2[:, INTER:])
    for e in range(16):
        w1_all[2 + e] = lay_w13(w1[e])
        w3_all[2 + e] = lay_w13(w3[e])
        w2_all[2 + e] = lay_w2(w2[e])
    g["w1_all"] = np.ascontiguousarray(w1_all)
    g["w3_all"] = np.ascontiguousarray(w3_all)
    g["w2_all"] = np.ascontiguousarray(w2_all)
    return g


SB_BASE = 16768
SB_TOP = 229344

def _mk_map():
    M = {}
    off = SB_BASE
    def A(name, nbytes):
        nonlocal off
        off = (off + 31) // 32 * 32
        M[name] = off
        off += nbytes
    A("idf32", 512); A("idb16", 256); A("id64x2", 128)
    A("wout", 128); A("gate", 512)
    A("bqc", 64); A("bkc", 128); A("bvc", 64); A("bout2", 32)
    A("wall", 512); A("stats", 256); A("gsc", 256); A("cst", 32)
    A("rt_rts", 384); A("rt_rts16", 32); A("rt_rtsi", 8); A("rt_metg", 32)
    A("rt_dsti", 8); A("rt_b0row", 64); A("rt_cntr", 64); A("rt_ecb", 64)
    A("rt_iocol", 32); A("rt_lt", 256); A("rt_o1r", 512); A("rt_o1c", 32)
    A("scr", 8192)
    A("xslab", 16384); A("oslab", 16384)
    perm_end = off
    off = perm_end
    A("wqkv", 16384)
    A("tfmA", 8192)
    A("Qb", 16384); A("Kb", 32768); A("Vb", 16384)
    A("Sb", 16384); A("Eb", 16384); A("Etb", 16384)
    A("zf", 512); A("rz", 512); A("ub", 512); A("u16", 256)
    A("ctx16", 4096); A("afm", 8192); A("atok", 8192)
    attn_end = off
    off = perm_end
    A("tfm", 32768)
    A("hst", 22528)
    A("w13", 16384)
    A("w2q", 22528)
    A("yac", 65536)
    A("sil", 4096)
    ffn_end = off
    assert max(attn_end, ffn_end) <= SB_TOP, (attn_end, ffn_end)
    return M

MM = _mk_map()


def build_nc(dbg=False, phases=4):
    nc = bass.Bass("TRN2", target_bir_lowering=False, detect_race_conditions=False)

    d_x = nc.dram_tensor("x_in", [TOKC, D], F32, kind="ExternalInput").ap()
    d_wq = nc.dram_tensor("wq", [16, 128, 128], F32, kind="ExternalInput").ap()
    d_wk = nc.dram_tensor("wk", [32, 128, 128], F32, kind="ExternalInput").ap()
    d_wv = nc.dram_tensor("wv", [16, 128, 128], F32, kind="ExternalInput").ap()
    d_bqc = nc.dram_tensor("bqc", [128, 16], F32, kind="ExternalInput").ap()
    d_bkc = nc.dram_tensor("bkc", [128, 32], F32, kind="ExternalInput").ap()
    d_bvc = nc.dram_tensor("bvc", [128, 16], F32, kind="ExternalInput").ap()
    d_wout = nc.dram_tensor("wout", [128, 64], F32, kind="ExternalInput").ap()
    d_bout2 = nc.dram_tensor("bout2", [128, 1], F32, kind="ExternalInput").ap()
    d_gate = nc.dram_tensor("gate", [128, NFC, 16], F32, kind="ExternalInput").ap()
    d_brow = nc.dram_tensor("brow", [1, 64 * 128], F32, kind="ExternalInput").ap()
    d_w1 = nc.dram_tensor("w1_all", [NEXP, NIC, 128, NFC, 128], F32, kind="ExternalInput").ap()
    d_w3 = nc.dram_tensor("w3_all", [NEXP, NIC, 128, NFC, 128], F32, kind="ExternalInput").ap()
    d_w2 = nc.dram_tensor("w2_all", [NEXP, 4, NIC, 128, 512], F32, kind="ExternalInput").ap()
    d_w1b = nc.dram_tensor("w1b", [NEXP, NIC, 128, NFC, 128], BF16, kind="Internal").ap()
    d_w3b = nc.dram_tensor("w3b", [NEXP, NIC, 128, NFC, 128], BF16, kind="Internal").ap()
    d_w2b = nc.dram_tensor("w2b", [NEXP, 4, NIC, 128, 512], BF16, kind="Internal").ap()
    d_out = nc.dram_tensor("out", [TOKC, D], F32, kind="ExternalOutput").ap()
    d_x2 = nc.dram_tensor("x2spill", [TOKC, D], F32, kind="Internal").ap()
    d_gb = nc.dram_tensor("gbuf", [GROWS, D], BF16, kind="Internal").ap()
    d_yb = nc.dram_tensor("ybuf", [3 * TOKC, D], F32, kind="Internal").ap()
    d_md = nc.dram_tensor("mbuf", [GROWS, 4], F32, kind="Internal").ap()

    c_idf32 = nc.inline_tensor(np.eye(128, dtype=np.float32), name="c_idf32").ap()
    c_idb16 = nc.inline_tensor(np.eye(128, dtype=BFD), name="c_idb16").ap()
    c_id64x2 = nc.inline_tensor(
        np.concatenate([np.eye(64), np.eye(64)], axis=0).astype(BFD), name="c_id64x2").ap()
    c_lt = nc.inline_tensor(np.triu(np.ones((128, 128))).astype(BFD), name="c_lt").ap()
    c_o1r = nc.inline_tensor(np.ones((1, 128), np.float32), name="c_o1r").ap()
    c_o1c = nc.inline_tensor(np.ones((128, 1), dtype=BFD), name="c_o1c").ap()
    c_o1w = nc.inline_tensor(np.ones((1, 256), dtype=BFD), name="c_o1w").ap()
    c_ecb = nc.inline_tensor(np.tile(np.arange(16, dtype=np.float32) * CCAP, (128, 1)),
                             name="c_ecb").ap()
    c_iot = nc.inline_tensor(np.arange(128, dtype=np.float32)[:, None], name="c_iot").ap()
    c_z16 = nc.inline_tensor(np.zeros((128, D), dtype=BFD), name="c_z16").ap()
    c_z32 = nc.inline_tensor(np.zeros((128, D), np.float32), name="c_z32").ap()
    minit_np = np.zeros((GROWS, 4), np.float32); minit_np[:, 1] = 1e5
    c_mi = nc.inline_tensor(minit_np, name="c_mi").ap()
    CONSTS = [EPS, -0.5, 0.125]
    c_cst = nc.inline_tensor(
        np.tile(np.array([CONSTS], dtype=np.float32), (128, 1)), name="c_cst").ap()

    T = {}
    def sb(name, shape, dtype):
        T[name] = nc.alloc_sbuf_tensor_at(name, shape, dtype, offset=MM[name])
        return T[name]

    sb("idf32", [128, 128], F32); sb("idb16", [128, 128], BF16)
    T["lt"] = nc.alloc_sbuf_tensor_at("lt", [128, 128], BF16, offset=MM["rt_lt"])
    T["o1r"] = nc.alloc_sbuf_tensor_at("o1r", [1, 128], F32, offset=MM["rt_o1r"])
    T["o1c"] = nc.alloc_sbuf_tensor_at("o1c", [128, 1], BF16, offset=MM["rt_o1c"])
    sb("id64x2", [128, 64], BF16)
    sb("wout", [128, 64], BF16); sb("gate", [128, NFC * 16], BF16)
    sb("bqc", [128, 16], F32); sb("bkc", [128, 32], F32); sb("bvc", [128, 16], F32)
    sb("bout2", [128, 1], F32)
    sb("wall", [128, 128], F32)
    sb("cst", [128, 3], F32)
    for ci_, cv_ in enumerate([EPS, -0.5, 0.125]):
        nc.const_aps.register(F32, cv_, T["cst"][:, ci_:ci_ + 1])
    sb("stats", [128, 64], F32)
    sb("gsc", [128, 64], F32)
    sb("scr", [128, 2048], F32)
    sb("xslab", [128, 2 * 2048], F32)
    sb("oslab", [128, 2 * 2048], F32)
    sb("wqkv", [128, 64 * 128], BF16)
    sb("tfmA", [128, 2 * 2048], BF16)
    sb("Qb", [128, 4096], BF16); sb("Kb", [128, 8192], BF16); sb("Vb", [128, 4096], F32)
    sb("Sb", [128, 4096], BF16)
    sb("Eb", [128, 2 * 4096], BF16); sb("Etb", [128, 2 * 4096], BF16)
    sb("zf", [128, 128], F32); sb("rz", [128, 128], F32)
    sb("ub", [128, 128], F32); sb("u16", [128, 128], BF16)
    sb("ctx16", [128, 2048], BF16)
    sb("afm", [128, 2048], F32); sb("atok", [128, 2048], F32)
    T["ctx16b"] = nc.alloc_sbuf_tensor_at("ctx16b", [64, 2048], BF16, offset=MM["atok"])
    T["brow"] = nc.alloc_sbuf_tensor_at("brow", [1, 64 * 128], BF16,
                                        offset=MM["Kb"] + 16384)
    T["onesb"] = nc.alloc_sbuf_tensor_at("onesb", [1, 256], BF16,
                                         offset=MM["Sb"] + 8192)
    sb("tfm", [128, NFC * 1024], BF16)
    sb("hst", [128, NIC * 1024], BF16)
    sb("w13", [128, 2 * 2 * 2048], BF16)
    sb("w2q", [128, 2 * NIC * 512], BF16)
    sb("yac", [128, NSLAB * 2048], F32)
    sb("sil", [128, 2 * 512], F32)
    R = MM["yac"]
    def sbat(name, off, shape, dtype):
        T[name] = nc.alloc_sbuf_tensor_at(name, shape, dtype, offset=off)
    sbat("t16tok", R, [128, 2 * 2048], BF16)
    sbat("metasrc", R + 8192, [128, 8], F32)
    sbat("tfs", R, [128, 2 * 4096], BF16)
    sbat("gsl", R + 16384, [128, 2 * 2048], F32)
    sbat("ysc", R + 32768, [128, 2 * 2048], F32)
    sbat("yfull", R + 49152, [128, 2 * 2048], F32)
    for nm, shp, dt in [("rts", [128, 96], F32), ("rts16", [128, 16], BF16),
                        ("rtsi", [128, 2], I32), ("metg", [128, 8], F32),
                        ("dsti", [128, 2], I32), ("b0row", [1, 16], F32),
                        ("cntr", [1, 16], F32), ("ecb", [128, 16], F32),
                        ("iocol", [128, 1], F32)]:
        T[nm] = nc.alloc_sbuf_tensor_at(nm, shp, dt, offset=MM["rt_" + nm])

    pb = [nc.alloc_psum_tensor(f"pb{i}", [128, 512], F32) for i in range(8)]

    em = Em()
    BCREG = {}
    dbg_outs = {}
    if dbg:
        for name, shape in [("dbg_tfmA", [128, 2048]), ("dbg_Qb", [128, 4096]),
                            ("dbg_Kb", [128, 8192]), ("dbg_Vb", [128, 4096]),
                            ("dbg_ctx16", [128, 2048]),
                            ("dbg_atok", [128, 2048]), ("dbg_x2", [TOKC, D]),
                            ("dbg_tfm", [128, NFC * 1024]), ("dbg_hst", [128, NIC * 1024]),
                            ("dbg_yac0", [128, 2048]), ("dbg_wall", [128, 128])]:
            dbg_outs[name] = nc.dram_tensor(name, shape, F32, kind="ExternalOutput").ap()

    def dbg_cap(name, src_ap, waits):
        if not dbg:
            return None
        return em.dma("gpsimd", lambda g, s=src_ap, o=dbg_outs[name]:
                      g.dma_start(out=o, in_=s), waits=waits, inc="dbgst")

    ev_consts = None
    for fn in [
        lambda g: g.dma_start(out=T["cst"][:, :], in_=c_cst),
        lambda g: g.dma_start(out=T["lt"][:, :], in_=c_lt),
        lambda g: g.dma_start(out=T["o1r"][:, :], in_=c_o1r),
        lambda g: g.dma_start(out=T["o1c"][:, :], in_=c_o1c),
        lambda g: g.dma_start(out=T["ecb"][:, :], in_=c_ecb),
        lambda g: g.dma_start(out=T["iocol"][:, :], in_=c_iot),
        lambda g: g.dma_start(out=T["idf32"][:, :], in_=c_idf32),
        lambda g: g.dma_start(out=T["idb16"][:, :], in_=c_idb16),
        lambda g: g.dma_start(out=T["id64x2"][:, :], in_=c_id64x2),
        lambda g: g.dma_start(out=T["wout"][:, :], in_=d_wout),
        lambda g: g.dma_start(out=T["gate"][:, :], in_=d_gate.rearrange("p a e -> p (a e)")),
        lambda g: g.dma_start(out=T["bqc"][:, :], in_=d_bqc),
        lambda g: g.dma_start(out=T["bkc"][:, :], in_=d_bkc),
        lambda g: g.dma_start(out=T["bvc"][:, :], in_=d_bvc),
        lambda g: g.dma_start(out=T["bout2"][:, :], in_=d_bout2),
        lambda g: g.dma_start(out=T["brow"][:, :], in_=d_brow),
        lambda g: g.dma_start(out=T["onesb"][:, :], in_=c_o1w),
        lambda g: g.dma_start(out=T["wqkv"][:, 0:2048].rearrange("p (a q) -> p a q", a=16),
                              in_=d_wq.rearrange("a p q -> p a q")),
        lambda g: g.dma_start(out=T["wqkv"][:, 2048:6144].rearrange("p (a q) -> p a q", a=32),
                              in_=d_wk.rearrange("a p q -> p a q")),
        lambda g: g.dma_start(out=T["wqkv"][:, 6144:8192].rearrange("p (a q) -> p a q", a=16),
                              in_=d_wv.rearrange("a p q -> p a q")),
    ]:
        ev_consts = em.dma("gpsimd", fn, inc="ldw")

    ev_xload = [None] * NSLAB
    ev_xfree = [None] * NSLAB
    ev_ofree = [None] * NSLAB

    def load_x(s):
        par = s % 2
        w = [ev_xfree[s - 2]] if s >= 2 else []
        ev_xload[s] = em.dma(
            "sync", lambda e, s=s, par=par: e.dma_start(
                out=T["xslab"][:, par * 2048:(par + 1) * 2048],
                in_=d_x[s * 128:(s + 1) * 128, :]), waits=w, inc="ldx")

    ev_gbz = None
    ev_ybz = None
    ev_mdz = None
    if SPARSE:
        for r0 in range(0, GROWS, 128):
            ev_gbz = em.dma("sync", lambda e_, r0=r0: e_.dma_start(
                out=d_gb[r0:r0 + 128, :], in_=c_z16), inc="st")
        for r0 in range(0, 2 * TOKC, 128):
            ev_ybz = em.dma("sync", lambda e_, r0=r0: e_.dma_start(
                out=d_yb[r0:r0 + 128, :], in_=c_z32), inc="st")
        ev_mdz = em.dma("sync", lambda e_: e_.dma_start(out=d_md, in_=c_mi), inc="st")

    load_x(0); load_x(1)

    SC = {"sx": 0, "ssq": 1, "mean": 2, "msq": 3, "var": 4, "lnv": 5, "rstd": 6, "t7": 7,
          "gmax": 8, "gneg": 9, "gsum": 10, "grs": 11, "gm1": 12, "gm2": 13}

    def stat(k):
        return T["stats"][:, SC[k]:SC[k] + 1]

    scr_guard = [[]]

    def ln_to_scr(X_ap, waits_in):
        wl = list(waits_in)
        ev_sq = em.op("scalar", lambda s, X=X_ap: s.activation(
            out=T["scr"][:, :], in_=X, func=AF.Square),
            waits=wl + scr_guard[0], inc="s")
        em.op("vector", lambda v, X=X_ap: v.tensor_reduce(
            out=stat("sx"), in_=X, axis=AX.X, op=ALU.add), waits=wl)
        ev_red = em.op("vector", lambda v: v.tensor_reduce(
            out=stat("ssq"), in_=T["scr"][:, :], axis=AX.X, op=ALU.add),
            waits=[ev_sq], inc="v")
        em.op("gpsimd", lambda g: g.tensor_scalar_mul(
            out=stat("mean"), in0=stat("sx"), scalar1=1.0 / D), waits=[ev_red])
        em.op("gpsimd", lambda g: g.tensor_scalar_mul(
            out=stat("msq"), in0=stat("mean"), scalar1=stat("mean")))
        em.op("gpsimd", lambda g: g.tensor_scalar_mul(
            out=stat("t7"), in0=stat("ssq"), scalar1=1.0 / D))
        ev_var = em.op("gpsimd", lambda g: g.tensor_sub(
            out=stat("var"), in0=stat("t7"), in1=stat("msq")), inc="p")
        em.op("scalar", lambda s: s.activation(
            out=stat("lnv"), in_=stat("var"), func=AF.Ln, bias=EPS), waits=[ev_var])
        ev_rstd = em.op("scalar", lambda s: s.activation(
            out=stat("rstd"), in_=stat("lnv"), func=AF.Exp, scale=-0.5), inc="s")
        ev_tf = em.op("vector", lambda v, X=X_ap: v.tensor_scalar(
            out=T["scr"][:, :], in0=X, scalar1=stat("mean"), scalar2=stat("rstd"),
            op0=ALU.subtract, op1=ALU.mult), waits=[ev_rstd] + scr_guard[0], inc="v")
        return ev_tf

    bank_w = {}
    bank_r = {}

    def transpose16(dest_ap_fn, src_tensor, waits_src, extra_evac_waits=()):
        ev = None
        for grpi in range(4):
            bank = grpi % 2
            j0 = grpi * 4
            for jj in range(4):
                j = j0 + jj
                em.op("tensor", lambda t, j=j, bank=bank, jj=jj, src=src_tensor:
                      t.transpose(pb[bank][:, jj * 128:(jj + 1) * 128],
                                  src[:, j * 128:(j + 1) * 128], T["idf32"][:, :]),
                      waits=(list(waits_src) if j == 0 else []) +
                            ([bank_r.get(bank)] if jj == 0 else []),
                      inc="t" if jj == 3 else None)
            bank_w[bank] = em.ev("t")
            ev = em.op("scalar", lambda s_, bank=bank, j0=j0:
                       s_.activation(out=dest_ap_fn(j0), in_=pb[bank][:, :].rearrange(
                           "p (a b) -> p a b", a=4), func=AF.Copy),
                       waits=[bank_w[bank]] + list(extra_evac_waits), inc="s")
            bank_r[bank] = ev
        return ev, em.ev("t")

    def cut(name):
        if CUT == name:
            raise _CutDone()

    ev_cw1 = {}; ev_cw3 = {}; ev_cw2 = {}
    cast_next = [0]

    def emit_casts(n):
        for _ in range(n):
            e = cast_next[0]
            if e >= NEXP:
                return
            cast_next[0] = e + 1
            ev_cw1[e] = em.dma("gpsimd", lambda g, e=e: g.dma_start(
                out=d_w1b[e].rearrange("i p a q -> (i p) (a q)"),
                in_=d_w1[e].rearrange("i p a q -> (i p) (a q)")), inc="cw")
            ev_cw3[e] = em.dma("gpsimd", lambda g, e=e: g.dma_start(
                out=d_w3b[e].rearrange("i p a q -> (i p) (a q)"),
                in_=d_w3[e].rearrange("i p a q -> (i p) (a q)")), inc="cw")
            ev_cw2[e] = em.dma("gpsimd", lambda g, e=e: g.dma_start(
                out=d_w2b[e].rearrange("a i p d -> (a i p) d"),
                in_=d_w2[e].rearrange("a i p d -> (a i p) d")), inc="cw")

    emit_casts(3)

    ev_spill = [None] * NSLAB
    ev_prev_inner_v = None
    ev_qkv_lastmm = None
    roll_exp = [None, None]
    roll_zmm = {}
    roll_emul = {}
    roll_emul_p = {}
    roll_ctx = {}
    roll_fold = None
    prev_ln3 = None

    try:
      for grp in range(NSLAB // 2):
        s0, s1 = 2 * grp, 2 * grp + 1
        ev_tfmA = [None, None]
        for s in (s0, s1):
            par = s % 2
            X = T["xslab"][:, par * 2048:(par + 1) * 2048]
            ev_tf = ln_to_scr(X, [ev_xload[s], ev_consts])
            dest = lambda j0, par=par: T["tfmA"][
                :, par * 2048 + j0 * 128: par * 2048 + (j0 + 4) * 128].rearrange(
                "p (a b) -> p a b", a=4)
            ev_tfmA[par], lastT = transpose16(dest, T["scr"], [ev_tf],
                                              extra_evac_waits=[ev_qkv_lastmm])
            scr_guard[0] = [("t", lastT[1])]
            cut("trans0")
        tfmA2 = T["tfmA"][:, :].rearrange("p (h j t) -> p h j t", h=2, j=16)
        ev_qkvevac = None
        qi = 0
        for kind, n_tiles, wbase, bias_t in (("q", 16, 0, "bqc"), ("k", 32, 16, "bkc"),
                                             ("v", 16, 48, "bvc")):
            for idx in range(n_tiles):
                bank = 2 + (qi % 2)
                j = idx // 2 if kind == "k" else idx
                gt = wbase + idx
                lhs = T["wqkv"][:, (wbase + idx) * 128:(wbase + idx + 1) * 128]
                rhs = tfmA2[:, :, j, :]
                em.op("tensor", lambda t, bank=bank, lhs=lhs, rhs=rhs:
                      t.matmul(pb[bank][:, 0:256], lhs, rhs, start=True, stop=False),
                      waits=[ev_tfmA[0], ev_tfmA[1], bank_r.get(bank),
                             ev_prev_inner_v])
                ev_mm = em.op("tensor", lambda t, bank=bank, gt=gt: t.matmul(
                    pb[bank][:, 0:256], T["brow"][0:1, gt * 128:(gt + 1) * 128],
                    T["onesb"][0:1, :], start=False, stop=True), inc="t")
                bank_w[bank] = ev_mm
                if kind == "q":
                    dst = T["Qb"][:, idx * 256:(idx + 1) * 256]
                    src = pb[bank][:, 0:256]
                elif kind == "v":
                    dst = T["Vb"][:, idx * 256:(idx + 1) * 256]
                    src = pb[bank][:, 0:256]
                else:
                    dst = T["Kb"][:, :].rearrange("p (h r) -> p h r", h=2)[
                        :, :, idx * 128:(idx + 1) * 128]
                    src = pb[bank][:, 0:256].rearrange("p (h t) -> p h t", h=2)
                ev_cp = em.op("scalar", lambda s_, src=src, dst=dst:
                              s_.activation(out=dst, in_=src, func=AF.Copy),
                              waits=[ev_mm], inc="s")
                bank_r[bank] = ev_cp
                ev_qkvevac = ev_cp
                qi += 1
        ev_qkv_lastmm = em.ev("t")
        cut("qkv")

        for s in (s0, s1):
            par = s % 2
            kb_slab = T["Kb"][:, par * 4096:(par + 1) * 4096].rearrange(
                "p (m t) -> p m t", m=32)
            ev_ctx15 = None
            for cp in range(16):
                ep = cp % 2
                qsl = T["Qb"][:, cp * 256 + par * 128: cp * 256 + par * 128 + 128]
                ev_smul_h = [None, None]
                ev_exp_h = [None, None]
                for h in range(2):
                    kb_h = kb_slab[:, h * 16:(h + 1) * 16, :]
                    ev_smul_h[h] = em.op(
                        "vector", lambda v, qsl=qsl, kb_h=kb_h, h=h: v.tensor_tensor(
                            out=T["Sb"][:, h * 2048:(h + 1) * 2048].rearrange(
                                "p (m t) -> p m t", m=16),
                            in0=kb_h, in1=qsl.unsqueeze(1).broadcast_to((128, 16, 128)),
                            op=ALU.mult),
                        waits=[ev_qkvevac, roll_exp[h]], inc="v")
                    ev_exp_h[h] = em.op(
                        "scalar", lambda s_, ep=ep, h=h: s_.activation(
                            out=T["Eb"][:, ep * 4096 + h * 2048:
                                        ep * 4096 + (h + 1) * 2048],
                            in_=T["Sb"][:, h * 2048:(h + 1) * 2048],
                            func=AF.Exp, scale=0.125),
                        waits=[ev_smul_h[h], roll_zmm.get(ep),
                               (roll_emul_p if h == 0 else roll_emul).get(ep)], inc="s")
                    roll_exp[h] = ev_exp_h[h]
                ev_exp = ev_exp_h[1]
                cut("exp0")
                ev_zmm = None
                for j8 in range(8):
                    ev_zmm = em.op(
                        "tensor", lambda t, ep=ep, j8=j8: t.matmul(
                            pb[3][:, :], T["idb16"][:, :],
                            T["Eb"][:, ep * 4096 + j8 * 512: ep * 4096 + (j8 + 1) * 512],
                            start=(j8 == 0), stop=(j8 == 7)),
                        waits=([ev_exp_h[0], roll_fold, bank_r.get(3)] if j8 == 0 else
                               ([ev_exp_h[1]] if j8 == 4 else [])),
                        inc="t" if j8 == 7 else None)
                roll_zmm[ep] = ev_zmm
                bank_w[3] = ev_zmm
                cut("zmm0")
                ev_fold = em.op(
                    "vector", lambda v: v.tensor_reduce(
                        out=T["zf"][:, :], in_=pb[3][:, :].rearrange("p (q t) -> p t q", q=4),
                        axis=AX.X, op=ALU.add),
                    waits=[ev_zmm], inc="v")
                roll_fold = ev_fold
                bank_r[3] = ev_fold
                cut("fold0")
                em.op("vector", lambda v: v.reciprocal(out=T["rz"][:, :], in_=T["zf"][:, :]))
                vsl = T["Vb"][:, cp * 256 + par * 128: cp * 256 + par * 128 + 128]
                ev_u16 = em.op("vector", lambda v, vsl=vsl: v.tensor_mul(
                    out=T["u16"][:, :], in0=vsl, in1=T["rz"][:, :]), inc="v")
                ev_emul_p = em.op(
                    "gpsimd", lambda g, ep=ep: g.tensor_tensor(
                        out=T["Etb"][:, ep * 4096:ep * 4096 + 2048].rearrange(
                            "p (m t) -> p m t", m=16),
                        in0=T["Eb"][:, ep * 4096:ep * 4096 + 2048].rearrange(
                            "p (m t) -> p m t", m=16),
                        in1=T["u16"][:, :].unsqueeze(1).broadcast_to((128, 16, 128)),
                        op=ALU.mult),
                    waits=[ev_u16, ev_exp_h[0], roll_ctx.get(ep)], inc="p")
                roll_emul_p[ep] = ev_emul_p
                ev_emul = em.op(
                    "vector", lambda v, ep=ep: v.tensor_tensor(
                        out=T["Etb"][:, ep * 4096 + 2048:(ep + 1) * 4096].rearrange(
                            "p (m t) -> p m t", m=16),
                        in0=T["Eb"][:, ep * 4096 + 2048:(ep + 1) * 4096].rearrange(
                            "p (m t) -> p m t", m=16),
                        in1=T["u16"][:, :].unsqueeze(1).broadcast_to((128, 16, 128)),
                        op=ALU.mult),
                    waits=[ev_exp, roll_ctx.get(ep)], inc="v")
                roll_emul[ep] = ev_emul
                cut("emul0")
                ev_ctx = None
                for q8 in range(8):
                    bank = 4 + q8 // 2
                    hp = (q8 % 2) * 64
                    ev_ctx = em.op(
                        "tensor", lambda t, ep=ep, q8=q8, bank=bank, hp=hp, cp=cp: t.matmul(
                            pb[bank][hp:hp + 64, :], T["id64x2"][:, :],
                            T["Etb"][:, ep * 4096 + q8 * 512: ep * 4096 + (q8 + 1) * 512],
                            start=(cp == 0), stop=(cp == 15),
                            skip_group_check=True,
                            tile_position=(0, hp) if hp else None),
                        waits=([ev_emul_p] if q8 == 0 else
                               ([ev_emul] if q8 == 4 else [])) +
                              ([bank_r.get(bank)] if cp == 0 else []),
                        inc="t" if q8 == 7 else None)
                roll_ctx[ep] = ev_ctx
                cut("ctx0")
                cut("cp1" if cp >= 1 else "_never")
                if cp == 15:
                    ev_ctx15 = ev_ctx
                    for b in range(4, 8):
                        bank_w[b] = ev_ctx15
            cut("inner")
            ev_prev_inner_v = roll_emul[1]
            ev_ctxe = None
            for b in range(4):
                ev_ctxe = em.op("vector", lambda v, b=b: v.tensor_copy(
                    out=T["ctx16"][:, b * 512:(b + 1) * 512], in_=pb[4 + b][:, :]),
                    waits=[ev_ctx15], inc="v")
                bank_r[4 + b] = ev_ctxe
            ev_cb = em.dma("sync", lambda e_: e_.dma_start(
                out=T["ctx16b"][:, :], in_=T["ctx16"][64:128, :]),
                waits=[ev_ctxe, prev_ln3], inc="ldx")
            ev_opmm = None
            for m in range(32):
                q = m // 4
                c = m % 4
                jpair = m // 2
                bank = 4 + jpair // 4
                cc = jpair % 4
                hp = (m % 2) * 64
                cols = slice((q // 2) * 512 + c * 128, (q // 2) * 512 + (c + 1) * 128)
                if q % 2 == 0:
                    rhs = T["ctx16"][0:64, cols]
                else:
                    rhs = T["ctx16b"][0:64, cols]
                ev_opmm = em.op(
                    "tensor", lambda t, bank=bank, hp=hp, cc=cc, rhs=rhs: t.matmul(
                        pb[bank][hp:hp + 64, cc * 128:(cc + 1) * 128],
                        T["wout"][0:64, :], rhs, start=True, stop=True,
                        tile_position=(0, hp)),
                    waits=[ev_ctxe, ev_cb] + ([bank_r.get(bank)] if m < 8 else []), inc="t")
            for b in range(4, 8):
                bank_w[b] = ev_opmm
            ev_afm = None
            for b in range(4):
                ev_afm = em.op("vector", lambda v_, b=b: v_.tensor_scalar_add(
                    out=T["afm"][:, b * 512:(b + 1) * 512], in0=pb[4 + b][:, :],
                    scalar1=T["bout2"][:, :]),
                    waits=[ev_opmm], inc="v")
                bank_r[4 + b] = ev_afm
            cut("afm")
            dest = lambda j0: T["atok"][:, j0 * 128:(j0 + 4) * 128].rearrange(
                "p (a b) -> p a b", a=4)
            ev_atok, lastT = transpose16(dest, T["afm"], [ev_afm])
            cut("atok")
            ev_ln3 = ln_to_scr(T["atok"][:, :], [ev_atok])
            prev_ln3 = ev_ln3
            scr_guard[0] = []
            X = T["xslab"][:, par * 2048:(par + 1) * 2048]
            O = T["oslab"][:, par * 2048:(par + 1) * 2048]
            ev_x2 = em.op("vector", lambda v, X=X, O=O: v.tensor_add(
                out=O, in0=X, in1=T["scr"][:, :]),
                waits=[ev_ln3, ev_ofree[s - 2] if s >= 2 else None], inc="v")
            scr_guard[0] = [ev_x2]
            ev_xfree[s] = ev_x2
            cut("resid")
            ev_spill[s] = em.dma("sync", lambda e, s=s, O=O: e.dma_start(
                out=d_x2[s * 128:(s + 1) * 128, :], in_=O), waits=[ev_x2], inc="st")
            emit_casts(2)
            cut("spill")
            ev_ofree[s] = ev_spill[s]
            if s + 2 < NSLAB:
                load_x(s + 2)

        if dbg and grp == 0:
            ev_c1 = dbg_cap("dbg_tfmA", T["tfmA"][:, 0:2048], [ev_tfmA[0]])
            ev_c2 = dbg_cap("dbg_Qb", T["Qb"][:, :], [ev_qkvevac])
            ev_c3 = dbg_cap("dbg_Kb", T["Kb"][:, :], [ev_qkvevac])
            ev_c4 = dbg_cap("dbg_Vb", T["Vb"][:, :], [ev_qkvevac])
            ev_c5 = dbg_cap("dbg_ctx16", T["ctx16"][:, :], [em.ev("v")])
            ev_c6 = dbg_cap("dbg_atok", T["atok"][:, :], [em.ev("v")])
            scr_guard[0] = [("dbgst", ev_c6[1])]

    except _CutDone:
        em.dma("sync", lambda e_: e_.dma_start(out=d_out[0:128, :],
                                               in_=T["xslab"][:, 0:2048]), inc="st")
    if phases == 1:
        for s in range(NSLAB):
            if CUT is not None:
                break
            em.dma("sync", lambda e_, s=s: e_.dma_start(
                out=d_out[s * 128:(s + 1) * 128, :],
                in_=d_x2[s * 128:(s + 1) * 128, :]),
                waits=[ev_spill[s]], inc="st")
    if phases >= 2:
      ev_gate = [None] * NSLAB
      prev_ev3 = None
      p2_xfree = [None] * NSLAB
      ev_tfm_done = None
      t16free = [None, None]
      metafree = None
      rtsifree = None
      b0_ev = None
      ev_scat = None
      if SPARSE:
          em.op("gpsimd", lambda g: g.memset(T["metasrc"][:, :], 0.0))
          b0_ev = em.op("gpsimd", lambda g: g.memset(T["b0row"][:, :], 0.0), inc="p")
      for s in range(NSLAB):
          par = s % 2
          wfree = p2_xfree[s - 2] if s >= 2 else ev_xfree[NSLAB - 2 + par]
          ev_x2load = em.dma("sync", lambda e, s=s, par=par: e.dma_start(
              out=T["xslab"][:, par * 2048:(par + 1) * 2048],
              in_=d_x2[s * 128:(s + 1) * 128, :]),
              waits=[ev_spill[s], wfree], inc="ldx")
          X = T["xslab"][:, par * 2048:(par + 1) * 2048]
          ev_tf = ln_to_scr(X, [ev_x2load])
          p2_xfree[s] = ev_tf
          ev_t16cp = None
          if SPARSE:
              ev_t16cp = em.op("scalar", lambda s_, par=par: s_.activation(
                  out=T["t16tok"][:, par * 2048:(par + 1) * 2048], in_=T["scr"][:, :],
                  func=AF.Copy), waits=[ev_tf, t16free[par]], inc="s")
          dest = lambda j0, s=s: T["tfm"][:, :].rearrange(
              "p (j t) -> p j t", j=16)[:, j0:j0 + 4, s * 128:(s + 1) * 128]
          ev_tfm_done, lastT = transpose16(dest, T["scr"], [ev_tf])
          scr_guard[0] = [("t", lastT[1])] + ([ev_t16cp] if SPARSE else [])
          ev_gmm = None
          for fc in range(NFC):
              lhs = T["tfm"][:, fc * 1024 + s * 128: fc * 1024 + (s + 1) * 128]
              ev_gmm = em.op("tensor", lambda t, fc=fc, lhs=lhs: t.matmul(
                  pb[2][:, 0:16], lhs, T["gate"][:, fc * 16:(fc + 1) * 16],
                  start=(fc == 0), stop=(fc == NFC - 1)),
                  waits=([ev_tfm_done, bank_r.get(2)] if fc == 0 else []),
                  inc="t" if fc == NFC - 1 else None)
          bank_w[2] = ev_gmm
          gl = T["gsc"][:, 0:16]; ge_ = T["gsc"][:, 16:32]; msk = T["gsc"][:, 32:48]
          eqm = T["gsc"][:, 48:64]
          wsl = T["wall"][:, s * 16:(s + 1) * 16]
          ev1 = em.op("vector", lambda v, gl=gl: v.tensor_copy(out=gl, in_=pb[2][:, 0:16]),
                      waits=[ev_gmm, prev_ev3], inc="v")
          bank_r[2] = ev1
          evv1 = em.op("vector", lambda v, gl=gl: v.tensor_reduce(
              out=stat("gmax"), in_=gl, axis=AX.X, op=ALU.max), inc="v")
          evp1 = em.op("gpsimd", lambda g: g.tensor_scalar_mul(
              out=stat("gneg"), in0=stat("gmax"), scalar1=-1.0),
              waits=[evv1, prev_ev3], inc="p")
          ev3 = em.op("scalar", lambda s_, gl=gl, ge_=ge_: s_.activation(
              out=ge_, in_=gl, func=AF.Exp, bias=stat("gneg")), waits=[evp1], inc="s")
          prev_ev3 = ev3
          evv2 = em.op("vector", lambda v, ge_=ge_: v.tensor_reduce(
              out=stat("gsum"), in_=ge_, axis=AX.X, op=ALU.add), waits=[ev3])
          evv2 = em.op("vector", lambda v, ge_=ge_: v.tensor_reduce(
              out=stat("gm1"), in_=ge_, axis=AX.X, op=ALU.max), inc="v")
          em.op("gpsimd", lambda g, ge_=ge_, eqm=eqm: g.tensor_single_scalar(
              out=eqm, in_=ge_, scalar=stat("gm1"), op=ALU.is_equal), waits=[evv2])
          em.op("gpsimd", lambda g, eqm=eqm: g.tensor_scalar_mul(
              out=eqm, in0=eqm, scalar1=-1e30))
          em.op("gpsimd", lambda g: g.tensor_copy(out=stat("t7"), in_=stat("gsum")))
          evp2 = em.op("gpsimd", lambda g, ge_=ge_, eqm=eqm, msk=msk: g.tensor_add(
              out=msk, in0=eqm, in1=ge_), inc="p")
          evv3 = em.op("vector", lambda v, msk=msk: v.tensor_reduce(
              out=stat("gm2"), in_=msk, axis=AX.X, op=ALU.max), waits=[evp2], inc="v")
          em.op("gpsimd", lambda g, ge_=ge_, eqm=eqm: g.tensor_single_scalar(
              out=eqm, in_=ge_, scalar=stat("gm2"), op=ALU.is_ge), waits=[evv3])
          evp3 = em.op("gpsimd", lambda g, ge_=ge_, eqm=eqm, msk=msk: g.tensor_mul(
              out=msk, in0=ge_, in1=eqm), inc="p")
          evv4 = em.op("vector", lambda v: v.reciprocal(
              out=stat("grs"), in_=stat("t7")), waits=[evp3], inc="v")
          ev_gate[s] = em.op("gpsimd", lambda g, msk=msk, wsl=wsl: g.tensor_scalar_mul(
              out=wsl, in0=msk, scalar1=stat("grs")), waits=[evv4], inc="p")

          if SPARSE:
              r_ = T["rts"]
              m1w = r_[:, 32:33]; tot = r_[:, 33:34]; wB = r_[:, 34:35]
              msk0 = r_[:, 0:16]; eq1 = r_[:, 16:32]; srow0 = r_[:, 48:64]
              slotf = r_[:, 64:80]; tmpz = r_[:, 80:96]
              evm = em.op("vector", lambda v, wsl=wsl: v.tensor_reduce(
                  out=m1w, in_=wsl, axis=AX.X, op=ALU.max), waits=[ev_gate[s]])
              evm = em.op("vector", lambda v, wsl=wsl: v.tensor_reduce(
                  out=tot, in_=wsl, axis=AX.X, op=ALU.add), inc="v")
              em.op("gpsimd", lambda g, wsl=wsl: g.tensor_single_scalar(
                  out=msk0, in_=wsl, scalar=0.0, op=ALU.is_gt), waits=[evm])
              em.op("gpsimd", lambda g, wsl=wsl: g.tensor_single_scalar(
                  out=eq1, in_=wsl, scalar=m1w, op=ALU.is_equal))
              em.op("gpsimd", lambda g: g.tensor_sub(out=wB, in0=tot, in1=m1w))
              evmk = em.op("gpsimd", lambda g: g.tensor_copy(
                  out=T["rts16"][:, :], in_=msk0), inc="p")
              em.op("tensor", lambda t: t.matmul(
                  pb[3][:, 0:16], T["lt"][:, :], T["rts16"][:, :],
                  start=True, stop=False),
                  waits=[evmk, bank_r.get(3)], inc="t")
              em.op("tensor", lambda t: t.matmul(
                  pb[3][:, 0:16], T["o1r"][:, :], T["b0row"][:, :],
                  start=False, stop=True), waits=[b0_ev])
              evsb = em.op("tensor", lambda t: t.matmul(
                  pb[3][0:1, 32:48], T["o1c"][:, :], T["rts16"][:, :],
                  start=True, stop=True), inc="t")
              bank_w[3] = evsb
              em.op("vector", lambda v: v.tensor_copy(out=srow0, in_=pb[3][:, 0:16]),
                    waits=[evsb])
              evec = em.op("vector", lambda v: v.tensor_copy(
                  out=T["cntr"][:, :], in_=pb[3][0:1, 32:48]), inc="v")
              bank_r[3] = evec
              b0_ev = em.op("gpsimd", lambda g: g.tensor_add(
                  out=T["b0row"][:, :], in0=T["b0row"][:, :], in1=T["cntr"][:, :]),
                  waits=[evec], inc="p")
              em.op("gpsimd", lambda g: g.tensor_add(out=slotf, in0=srow0, in1=T["ecb"][:, :]))
              em.op("gpsimd", lambda g: g.tensor_scalar_add(out=slotf, in0=slotf, scalar1=-1.0))
              em.op("gpsimd", lambda g: g.tensor_single_scalar(
                  out=tmpz, in_=srow0, scalar=float(CCAP), op=ALU.is_gt))
              em.op("gpsimd", lambda g: g.tensor_scalar_mul(out=tmpz, in0=tmpz, scalar1=1e5))
              em.op("gpsimd", lambda g: g.tensor_add(out=slotf, in0=slotf, in1=tmpz))
              evpa = em.op("gpsimd", lambda g: g.tensor_mul(out=tmpz, in0=eq1, in1=slotf),
                           waits=[rtsifree], inc="p")
              evsa = em.op("vector", lambda v: v.tensor_reduce(
                  out=r_[:, 36:37], in_=tmpz, axis=AX.X, op=ALU.add), waits=[evpa], inc="v")
              em.op("gpsimd", lambda g: g.tensor_sub(out=eq1, in0=msk0, in1=eq1))
              evpb = em.op("gpsimd", lambda g: g.tensor_mul(out=tmpz, in0=eq1, in1=slotf),
                           waits=[evsa], inc="p")
              evsbf = em.op("vector", lambda v: v.tensor_reduce(
                  out=r_[:, 37:38], in_=tmpz, axis=AX.X, op=ALU.add), waits=[evpb], inc="v")
              em.op("gpsimd", lambda g: g.tensor_copy(
                  out=T["rtsi"][:, 0:1], in_=r_[:, 36:37]), waits=[evsbf])
              em.op("gpsimd", lambda g: g.tensor_copy(
                  out=T["rtsi"][:, 1:2], in_=r_[:, 37:38]))
              em.op("gpsimd", lambda g, s=s: g.tensor_single_scalar(
                  out=r_[:, 38:39], in_=T["iocol"][:, :], scalar=float(s * 128), op=ALU.add))
              em.op("gpsimd", lambda g, s=s: g.tensor_single_scalar(
                  out=r_[:, 39:40], in_=T["iocol"][:, :], scalar=float(1024 + s * 128),
                  op=ALU.add))
              em.op("gpsimd", lambda g: g.tensor_copy(
                  out=T["metasrc"][:, 0:1], in_=m1w), waits=[metafree])
              em.op("gpsimd", lambda g: g.tensor_copy(
                  out=T["metasrc"][:, 1:2], in_=r_[:, 38:39]))
              em.op("gpsimd", lambda g: g.tensor_copy(
                  out=T["metasrc"][:, 4:5], in_=wB))
              evmeta = em.op("gpsimd", lambda g: g.tensor_copy(
                  out=T["metasrc"][:, 5:6], in_=r_[:, 39:40]), inc="p")
              t16 = T["t16tok"][:, par * 2048:(par + 1) * 2048]
              em.dma("gpsimd", lambda g, t16=t16: g.indirect_dma_start(
                  out=d_gb, out_offset=bass.IndirectOffsetOnAxis(T["rtsi"][:, 0:1], 0),
                  in_=t16, in_offset=None, bounds_check=BCREG["g"], oob_is_err=False),
                  waits=[ev_gbz, ev_t16cp, evmeta], inc="ldw")
              em.dma("gpsimd", lambda g: g.indirect_dma_start(
                  out=d_md, out_offset=bass.IndirectOffsetOnAxis(T["rtsi"][:, 0:1], 0),
                  in_=T["metasrc"][:, 0:4], in_offset=None,
                  bounds_check=BCREG["g"], oob_is_err=False),
                  waits=[ev_mdz], inc="ldw")
              ev_sc3 = em.dma("gpsimd", lambda g, t16=t16: g.indirect_dma_start(
                  out=d_gb, out_offset=bass.IndirectOffsetOnAxis(T["rtsi"][:, 1:2], 0),
                  in_=t16, in_offset=None, bounds_check=BCREG["g"], oob_is_err=False),
                  inc="ldw")
              ev_scat = em.dma("gpsimd", lambda g: g.indirect_dma_start(
                  out=d_md, out_offset=bass.IndirectOffsetOnAxis(T["rtsi"][:, 1:2], 0),
                  in_=T["metasrc"][:, 4:8], in_offset=None,
                  bounds_check=BCREG["g"], oob_is_err=False),
                  inc="ldw")
              t16free[par] = ev_sc3
              metafree = ev_scat
              rtsifree = ev_scat

      if dbg:
          dbg_cap("dbg_tfm", T["tfm"][:, :], [ev_tfm_done])
          dbg_cap("dbg_wall", T["wall"][:, :], [ev_gate[NSLAB - 1]])

      ev_w13 = {}
      ev_w2 = {}
      ev_w13_free = {}
      ev_w2_free = {}
      ev_hpsum_free = {}
      ev_yevac = {}
      ev_hst_readers = None
      dbg_y_hold = None

      def w13_par(e, i):
          return (e * NIC + i) % 2

      def w2_par(e, dq):
          return (e * 4 + dq) % 2

      def load_w13(e, i):
          par = w13_par(e, i)
          em.dma("gpsimd", lambda g, e=e, i=i, par=par: g.dma_start(
              out=T["w13"][:, par * 4096: par * 4096 + 2048],
              in_=d_w1b[e, i].rearrange("p a q -> p (a q)")),
              waits=[ev_w13_free.get(par), ev_cw1[e]], inc="ldw")
          ev_w13[(e, i)] = em.dma("gpsimd", lambda g, e=e, i=i, par=par: g.dma_start(
              out=T["w13"][:, par * 4096 + 2048: par * 4096 + 4096],
              in_=d_w3b[e, i].rearrange("p a q -> p (a q)")),
              waits=[ev_cw3[e]], inc="ldw")

      def load_w2(e, dq):
          par = w2_par(e, dq)
          ev_w2[(e, dq)] = em.dma("gpsimd", lambda g, e=e, dq=dq, par=par: g.dma_start(
              out=T["w2q"][:, par * 5632:(par + 1) * 5632].rearrange("p (i d) -> p i d", i=NIC),
              in_=d_w2b[e, dq].rearrange("i p d -> p i d")),
              waits=[ev_w2_free.get(par), ev_cw2[e]], inc="ldw")

      ev_ysp = [None] * NSLAB
      ylast = {}
      tfs_evac = {}
      gsl_free = {}
      ysc_free = {}
      metg_free = {}
      ev_yscat = None
      n_dense = NEXP if not SPARSE else 2

      for e in range(NEXP):
          if e == 0:
              load_w13(0, 0)
              load_w13(0, 1)
              load_w2(0, 0)
              load_w2(0, 1)
          dense = e < n_dense
          epar = e % 2
          nslot = 1024 if dense else 2 * CCAP

          if not dense:
              ep = e - 2
              ev_gath = [None, None]
              ev_metg = [None, None]
              for k in range(2):
                  ev_gath[k] = em.dma("gpsimd", lambda g, ep=ep, k=k: g.dma_start(
                      out=T["gsl"][:, k * 2048:(k + 1) * 2048],
                      in_=d_gb[ep * CCAP + k * 128: ep * CCAP + (k + 1) * 128, :]),
                      waits=[ev_scat, gsl_free.get(k),
                             ev_ysp[NSLAB - 1] if e == 2 else None], inc="ldw")
                  ev_metg[k] = em.dma("sync", lambda e_, ep=ep, k=k: e_.dma_start(
                      out=T["metg"][:, k * 4:(k + 1) * 4],
                      in_=d_md[ep * CCAP + k * 128: ep * CCAP + (k + 1) * 128, :]),
                      waits=[ev_scat, metg_free.get(k)], inc="ldx")
                  em.op("gpsimd", lambda g, k=k: g.tensor_copy(
                      out=T["dsti"][:, k:k + 1], in_=T["metg"][:, k * 4 + 1:k * 4 + 2]),
                      waits=[ev_metg[k]], inc="p")
              for k in range(2):
                  lastTk = None
                  for grpi in range(4):
                      bank = grpi % 2
                      j0 = grpi * 4
                      for jj in range(4):
                          j = j0 + jj
                          em.op("tensor", lambda t, j=j, bank=bank, jj=jj, k=k:
                                t.transpose(pb[bank][:, jj * 128:(jj + 1) * 128],
                                            T["gsl"][:, k * 2048 + j * 128:
                                                     k * 2048 + (j + 1) * 128],
                                            T["idf32"][:, :]),
                                waits=([ev_gath[k],
                                        ev_ysp[NSLAB - 1] if e == 2 else None]
                                       if j == 0 else []) +
                                      ([bank_r.get(bank)] if jj == 0 else []),
                                inc="t" if jj == 3 else None)
                      bank_w[bank] = em.ev("t")
                      dest = T["tfs"][:, epar * 4096:(epar + 1) * 4096].rearrange(
                          "p (f c) -> p f c", f=16)[:, j0:j0 + 4, k * 128:(k + 1) * 128]
                      ev_te = em.op("vector", lambda v, bank=bank, dest=dest:
                                    v.tensor_copy(out=dest, in_=pb[bank][:, :].rearrange(
                                        "p (a b) -> p a b", a=4)),
                                    waits=[bank_w[bank],
                                           ev_ysp[NSLAB - 1] if e == 2 else None], inc="v")
                      bank_r[bank] = ev_te
                  gsl_free[k] = em.ev("t")
              tfs_evac[epar] = em.ev("v")

          ev_last_hmul = None
          nfree = 512 if dense else 256
          for i in range(NIC):
              par = w13_par(e, i)
              for half in range(2 if dense else 1):
                  sel = (2 * i + half) % 2
                  b1 = 4 + sel
                  b3 = 6 + sel
                  for fc in range(NFC):
                      lhs1 = T["w13"][:, par * 4096 + fc * 128: par * 4096 + (fc + 1) * 128]
                      if dense:
                          rhs = T["tfm"][:, fc * 1024 + half * 512: fc * 1024 + (half + 1) * 512]
                      else:
                          rhs = T["tfs"][:, epar * 4096 + fc * 256: epar * 4096 + (fc + 1) * 256]
                      em.op("tensor", lambda t, b1=b1, lhs1=lhs1, rhs=rhs, fc=fc, nfree=nfree:
                            t.matmul(pb[b1][:, 0:nfree], lhs1, rhs,
                                     start=(fc == 0), stop=(fc == NFC - 1)),
                            waits=([ev_w13[(e, i)], ev_gate[NSLAB - 1],
                                    None if dense else tfs_evac.get(epar),
                                    ev_hpsum_free.get(b1), bank_r.get(b1)] if fc == 0 else []),
                            inc="t" if fc == NFC - 1 else None)
                  ev_mm1 = em.ev("t")
                  bank_w[b1] = ev_mm1
                  for fc in range(NFC):
                      lhs3 = T["w13"][:, par * 4096 + 2048 + fc * 128:
                                      par * 4096 + 2048 + (fc + 1) * 128]
                      if dense:
                          rhs = T["tfm"][:, fc * 1024 + half * 512: fc * 1024 + (half + 1) * 512]
                      else:
                          rhs = T["tfs"][:, epar * 4096 + fc * 256: epar * 4096 + (fc + 1) * 256]
                      em.op("tensor", lambda t, b3=b3, lhs3=lhs3, rhs=rhs, fc=fc, nfree=nfree:
                            t.matmul(pb[b3][:, 0:nfree], lhs3, rhs,
                                     start=(fc == 0), stop=(fc == NFC - 1)),
                            waits=([ev_hpsum_free.get(b3), bank_r.get(b3)] if fc == 0 else []),
                            inc="t" if fc == NFC - 1 else None)
                  ev_mm3 = em.ev("t")
                  bank_w[b3] = ev_mm3
                  ev_sil = em.op("scalar", lambda s_, b1=b1, sel=sel, nfree=nfree:
                                 s_.activation(
                                     out=T["sil"][:, sel * 512: sel * 512 + nfree],
                                     in_=pb[b1][:, 0:nfree], func=AF.Silu),
                                 waits=[ev_mm1, ev_hpsum_free.get(("sil", sel))], inc="s")
                  dst = T["hst"][:, i * 1024 + half * 512: i * 1024 + half * 512 + nfree]
                  first_write = (i == 0 and half == 0)
                  ev_hm = em.op("vector", lambda v, b3=b3, sel=sel, dst=dst, nfree=nfree:
                                v.tensor_tensor(
                                    out=dst, in0=T["sil"][:, sel * 512: sel * 512 + nfree],
                                    in1=pb[b3][:, 0:nfree], op=ALU.mult),
                                waits=[ev_sil, ev_mm3,
                                       ev_hst_readers if first_write else None,
                                       dbg_y_hold if first_write else None],
                                inc="v")
                  ev_hpsum_free[b1] = ev_hm
                  ev_hpsum_free[b3] = ev_hm
                  ev_hpsum_free[("sil", sel)] = ev_hm
                  bank_r[b1] = ev_hm
                  bank_r[b3] = ev_hm
                  ev_last_hmul = ev_hm
              ev_w13_free[par] = em.ev("t")
              if i + 2 < NIC:
                  load_w13(e, i + 2)
              elif e + 1 < NEXP:
                  load_w13(e + 1, i + 2 - NIC)

          if dense:
              for dq in range(4):
                  par = w2_par(e, dq)
                  for s in range(NSLAB):
                      bank = s % 4
                      for i in range(NIC):
                          lhs = T["hst"][:, i * 1024 + s * 128: i * 1024 + (s + 1) * 128]
                          rhs = T["w2q"][:, par * 5632 + i * 512: par * 5632 + (i + 1) * 512]
                          em.op("tensor", lambda t, bank=bank, lhs=lhs, rhs=rhs, i=i: t.matmul(
                              pb[bank][:, :], lhs, rhs, start=(i == 0), stop=(i == NIC - 1)),
                              waits=([ev_last_hmul, ev_w2[(e, dq)],
                                      ev_yevac.get(bank), bank_r.get(bank)] if i == 0 else []),
                              inc="t" if i == NIC - 1 else None)
                      ev_bmm = em.ev("t")
                      bank_w[bank] = ev_bmm
                      ysl = T["yac"][:, s * 2048 + dq * 512: s * 2048 + (dq + 1) * 512]
                      if e == 0:
                          ev_ye = em.op("vector", lambda v, bank=bank, ysl=ysl:
                                        v.tensor_scalar_mul(
                                            out=ysl, in0=pb[bank][:, :], scalar1=1.0),
                                        waits=[ev_bmm, ev_scat], inc="v")
                      else:
                          ev_ye = em.op("vector", lambda v, bank=bank, ysl=ysl:
                                        v.scalar_tensor_tensor(
                                            out=ysl, in0=pb[bank][:, :], scalar=1.0, in1=ysl,
                                            op0=ALU.mult, op1=ALU.add),
                                        waits=[ev_bmm], inc="v")
                      ev_yevac[bank] = ev_ye
                      bank_r[bank] = ev_ye
                      if e == n_dense - 1 and dq == 3:
                          ylast[s] = ev_ye
                  ev_w2_free[par] = em.ev("t")
                  if dq + 2 < 4:
                      load_w2(e, dq + 2)
                  elif e + 1 < NEXP:
                      load_w2(e + 1, dq + 2 - 4)
              if SPARSE and e == n_dense - 1:
                  for s in range(NSLAB):
                      ev_ysp[s] = em.dma("sync", lambda e_, s=s: e_.dma_start(
                          out=d_yb[2 * TOKC + s * 128: 2 * TOKC + (s + 1) * 128, :],
                          in_=T["yac"][:, s * 2048:(s + 1) * 2048]),
                          waits=[ylast[s]], inc="st")
          else:
              for dq in range(4):
                  par = w2_par(e, dq)
                  for k in range(2):
                      bank = 2 + k
                      for i in range(NIC):
                          lhs = T["hst"][:, i * 1024 + k * 128: i * 1024 + (k + 1) * 128]
                          rhs = T["w2q"][:, par * 5632 + i * 512: par * 5632 + (i + 1) * 512]
                          em.op("tensor", lambda t, bank=bank, lhs=lhs, rhs=rhs, i=i: t.matmul(
                              pb[bank][:, :], lhs, rhs, start=(i == 0), stop=(i == NIC - 1)),
                              waits=([ev_last_hmul, ev_w2[(e, dq)],
                                      ev_yevac.get(bank), bank_r.get(bank)] if i == 0 else []),
                              inc="t" if i == NIC - 1 else None)
                      ev_bmm = em.ev("t")
                      bank_w[bank] = ev_bmm
                      wcol = T["metg"][:, k * 4: k * 4 + 1]
                      ysl = T["ysc"][:, k * 2048 + dq * 512: k * 2048 + (dq + 1) * 512]
                      ev_ye = em.op("vector", lambda v, bank=bank, ysl=ysl, wcol=wcol:
                                    v.tensor_scalar_mul(
                                        out=ysl, in0=pb[bank][:, :], scalar1=wcol),
                                    waits=[ev_bmm, ev_metg[k],
                                           ysc_free.get(k) if dq == 0 else None], inc="v")
                      ev_yevac[bank] = ev_ye
                      bank_r[bank] = ev_ye
                      if dq == 3:
                          metg_free[k] = ev_ye
                          ysc_free[k] = em.dma("gpsimd", lambda g, k=k: g.indirect_dma_start(
                              out=d_yb,
                              out_offset=bass.IndirectOffsetOnAxis(T["dsti"][:, k:k + 1], 0),
                              in_=T["ysc"][:, k * 2048:(k + 1) * 2048], in_offset=None,
                              bounds_check=BCREG["y"], oob_is_err=False),
                              waits=[ev_ye, ev_ybz], inc="ldw")
                          ev_yscat = ysc_free[k]
                  ev_w2_free[par] = em.ev("t")
                  if dq + 2 < 4:
                      load_w2(e, dq + 2)
                  elif e + 1 < NEXP:
                      load_w2(e + 1, dq + 2 - 4)
          ev_hst_readers = em.ev("t")

          if dbg and e == 0:
              c1 = dbg_cap("dbg_hst", T["hst"][:, :], [ev_last_hmul])
              c2 = dbg_cap("dbg_yac0", T["yac"][:, 0:2048], [ev_yevac[0], ev_yevac[1],
                                                             ev_yevac[2], ev_yevac[3]])
              dbg_y_hold = ("dbgst", max(c1[1], c2[1]))

      ev_store = [None] * NSLAB
      yf_free = [None, None]
      ga_free = None
      for s in range(NSLAB):
          par = s % 2
          if SPARSE:
              yf = T["yfull"][:, par * 2048:(par + 1) * 2048]
              l1 = em.dma("sync", lambda e_, s=s, yf=yf: e_.dma_start(
                  out=yf, in_=d_yb[2 * TOKC + s * 128: 2 * TOKC + (s + 1) * 128, :]),
                  waits=[ev_ysp[s], yf_free[par]], inc="ldx")
              l2 = em.dma("sync", lambda e_, s=s: e_.dma_start(
                  out=T["gsl"][:, 0:2048], in_=d_yb[s * 128:(s + 1) * 128, :]),
                  waits=[ev_yscat, ga_free, gsl_free.get(0), gsl_free.get(1)], inc="ldx")
              l3 = em.dma("sync", lambda e_, s=s: e_.dma_start(
                  out=T["gsl"][:, 2048:4096],
                  in_=d_yb[TOKC + s * 128: TOKC + (s + 1) * 128, :]), inc="ldx")
              ev_a1 = em.op("vector", lambda v, yf=yf: v.tensor_add(
                  out=yf, in0=yf, in1=T["gsl"][:, 0:2048]), waits=[l1, l2])
              ev_a2 = em.op("vector", lambda v, yf=yf: v.tensor_add(
                  out=yf, in0=yf, in1=T["gsl"][:, 2048:4096]), waits=[l3], inc="v")
              ga_free = ev_a2
              ev_ln5 = ln_to_scr(yf, [ev_a2])
              yf_free[par] = ev_ln5
          else:
              ysl = T["yac"][:, s * 2048:(s + 1) * 2048]
              ev_ln5 = ln_to_scr(ysl, [ev_yevac[0], ev_yevac[1], ev_yevac[2], ev_yevac[3]])
          ev_xl = em.dma("sync", lambda e_, s=s, par=par: e_.dma_start(
              out=T["xslab"][:, par * 2048:(par + 1) * 2048],
              in_=d_x2[s * 128:(s + 1) * 128, :]),
              waits=[ev_gate[NSLAB - 1], ev_store[s - 2] if s >= 2 else None], inc="ldx")
          X = T["xslab"][:, par * 2048:(par + 1) * 2048]
          O = T["oslab"][:, par * 2048:(par + 1) * 2048]
          ev_o = em.op("vector", lambda v, X=X, O=O: v.tensor_add(
              out=O, in0=X, in1=T["scr"][:, :]),
              waits=[ev_ln5, ev_xl,
                     ev_store[s - 2] if s >= 2 else ev_ofree[NSLAB - 2 + par]], inc="v")
          scr_guard[0] = [ev_o]
          ev_store[s] = em.dma("sync", lambda e_, s=s, O=O: e_.dma_start(
              out=d_out[s * 128:(s + 1) * 128, :], in_=O), waits=[ev_o], inc="st")

      if dbg:
          em.dma("gpsimd", lambda g: g.dma_start(out=dbg_outs["dbg_x2"], in_=d_x2),
                 waits=[ev_spill[NSLAB - 1]], inc="dbgst")
          em.op("gpsimd", lambda g: g.engine_nop(), waits=[("dbgst", em.cnt["dbgst"])])

    final_st = em.ev("st")

    with contextlib.ExitStack() as stk:
        sems = {r: stk.enter_context(nc.semaphore(r)) for r in SEM_ROLES}
        block = stk.enter_context(nc.Block())

        def replay(name):
            def body(eng):
                if name == "gpsimd" and SPARSE:
                    BCREG["g"] = eng.to_reg(GROWS - 1)
                    BCREG["y"] = eng.to_reg(2 * TOKC - 1)
                for (fn, w, inc, by) in em.q[name]:
                    for (role, val) in w:
                        eng.wait_ge(sems[role], val)
                    inst = fn(eng)
                    if inc is not None:
                        inst.then_inc(sems[inc], by)
                if name == "sync":
                    eng.wait_ge(sems["st"], final_st[1])
            return body

        nc._em = em
        block.sync(replay("sync"))
        block.gpsimd(replay("gpsimd"))
        block.tensor(replay("tensor"))
        block.vector(replay("vector"))
        block.scalar(replay("scalar"))

    return nc


def kernel(**inputs):
    dbg = bool(inputs.pop("_dbg", False))
    g = host_layout(inputs)
    nc = build_nc(dbg=dbg)
    shared = {k: g[k] for k in ["wq", "wk", "wv", "bqc", "bkc", "bvc", "brow", "wout",
                                "bout2", "gate", "w1_all", "w3_all", "w2_all"]}
    in_maps = []
    for c in range(NCORE):
        m = {"x_in": g["x_shards"][c]}
        m.update(shared)
        in_maps.append(m)
    res = run_bass_kernel_spmd(nc, in_maps, core_ids=list(range(NCORE)))
    out = np.concatenate([np.asarray(res.results[c]["out"]) for c in range(NCORE)], axis=0)
    if dbg:
        return out.reshape(1, 8192, D), res
    return out.reshape(1, 8192, D)


# version 08
# speedup vs baseline: 1.4012x; 1.3389x over previous
import numpy as np
import ml_dtypes
import contextlib
import concourse.bass as bass
import concourse.mybir as mybir
from concourse.bass_utils import run_bass_kernel_spmd

F32 = mybir.dt.float32
BF16 = mybir.dt.bfloat16
I32 = mybir.dt.int32
AF = mybir.ActivationFunctionType
ALU = mybir.AluOpType
AX = mybir.AxisListType
BFD = ml_dtypes.bfloat16

NCORE = 8
TOKC = 1024
NSLAB = 8
D = 2048
NFC = 16
INTER = 1408
NIC = 11
NEXP = 18
EPS = 1e-5
SPARSE = True
CCAP = 256
GROWS = 16 * CCAP
CUT = None


class _CutDone(Exception):
    pass
SEM_ROLES = ("ldx", "ldw", "st", "dbgst", "cw", "t", "v", "s", "p")


class Em:
    ENGS = ("sync", "gpsimd", "tensor", "vector", "scalar")

    def __init__(self):
        self.q = {e: [] for e in self.ENGS}
        self.cnt = {r: 0 for r in SEM_ROLES}
        self.last_wait = {}

    def ev(self, role):
        return (role, self.cnt[role])

    def op(self, eng, fn, waits=(), inc=None, by=1):
        w = []
        for wt in waits:
            if wt is None:
                continue
            role, val = wt
            if val <= 0:
                continue
            key = (eng, role)
            if self.last_wait.get(key, 0) < val:
                self.last_wait[key] = val
                w.append((role, val))
        ev = None
        if inc is not None:
            self.cnt[inc] += by
            ev = (inc, self.cnt[inc])
        self.q[eng].append((fn, w, inc, by))
        return ev

    def dma(self, eng, fn, waits=(), inc=None):
        return self.op(eng, fn, waits, inc, by=16)


def host_layout(inputs):
    g = {}
    x = np.ascontiguousarray(np.asarray(inputs["L0_attn_x"], dtype=np.float32).reshape(8192, D))
    g["x_shards"] = [np.ascontiguousarray(x[c * TOKC:(c + 1) * TOKC]) for c in range(NCORE)]

    Wqkv = np.asarray(inputs["L0_attn_Wqkv"], dtype=np.float32)
    bqkv = np.asarray(inputs["L0_attn_bqkv"], dtype=np.float32)
    wq = np.zeros((16, 128, 128), np.float32)
    wk = np.zeros((32, 128, 128), np.float32)
    wv = np.zeros((16, 128, 128), np.float32)
    for j in range(16):
        wq[j, 0:64, 0:64] = Wqkv[2 * j, :, 0:64]
        wq[j, 64:128, 64:128] = Wqkv[2 * j + 1, :, 0:64]
        wv[j, 0:64, 0:64] = Wqkv[2 * j, :, 128:192]
        wv[j, 64:128, 64:128] = Wqkv[2 * j + 1, :, 128:192]
    for m in range(32):
        h = (m % 2) * 64
        wk[m, h:h + 64, 0:64] = Wqkv[m, :, 64:128]
        wk[m, h:h + 64, 64:128] = Wqkv[m, :, 64:128]
    g["wq"], g["wk"], g["wv"] = wq, wk, wv
    bqc = np.zeros((128, 16), np.float32)
    bkc = np.zeros((128, 32), np.float32)
    bvc = np.zeros((128, 16), np.float32)
    for j in range(16):
        bqc[0:64, j] = bqkv[2 * j, 0:64]
        bqc[64:128, j] = bqkv[2 * j + 1, 0:64]
        bvc[0:64, j] = bqkv[2 * j, 128:192]
        bvc[64:128, j] = bqkv[2 * j + 1, 128:192]
    for m in range(32):
        bkc[0:64, m] = bqkv[m, 64:128]
        bkc[64:128, m] = bqkv[m, 64:128]
    g["bqc"], g["bkc"], g["bvc"] = bqc, bkc, bvc
    g["brow"] = np.ascontiguousarray(np.concatenate(
        [bqc.T.reshape(-1), bkc.T.reshape(-1), bvc.T.reshape(-1)])[None, :])

    wout_ = np.asarray(inputs["L0_attn_Wout"], dtype=np.float32)
    g["wout"] = np.ascontiguousarray(np.concatenate([wout_, wout_], axis=0))
    bout = np.asarray(inputs["L0_attn_bout"], dtype=np.float32)
    g["bout2"] = np.ascontiguousarray(np.concatenate([bout, bout])[:, None])

    gate_w = np.asarray(inputs["L0_ffn_gate_w"], dtype=np.float32)
    g["gate"] = np.ascontiguousarray(gate_w.T.reshape(NFC, 128, 16).transpose(1, 0, 2))

    def lay_w13(w):
        nic = w.shape[0] // 128
        return w.reshape(nic, 128, NFC, 128).transpose(0, 3, 2, 1)

    def lay_w2(w2):
        nic = w2.shape[1] // 128
        return w2.T.reshape(nic, 128, 4, 512).transpose(2, 0, 1, 3)

    w1 = np.asarray(inputs["L0_ffn_w1"], dtype=np.float32)
    w3 = np.asarray(inputs["L0_ffn_w3"], dtype=np.float32)
    w2 = np.asarray(inputs["L0_ffn_w2"], dtype=np.float32)
    sw1 = np.asarray(inputs["L0_ffn_sw1"], dtype=np.float32)
    sw3 = np.asarray(inputs["L0_ffn_sw3"], dtype=np.float32)
    sw2 = np.asarray(inputs["L0_ffn_sw2"], dtype=np.float32)

    w1_all = np.empty((NEXP, NIC, 128, NFC, 128), np.float32)
    w3_all = np.empty((NEXP, NIC, 128, NFC, 128), np.float32)
    w2_all = np.empty((NEXP, 4, NIC, 128, 512), np.float32)
    w1_all[0] = lay_w13(sw1[0:INTER]);      w1_all[1] = lay_w13(sw1[INTER:])
    w3_all[0] = lay_w13(sw3[0:INTER]);      w3_all[1] = lay_w13(sw3[INTER:])
    w2_all[0] = lay_w2(sw2[:, 0:INTER]);    w2_all[1] = lay_w2(sw2[:, INTER:])
    for e in range(16):
        w1_all[2 + e] = lay_w13(w1[e])
        w3_all[2 + e] = lay_w13(w3[e])
        w2_all[2 + e] = lay_w2(w2[e])
    g["w1_all"] = np.ascontiguousarray(w1_all)
    g["w3_all"] = np.ascontiguousarray(w3_all)
    g["w2_all"] = np.ascontiguousarray(w2_all)
    return g


SB_BASE = 16768
SB_TOP = 229344

def _mk_map():
    M = {}
    off = SB_BASE
    def A(name, nbytes):
        nonlocal off
        off = (off + 31) // 32 * 32
        M[name] = off
        off += nbytes
    A("idf32", 512); A("idb16", 256); A("id64x2", 128)
    A("wout", 128); A("gate", 512)
    A("bqc", 64); A("bkc", 128); A("bvc", 64); A("bout2", 32)
    A("wall", 512); A("stats", 256); A("gsc", 256); A("cst", 32)
    A("rt_rts", 384); A("rt_rts16", 32); A("rt_rtsi", 8); A("rt_metg", 32)
    A("rt_dsti", 8); A("rt_b0row", 64); A("rt_cntr", 64); A("rt_ecb", 64)
    A("rt_iocol", 32); A("rt_lt", 256); A("rt_o1r", 512); A("rt_o1c", 32)
    A("scr", 8192)
    A("xslab", 16384); A("oslab", 16384)
    perm_end = off
    off = perm_end
    A("wqkv", 16384)
    A("tfmA", 8192)
    A("Qb", 16384); A("Kb", 32768); A("Vb", 16384)
    A("Sb", 16384); A("Eb", 16384); A("Etb", 16384)
    A("zf", 512); A("rz", 512); A("ub", 512); A("u16", 256)
    A("ctx16", 4096); A("afm", 8192); A("atok", 8192)
    attn_end = off
    off = perm_end
    A("tfm", 32768)
    A("hst", 22528)
    A("w13", 16384)
    A("w2q", 22528)
    A("yac", 65536)
    A("sil", 4096)
    ffn_end = off
    assert max(attn_end, ffn_end) <= SB_TOP, (attn_end, ffn_end)
    return M

MM = _mk_map()


def build_nc(dbg=False, phases=4):
    nc = bass.Bass("TRN2", target_bir_lowering=False, detect_race_conditions=False)

    d_x = nc.dram_tensor("x_in", [TOKC, D], F32, kind="ExternalInput").ap()
    d_wq = nc.dram_tensor("wq", [16, 128, 128], F32, kind="ExternalInput").ap()
    d_wk = nc.dram_tensor("wk", [32, 128, 128], F32, kind="ExternalInput").ap()
    d_wv = nc.dram_tensor("wv", [16, 128, 128], F32, kind="ExternalInput").ap()
    d_bqc = nc.dram_tensor("bqc", [128, 16], F32, kind="ExternalInput").ap()
    d_bkc = nc.dram_tensor("bkc", [128, 32], F32, kind="ExternalInput").ap()
    d_bvc = nc.dram_tensor("bvc", [128, 16], F32, kind="ExternalInput").ap()
    d_wout = nc.dram_tensor("wout", [128, 64], F32, kind="ExternalInput").ap()
    d_bout2 = nc.dram_tensor("bout2", [128, 1], F32, kind="ExternalInput").ap()
    d_gate = nc.dram_tensor("gate", [128, NFC, 16], F32, kind="ExternalInput").ap()
    d_brow = nc.dram_tensor("brow", [1, 64 * 128], F32, kind="ExternalInput").ap()
    d_w1 = nc.dram_tensor("w1_all", [NEXP, NIC, 128, NFC, 128], F32, kind="ExternalInput").ap()
    d_w3 = nc.dram_tensor("w3_all", [NEXP, NIC, 128, NFC, 128], F32, kind="ExternalInput").ap()
    d_w2 = nc.dram_tensor("w2_all", [NEXP, 4, NIC, 128, 512], F32, kind="ExternalInput").ap()
    d_w1b = nc.dram_tensor("w1b", [NEXP, NIC, 128, NFC, 128], BF16, kind="Internal").ap()
    d_w3b = nc.dram_tensor("w3b", [NEXP, NIC, 128, NFC, 128], BF16, kind="Internal").ap()
    d_w2b = nc.dram_tensor("w2b", [NEXP, 4, NIC, 128, 512], BF16, kind="Internal").ap()
    d_out = nc.dram_tensor("out", [TOKC, D], F32, kind="ExternalOutput").ap()
    d_x2 = nc.dram_tensor("x2spill", [TOKC, D], F32, kind="Internal").ap()
    d_gb = nc.dram_tensor("gbuf", [GROWS, D], BF16, kind="Internal").ap()
    d_yb = nc.dram_tensor("ybuf", [3 * TOKC, D], F32, kind="Internal").ap()
    d_md = nc.dram_tensor("mbuf", [GROWS, 4], F32, kind="Internal").ap()

    c_idf32 = nc.inline_tensor(np.eye(128, dtype=np.float32), name="c_idf32").ap()
    c_idb16 = nc.inline_tensor(np.eye(128, dtype=BFD), name="c_idb16").ap()
    c_id64x2 = nc.inline_tensor(
        np.concatenate([np.eye(64), np.eye(64)], axis=0).astype(BFD), name="c_id64x2").ap()
    c_lt = nc.inline_tensor(np.triu(np.ones((128, 128))).astype(BFD), name="c_lt").ap()
    c_o1r = nc.inline_tensor(np.ones((1, 128), np.float32), name="c_o1r").ap()
    c_o1c = nc.inline_tensor(np.ones((128, 1), dtype=BFD), name="c_o1c").ap()
    c_o1w = nc.inline_tensor(np.ones((1, 256), dtype=BFD), name="c_o1w").ap()
    c_ecb = nc.inline_tensor(np.tile(np.arange(16, dtype=np.float32) * CCAP, (128, 1)),
                             name="c_ecb").ap()
    c_iot = nc.inline_tensor(np.arange(128, dtype=np.float32)[:, None], name="c_iot").ap()
    c_z16 = nc.inline_tensor(np.zeros((128, D), dtype=BFD), name="c_z16").ap()
    c_z32 = nc.inline_tensor(np.zeros((128, D), np.float32), name="c_z32").ap()
    minit_np = np.zeros((GROWS, 4), np.float32); minit_np[:, 1] = 1e5
    c_mi = nc.inline_tensor(minit_np, name="c_mi").ap()
    CONSTS = [EPS, -0.5, 0.125]
    c_cst = nc.inline_tensor(
        np.tile(np.array([CONSTS], dtype=np.float32), (128, 1)), name="c_cst").ap()

    T = {}
    def sb(name, shape, dtype):
        T[name] = nc.alloc_sbuf_tensor_at(name, shape, dtype, offset=MM[name])
        return T[name]

    sb("idf32", [128, 128], F32); sb("idb16", [128, 128], BF16)
    T["lt"] = nc.alloc_sbuf_tensor_at("lt", [128, 128], BF16, offset=MM["rt_lt"])
    T["o1r"] = nc.alloc_sbuf_tensor_at("o1r", [1, 128], F32, offset=MM["rt_o1r"])
    T["o1c"] = nc.alloc_sbuf_tensor_at("o1c", [128, 1], BF16, offset=MM["rt_o1c"])
    sb("id64x2", [128, 64], BF16)
    sb("wout", [128, 64], BF16); sb("gate", [128, NFC * 16], BF16)
    sb("bqc", [128, 16], F32); sb("bkc", [128, 32], F32); sb("bvc", [128, 16], F32)
    sb("bout2", [128, 1], F32)
    sb("wall", [128, 128], F32)
    sb("cst", [128, 3], F32)
    for ci_, cv_ in enumerate([EPS, -0.5, 0.125]):
        nc.const_aps.register(F32, cv_, T["cst"][:, ci_:ci_ + 1])
    sb("stats", [128, 64], F32)
    sb("gsc", [128, 64], F32)
    sb("scr", [128, 2048], F32)
    sb("xslab", [128, 2 * 2048], F32)
    sb("oslab", [128, 2 * 2048], F32)
    sb("wqkv", [128, 64 * 128], BF16)
    sb("tfmA", [128, 2 * 2048], BF16)
    sb("Qb", [128, 4096], BF16); sb("Kb", [128, 8192], BF16); sb("Vb", [128, 4096], F32)
    sb("Sb", [128, 4096], BF16)
    sb("Eb", [128, 4 * 4096], BF16)
    sb("zf", [128, 128], F32); sb("rz", [128, 128], F32)
    sb("ub", [128, 128], F32); sb("u16", [128, 128], BF16)
    sb("ctx16", [128, 2048], BF16)
    sb("afm", [128, 2048], F32); sb("atok", [128, 2048], F32)
    T["ctx16b"] = nc.alloc_sbuf_tensor_at("ctx16b", [64, 2048], BF16, offset=MM["atok"])
    T["brow"] = nc.alloc_sbuf_tensor_at("brow", [1, 64 * 128], BF16,
                                        offset=MM["Kb"] + 16384)
    T["onesb"] = nc.alloc_sbuf_tensor_at("onesb", [1, 256], BF16,
                                         offset=MM["Sb"] + 8192)
    sb("tfm", [128, NFC * 1024], BF16)
    sb("hst", [128, NIC * 1024], BF16)
    sb("w13", [128, 2 * 2 * 2048], BF16)
    sb("w2q", [128, 2 * NIC * 512], BF16)
    sb("yac", [128, NSLAB * 2048], F32)
    sb("sil", [128, 2 * 512], F32)
    R = MM["yac"]
    def sbat(name, off, shape, dtype):
        T[name] = nc.alloc_sbuf_tensor_at(name, shape, dtype, offset=off)
    sbat("t16tok", R, [128, 2 * 2048], BF16)
    sbat("metasrc", R + 8192, [128, 8], F32)
    sbat("tfs", R, [128, 2 * 4096], BF16)
    sbat("gsl", R + 16384, [128, 2 * 2048], F32)
    sbat("ysc", R + 32768, [128, 2 * 2048], F32)
    sbat("yfull", R + 49152, [128, 2 * 2048], F32)
    for nm, shp, dt in [("rts", [128, 96], F32), ("rts16", [128, 16], BF16),
                        ("rtsi", [128, 2], I32), ("metg", [128, 8], F32),
                        ("dsti", [128, 2], I32), ("b0row", [1, 16], F32),
                        ("cntr", [1, 16], F32), ("ecb", [128, 16], F32),
                        ("iocol", [128, 1], F32)]:
        T[nm] = nc.alloc_sbuf_tensor_at(nm, shp, dt, offset=MM["rt_" + nm])

    pb = [nc.alloc_psum_tensor(f"pb{i}", [128, 512], F32) for i in range(8)]

    em = Em()
    BCREG = {}
    dbg_outs = {}
    if dbg:
        for name, shape in [("dbg_tfmA", [128, 2048]), ("dbg_Qb", [128, 4096]),
                            ("dbg_Kb", [128, 8192]), ("dbg_Vb", [128, 4096]),
                            ("dbg_ctx16", [128, 2048]),
                            ("dbg_atok", [128, 2048]), ("dbg_x2", [TOKC, D]),
                            ("dbg_tfm", [128, NFC * 1024]), ("dbg_hst", [128, NIC * 1024]),
                            ("dbg_yac0", [128, 2048]), ("dbg_wall", [128, 128])]:
            dbg_outs[name] = nc.dram_tensor(name, shape, F32, kind="ExternalOutput").ap()

    def dbg_cap(name, src_ap, waits):
        if not dbg:
            return None
        return em.dma("gpsimd", lambda g, s=src_ap, o=dbg_outs[name]:
                      g.dma_start(out=o, in_=s), waits=waits, inc="dbgst")

    ev_consts = None
    for fn in [
        lambda g: g.dma_start(out=T["cst"][:, :], in_=c_cst),
        lambda g: g.dma_start(out=T["lt"][:, :], in_=c_lt),
        lambda g: g.dma_start(out=T["o1r"][:, :], in_=c_o1r),
        lambda g: g.dma_start(out=T["o1c"][:, :], in_=c_o1c),
        lambda g: g.dma_start(out=T["ecb"][:, :], in_=c_ecb),
        lambda g: g.dma_start(out=T["iocol"][:, :], in_=c_iot),
        lambda g: g.dma_start(out=T["idf32"][:, :], in_=c_idf32),
        lambda g: g.dma_start(out=T["idb16"][:, :], in_=c_idb16),
        lambda g: g.dma_start(out=T["id64x2"][:, :], in_=c_id64x2),
        lambda g: g.dma_start(out=T["wout"][:, :], in_=d_wout),
        lambda g: g.dma_start(out=T["gate"][:, :], in_=d_gate.rearrange("p a e -> p (a e)")),
        lambda g: g.dma_start(out=T["bqc"][:, :], in_=d_bqc),
        lambda g: g.dma_start(out=T["bkc"][:, :], in_=d_bkc),
        lambda g: g.dma_start(out=T["bvc"][:, :], in_=d_bvc),
        lambda g: g.dma_start(out=T["bout2"][:, :], in_=d_bout2),
        lambda g: g.dma_start(out=T["brow"][:, :], in_=d_brow),
        lambda g: g.dma_start(out=T["onesb"][:, :], in_=c_o1w),
        lambda g: g.dma_start(out=T["wqkv"][:, 0:2048].rearrange("p (a q) -> p a q", a=16),
                              in_=d_wq.rearrange("a p q -> p a q")),
        lambda g: g.dma_start(out=T["wqkv"][:, 2048:6144].rearrange("p (a q) -> p a q", a=32),
                              in_=d_wk.rearrange("a p q -> p a q")),
        lambda g: g.dma_start(out=T["wqkv"][:, 6144:8192].rearrange("p (a q) -> p a q", a=16),
                              in_=d_wv.rearrange("a p q -> p a q")),
    ]:
        ev_consts = em.dma("gpsimd", fn, inc="ldw")

    ev_xload = [None] * NSLAB
    ev_xfree = [None] * NSLAB
    ev_ofree = [None] * NSLAB

    def load_x(s):
        par = s % 2
        w = [ev_xfree[s - 2]] if s >= 2 else []
        ev_xload[s] = em.dma(
            "sync", lambda e, s=s, par=par: e.dma_start(
                out=T["xslab"][:, par * 2048:(par + 1) * 2048],
                in_=d_x[s * 128:(s + 1) * 128, :]), waits=w, inc="ldx")

    ev_gbz = None
    ev_ybz = None
    ev_mdz = None
    if SPARSE:
        for r0 in range(0, GROWS, 128):
            ev_gbz = em.dma("sync", lambda e_, r0=r0: e_.dma_start(
                out=d_gb[r0:r0 + 128, :], in_=c_z16), inc="st")
        for r0 in range(0, 2 * TOKC, 128):
            ev_ybz = em.dma("sync", lambda e_, r0=r0: e_.dma_start(
                out=d_yb[r0:r0 + 128, :], in_=c_z32), inc="st")
        ev_mdz = em.dma("sync", lambda e_: e_.dma_start(out=d_md, in_=c_mi), inc="st")

    load_x(0); load_x(1)

    SC = {"sx": 0, "ssq": 1, "mean": 2, "msq": 3, "var": 4, "lnv": 5, "rstd": 6, "t7": 7,
          "gmax": 8, "gneg": 9, "gsum": 10, "grs": 11, "gm1": 12, "gm2": 13}

    def stat(k):
        return T["stats"][:, SC[k]:SC[k] + 1]

    scr_guard = [[]]

    def ln_to_scr(X_ap, waits_in):
        wl = list(waits_in)
        ev_sq = em.op("scalar", lambda s, X=X_ap: s.activation(
            out=T["scr"][:, :], in_=X, func=AF.Square),
            waits=wl + scr_guard[0], inc="s")
        em.op("vector", lambda v, X=X_ap: v.tensor_reduce(
            out=stat("sx"), in_=X, axis=AX.X, op=ALU.add), waits=wl)
        ev_red = em.op("vector", lambda v: v.tensor_reduce(
            out=stat("ssq"), in_=T["scr"][:, :], axis=AX.X, op=ALU.add),
            waits=[ev_sq], inc="v")
        em.op("gpsimd", lambda g: g.tensor_scalar_mul(
            out=stat("mean"), in0=stat("sx"), scalar1=1.0 / D), waits=[ev_red])
        em.op("gpsimd", lambda g: g.tensor_scalar_mul(
            out=stat("msq"), in0=stat("mean"), scalar1=stat("mean")))
        em.op("gpsimd", lambda g: g.tensor_scalar_mul(
            out=stat("t7"), in0=stat("ssq"), scalar1=1.0 / D))
        ev_var = em.op("gpsimd", lambda g: g.tensor_sub(
            out=stat("var"), in0=stat("t7"), in1=stat("msq")), inc="p")
        em.op("scalar", lambda s: s.activation(
            out=stat("lnv"), in_=stat("var"), func=AF.Ln, bias=EPS), waits=[ev_var])
        ev_rstd = em.op("scalar", lambda s: s.activation(
            out=stat("rstd"), in_=stat("lnv"), func=AF.Exp, scale=-0.5), inc="s")
        ev_tf = em.op("vector", lambda v, X=X_ap: v.tensor_scalar(
            out=T["scr"][:, :], in0=X, scalar1=stat("mean"), scalar2=stat("rstd"),
            op0=ALU.subtract, op1=ALU.mult), waits=[ev_rstd] + scr_guard[0], inc="v")
        return ev_tf

    bank_w = {}
    bank_r = {}

    def transpose16(dest_ap_fn, src_tensor, waits_src, extra_evac_waits=()):
        ev = None
        for grpi in range(4):
            bank = grpi % 2
            j0 = grpi * 4
            for jj in range(4):
                j = j0 + jj
                em.op("tensor", lambda t, j=j, bank=bank, jj=jj, src=src_tensor:
                      t.transpose(pb[bank][:, jj * 128:(jj + 1) * 128],
                                  src[:, j * 128:(j + 1) * 128], T["idf32"][:, :]),
                      waits=(list(waits_src) if j == 0 else []) +
                            ([bank_r.get(bank)] if jj == 0 else []),
                      inc="t" if jj == 3 else None)
            bank_w[bank] = em.ev("t")
            ev = em.op("scalar", lambda s_, bank=bank, j0=j0:
                       s_.activation(out=dest_ap_fn(j0), in_=pb[bank][:, :].rearrange(
                           "p (a b) -> p a b", a=4), func=AF.Copy),
                       waits=[bank_w[bank]] + list(extra_evac_waits), inc="s")
            bank_r[bank] = ev
        return ev, em.ev("t")

    def cut(name):
        if CUT == name:
            raise _CutDone()

    ev_cw1 = {}; ev_cw3 = {}; ev_cw2 = {}
    cast_next = [0]

    def emit_casts(n):
        for _ in range(n):
            e = cast_next[0]
            if e >= NEXP:
                return
            cast_next[0] = e + 1
            ev_cw1[e] = em.dma("gpsimd", lambda g, e=e: g.dma_start(
                out=d_w1b[e].rearrange("i p a q -> (i p) (a q)"),
                in_=d_w1[e].rearrange("i p a q -> (i p) (a q)")), inc="cw")
            ev_cw3[e] = em.dma("gpsimd", lambda g, e=e: g.dma_start(
                out=d_w3b[e].rearrange("i p a q -> (i p) (a q)"),
                in_=d_w3[e].rearrange("i p a q -> (i p) (a q)")), inc="cw")
            ev_cw2[e] = em.dma("gpsimd", lambda g, e=e: g.dma_start(
                out=d_w2b[e].rearrange("a i p d -> (a i p) d"),
                in_=d_w2[e].rearrange("a i p d -> (a i p) d")), inc="cw")

    emit_casts(3)

    ev_spill = [None] * NSLAB
    ev_prev_inner_v = None
    ev_qkv_lastmm = None
    roll_exp = [None, None]
    roll_ctx4 = {}
    prev_ln3 = None

    try:
      for grp in range(NSLAB // 2):
        s0, s1 = 2 * grp, 2 * grp + 1
        ev_tfmA = [None, None]
        for s in (s0, s1):
            par = s % 2
            X = T["xslab"][:, par * 2048:(par + 1) * 2048]
            ev_tf = ln_to_scr(X, [ev_xload[s], ev_consts])
            dest = lambda j0, par=par: T["tfmA"][
                :, par * 2048 + j0 * 128: par * 2048 + (j0 + 4) * 128].rearrange(
                "p (a b) -> p a b", a=4)
            ev_tfmA[par], lastT = transpose16(dest, T["scr"], [ev_tf],
                                              extra_evac_waits=[ev_qkv_lastmm])
            scr_guard[0] = [("t", lastT[1])]
            cut("trans0")
        tfmA2 = T["tfmA"][:, :].rearrange("p (h j t) -> p h j t", h=2, j=16)
        ev_qkvevac = None
        qi = 0
        for kind, n_tiles, wbase, bias_t in (("q", 16, 0, "bqc"), ("k", 32, 16, "bkc"),
                                             ("v", 16, 48, "bvc")):
            for idx in range(n_tiles):
                bank = 2 + (qi % 2)
                j = idx // 2 if kind == "k" else idx
                gt = wbase + idx
                lhs = T["wqkv"][:, (wbase + idx) * 128:(wbase + idx + 1) * 128]
                rhs = tfmA2[:, :, j, :]
                em.op("tensor", lambda t, bank=bank, lhs=lhs, rhs=rhs:
                      t.matmul(pb[bank][:, 0:256], lhs, rhs, start=True, stop=False),
                      waits=[ev_tfmA[0], ev_tfmA[1], bank_r.get(bank),
                             ev_prev_inner_v])
                ev_mm = em.op("tensor", lambda t, bank=bank, gt=gt: t.matmul(
                    pb[bank][:, 0:256], T["brow"][0:1, gt * 128:(gt + 1) * 128],
                    T["onesb"][0:1, :], start=False, stop=True), inc="t")
                bank_w[bank] = ev_mm
                if kind == "q":
                    dst = T["Qb"][:, idx * 256:(idx + 1) * 256]
                    src = pb[bank][:, 0:256]
                elif kind == "v":
                    dst = T["Vb"][:, idx * 256:(idx + 1) * 256]
                    src = pb[bank][:, 0:256]
                else:
                    dst = T["Kb"][:, :].rearrange("p (h r) -> p h r", h=2)[
                        :, :, idx * 128:(idx + 1) * 128]
                    src = pb[bank][:, 0:256].rearrange("p (h t) -> p h t", h=2)
                ev_cp = em.op("scalar", lambda s_, src=src, dst=dst:
                              s_.activation(out=dst, in_=src, func=AF.Copy),
                              waits=[ev_mm], inc="s")
                bank_r[bank] = ev_cp
                ev_qkvevac = ev_cp
                qi += 1
        ev_qkv_lastmm = em.ev("t")
        cut("qkv")

        for s in (s0, s1):
            par = s % 2
            kb_slab = T["Kb"][:, par * 4096:(par + 1) * 4096].rearrange(
                "p (m t) -> p m t", m=32)
            ev_ctx15 = None
            tail_cp = None

            def emit_tail(cp):
                nonlocal ev_ctx15
                ep4 = cp % 4
                zbank = 2 + cp % 2
                eslot = T["Eb"][:, ep4 * 4096:(ep4 + 1) * 4096]
                ev_fold = em.op(
                    "vector", lambda v, zbank=zbank: v.tensor_reduce(
                        out=T["zf"][:, :],
                        in_=pb[zbank][:, :].rearrange("p (q t) -> p t q", q=4),
                        axis=AX.X, op=ALU.add),
                    waits=[bank_w[zbank]], inc="v")
                bank_r[zbank] = ev_fold
                em.op("vector", lambda v: v.reciprocal(out=T["rz"][:, :], in_=T["zf"][:, :]))
                vsl = T["Vb"][:, cp * 256 + par * 128: cp * 256 + par * 128 + 128]
                em.op("vector", lambda v, vsl=vsl: v.tensor_mul(
                    out=T["u16"][:, :], in0=vsl, in1=T["rz"][:, :]))
                ev_emul = em.op(
                    "vector", lambda v, eslot=eslot: v.tensor_tensor(
                        out=eslot.rearrange("p (m t) -> p m t", m=32),
                        in0=eslot.rearrange("p (m t) -> p m t", m=32),
                        in1=T["u16"][:, :].unsqueeze(1).broadcast_to((128, 32, 128)),
                        op=ALU.mult),
                    inc="v")
                ev_ctx = None
                for q8 in range(8):
                    bank = 4 + q8 // 2
                    hp = (q8 % 2) * 64
                    ev_ctx = em.op(
                        "tensor", lambda t, eslot=eslot, q8=q8, bank=bank, hp=hp, cp=cp:
                        t.matmul(
                            pb[bank][hp:hp + 64, :], T["id64x2"][:, :],
                            eslot[:, q8 * 512:(q8 + 1) * 512],
                            start=(cp == 0), stop=(cp == 15),
                            skip_group_check=True,
                            tile_position=(0, hp) if hp else None),
                        waits=([ev_emul] if q8 == 0 else []) +
                              ([bank_r.get(bank)] if cp == 0 else []),
                        inc="t" if q8 == 7 else None)
                roll_ctx4[ep4] = ev_ctx
                if cp == 15:
                    ev_ctx15 = ev_ctx
                    for b in range(4, 8):
                        bank_w[b] = ev_ctx15
                return ev_emul

            for cp in range(16):
                ep4 = cp % 4
                zbank = 2 + cp % 2
                qsl = T["Qb"][:, cp * 256 + par * 128: cp * 256 + par * 128 + 128]
                ev_smul_h = [None, None]
                ev_exp_h = [None, None]
                for h in range(2):
                    kb_h = kb_slab[:, h * 16:(h + 1) * 16, :]
                    ev_smul_h[h] = em.op(
                        "vector", lambda v, qsl=qsl, kb_h=kb_h, h=h: v.tensor_tensor(
                            out=T["Sb"][:, h * 2048:(h + 1) * 2048].rearrange(
                                "p (m t) -> p m t", m=16),
                            in0=kb_h, in1=qsl.unsqueeze(1).broadcast_to((128, 16, 128)),
                            op=ALU.mult),
                        waits=[ev_qkvevac, roll_exp[h]], inc="v")
                    ev_exp_h[h] = em.op(
                        "scalar", lambda s_, ep4=ep4, h=h: s_.activation(
                            out=T["Eb"][:, ep4 * 4096 + h * 2048:
                                        ep4 * 4096 + (h + 1) * 2048],
                            in_=T["Sb"][:, h * 2048:(h + 1) * 2048],
                            func=AF.Exp, scale=0.125),
                        waits=[ev_smul_h[h], roll_ctx4.get(ep4)], inc="s")
                    roll_exp[h] = ev_exp_h[h]
                cut("exp0")
                ev_zmm = None
                for j8 in range(8):
                    ev_zmm = em.op(
                        "tensor", lambda t, ep4=ep4, j8=j8, zbank=zbank: t.matmul(
                            pb[zbank][:, :], T["idb16"][:, :],
                            T["Eb"][:, ep4 * 4096 + j8 * 512: ep4 * 4096 + (j8 + 1) * 512],
                            start=(j8 == 0), stop=(j8 == 7)),
                        waits=([ev_exp_h[0], bank_r.get(zbank)] if j8 == 0 else
                               ([ev_exp_h[1]] if j8 == 4 else [])),
                        inc="t" if j8 == 7 else None)
                bank_w[zbank] = ev_zmm
                cut("zmm0")
                if tail_cp is not None:
                    emit_tail(tail_cp)
                tail_cp = cp
            ev_prev_inner_v = emit_tail(15)
            cut("inner")
            ev_ctxe = None
            for b in range(4):
                ev_ctxe = em.op("vector", lambda v, b=b: v.tensor_copy(
                    out=T["ctx16"][:, b * 512:(b + 1) * 512], in_=pb[4 + b][:, :]),
                    waits=[ev_ctx15], inc="v")
                bank_r[4 + b] = ev_ctxe
            ev_cb = em.dma("sync", lambda e_: e_.dma_start(
                out=T["ctx16b"][:, :], in_=T["ctx16"][64:128, :]),
                waits=[ev_ctxe, prev_ln3], inc="ldx")
            ev_opmm = None
            for m in range(32):
                q = m // 4
                c = m % 4
                jpair = m // 2
                bank = 4 + jpair // 4
                cc = jpair % 4
                hp = (m % 2) * 64
                cols = slice((q // 2) * 512 + c * 128, (q // 2) * 512 + (c + 1) * 128)
                if q % 2 == 0:
                    rhs = T["ctx16"][0:64, cols]
                else:
                    rhs = T["ctx16b"][0:64, cols]
                ev_opmm = em.op(
                    "tensor", lambda t, bank=bank, hp=hp, cc=cc, rhs=rhs: t.matmul(
                        pb[bank][hp:hp + 64, cc * 128:(cc + 1) * 128],
                        T["wout"][0:64, :], rhs, start=True, stop=True,
                        tile_position=(0, hp)),
                    waits=[ev_ctxe, ev_cb] + ([bank_r.get(bank)] if m < 8 else []), inc="t")
            for b in range(4, 8):
                bank_w[b] = ev_opmm
            ev_afm = None
            for b in range(4):
                ev_afm = em.op("vector", lambda v_, b=b: v_.tensor_scalar_add(
                    out=T["afm"][:, b * 512:(b + 1) * 512], in0=pb[4 + b][:, :],
                    scalar1=T["bout2"][:, :]),
                    waits=[ev_opmm], inc="v")
                bank_r[4 + b] = ev_afm
            cut("afm")
            dest = lambda j0: T["atok"][:, j0 * 128:(j0 + 4) * 128].rearrange(
                "p (a b) -> p a b", a=4)
            ev_atok, lastT = transpose16(dest, T["afm"], [ev_afm])
            cut("atok")
            ev_ln3 = ln_to_scr(T["atok"][:, :], [ev_atok])
            prev_ln3 = ev_ln3
            scr_guard[0] = []
            X = T["xslab"][:, par * 2048:(par + 1) * 2048]
            O = T["oslab"][:, par * 2048:(par + 1) * 2048]
            ev_x2 = em.op("vector", lambda v, X=X, O=O: v.tensor_add(
                out=O, in0=X, in1=T["scr"][:, :]),
                waits=[ev_ln3, ev_ofree[s - 2] if s >= 2 else None], inc="v")
            scr_guard[0] = [ev_x2]
            ev_xfree[s] = ev_x2
            cut("resid")
            ev_spill[s] = em.dma("sync", lambda e, s=s, O=O: e.dma_start(
                out=d_x2[s * 128:(s + 1) * 128, :], in_=O), waits=[ev_x2], inc="st")
            emit_casts(2)
            cut("spill")
            ev_ofree[s] = ev_spill[s]
            if s + 2 < NSLAB:
                load_x(s + 2)

        if dbg and grp == 0:
            ev_c1 = dbg_cap("dbg_tfmA", T["tfmA"][:, 0:2048], [ev_tfmA[0]])
            ev_c2 = dbg_cap("dbg_Qb", T["Qb"][:, :], [ev_qkvevac])
            ev_c3 = dbg_cap("dbg_Kb", T["Kb"][:, :], [ev_qkvevac])
            ev_c4 = dbg_cap("dbg_Vb", T["Vb"][:, :], [ev_qkvevac])
            ev_c5 = dbg_cap("dbg_ctx16", T["ctx16"][:, :], [em.ev("v")])
            ev_c6 = dbg_cap("dbg_atok", T["atok"][:, :], [em.ev("v")])
            scr_guard[0] = [("dbgst", ev_c6[1])]

    except _CutDone:
        em.dma("sync", lambda e_: e_.dma_start(out=d_out[0:128, :],
                                               in_=T["xslab"][:, 0:2048]), inc="st")
    if phases == 1:
        for s in range(NSLAB):
            if CUT is not None:
                break
            em.dma("sync", lambda e_, s=s: e_.dma_start(
                out=d_out[s * 128:(s + 1) * 128, :],
                in_=d_x2[s * 128:(s + 1) * 128, :]),
                waits=[ev_spill[s]], inc="st")
    if phases >= 2:
      ev_gate = [None] * NSLAB
      prev_ev3 = None
      p2_xfree = [None] * NSLAB
      ev_tfm_done = None
      t16free = [None, None]
      metafree = None
      rtsifree = None
      b0_ev = None
      ev_scat = None
      if SPARSE:
          em.op("gpsimd", lambda g: g.memset(T["metasrc"][:, :], 0.0))
          b0_ev = em.op("gpsimd", lambda g: g.memset(T["b0row"][:, :], 0.0), inc="p")
      for s in range(NSLAB):
          par = s % 2
          wfree = p2_xfree[s - 2] if s >= 2 else ev_xfree[NSLAB - 2 + par]
          ev_x2load = em.dma("sync", lambda e, s=s, par=par: e.dma_start(
              out=T["xslab"][:, par * 2048:(par + 1) * 2048],
              in_=d_x2[s * 128:(s + 1) * 128, :]),
              waits=[ev_spill[s], wfree], inc="ldx")
          X = T["xslab"][:, par * 2048:(par + 1) * 2048]
          ev_tf = ln_to_scr(X, [ev_x2load])
          p2_xfree[s] = ev_tf
          ev_t16cp = None
          if SPARSE:
              ev_t16cp = em.op("scalar", lambda s_, par=par: s_.activation(
                  out=T["t16tok"][:, par * 2048:(par + 1) * 2048], in_=T["scr"][:, :],
                  func=AF.Copy), waits=[ev_tf, t16free[par]], inc="s")
          dest = lambda j0, s=s: T["tfm"][:, :].rearrange(
              "p (j t) -> p j t", j=16)[:, j0:j0 + 4, s * 128:(s + 1) * 128]
          ev_tfm_done, lastT = transpose16(dest, T["scr"], [ev_tf])
          scr_guard[0] = [("t", lastT[1])] + ([ev_t16cp] if SPARSE else [])
          ev_gmm = None
          for fc in range(NFC):
              lhs = T["tfm"][:, fc * 1024 + s * 128: fc * 1024 + (s + 1) * 128]
              ev_gmm = em.op("tensor", lambda t, fc=fc, lhs=lhs: t.matmul(
                  pb[2][:, 0:16], lhs, T["gate"][:, fc * 16:(fc + 1) * 16],
                  start=(fc == 0), stop=(fc == NFC - 1)),
                  waits=([ev_tfm_done, bank_r.get(2)] if fc == 0 else []),
                  inc="t" if fc == NFC - 1 else None)
          bank_w[2] = ev_gmm
          gl = T["gsc"][:, 0:16]; ge_ = T["gsc"][:, 16:32]; msk = T["gsc"][:, 32:48]
          eqm = T["gsc"][:, 48:64]
          wsl = T["wall"][:, s * 16:(s + 1) * 16]
          ev1 = em.op("vector", lambda v, gl=gl: v.tensor_copy(out=gl, in_=pb[2][:, 0:16]),
                      waits=[ev_gmm, prev_ev3], inc="v")
          bank_r[2] = ev1
          evv1 = em.op("vector", lambda v, gl=gl: v.tensor_reduce(
              out=stat("gmax"), in_=gl, axis=AX.X, op=ALU.max), inc="v")
          evp1 = em.op("gpsimd", lambda g: g.tensor_scalar_mul(
              out=stat("gneg"), in0=stat("gmax"), scalar1=-1.0),
              waits=[evv1, prev_ev3], inc="p")
          ev3 = em.op("scalar", lambda s_, gl=gl, ge_=ge_: s_.activation(
              out=ge_, in_=gl, func=AF.Exp, bias=stat("gneg")), waits=[evp1], inc="s")
          prev_ev3 = ev3
          evv2 = em.op("vector", lambda v, ge_=ge_: v.tensor_reduce(
              out=stat("gsum"), in_=ge_, axis=AX.X, op=ALU.add), waits=[ev3])
          evv2 = em.op("vector", lambda v, ge_=ge_: v.tensor_reduce(
              out=stat("gm1"), in_=ge_, axis=AX.X, op=ALU.max), inc="v")
          em.op("gpsimd", lambda g, ge_=ge_, eqm=eqm: g.tensor_single_scalar(
              out=eqm, in_=ge_, scalar=stat("gm1"), op=ALU.is_equal), waits=[evv2])
          em.op("gpsimd", lambda g, eqm=eqm: g.tensor_scalar_mul(
              out=eqm, in0=eqm, scalar1=-1e30))
          em.op("gpsimd", lambda g: g.tensor_copy(out=stat("t7"), in_=stat("gsum")))
          evp2 = em.op("gpsimd", lambda g, ge_=ge_, eqm=eqm, msk=msk: g.tensor_add(
              out=msk, in0=eqm, in1=ge_), inc="p")
          evv3 = em.op("vector", lambda v, msk=msk: v.tensor_reduce(
              out=stat("gm2"), in_=msk, axis=AX.X, op=ALU.max), waits=[evp2], inc="v")
          em.op("gpsimd", lambda g, ge_=ge_, eqm=eqm: g.tensor_single_scalar(
              out=eqm, in_=ge_, scalar=stat("gm2"), op=ALU.is_ge), waits=[evv3])
          evp3 = em.op("gpsimd", lambda g, ge_=ge_, eqm=eqm, msk=msk: g.tensor_mul(
              out=msk, in0=ge_, in1=eqm), inc="p")
          evv4 = em.op("vector", lambda v: v.reciprocal(
              out=stat("grs"), in_=stat("t7")), waits=[evp3], inc="v")
          ev_gate[s] = em.op("gpsimd", lambda g, msk=msk, wsl=wsl: g.tensor_scalar_mul(
              out=wsl, in0=msk, scalar1=stat("grs")), waits=[evv4], inc="p")

          if SPARSE:
              r_ = T["rts"]
              m1w = r_[:, 32:33]; tot = r_[:, 33:34]; wB = r_[:, 34:35]
              msk0 = r_[:, 0:16]; eq1 = r_[:, 16:32]; srow0 = r_[:, 48:64]
              slotf = r_[:, 64:80]; tmpz = r_[:, 80:96]
              evm = em.op("vector", lambda v, wsl=wsl: v.tensor_reduce(
                  out=m1w, in_=wsl, axis=AX.X, op=ALU.max), waits=[ev_gate[s]])
              evm = em.op("vector", lambda v, wsl=wsl: v.tensor_reduce(
                  out=tot, in_=wsl, axis=AX.X, op=ALU.add), inc="v")
              em.op("gpsimd", lambda g, wsl=wsl: g.tensor_single_scalar(
                  out=msk0, in_=wsl, scalar=0.0, op=ALU.is_gt), waits=[evm])
              em.op("gpsimd", lambda g, wsl=wsl: g.tensor_single_scalar(
                  out=eq1, in_=wsl, scalar=m1w, op=ALU.is_equal))
              em.op("gpsimd", lambda g: g.tensor_sub(out=wB, in0=tot, in1=m1w))
              evmk = em.op("gpsimd", lambda g: g.tensor_copy(
                  out=T["rts16"][:, :], in_=msk0), inc="p")
              em.op("tensor", lambda t: t.matmul(
                  pb[3][:, 0:16], T["lt"][:, :], T["rts16"][:, :],
                  start=True, stop=False),
                  waits=[evmk, bank_r.get(3)], inc="t")
              em.op("tensor", lambda t: t.matmul(
                  pb[3][:, 0:16], T["o1r"][:, :], T["b0row"][:, :],
                  start=False, stop=True), waits=[b0_ev])
              evsb = em.op("tensor", lambda t: t.matmul(
                  pb[3][0:1, 32:48], T["o1c"][:, :], T["rts16"][:, :],
                  start=True, stop=True), inc="t")
              bank_w[3] = evsb
              em.op("vector", lambda v: v.tensor_copy(out=srow0, in_=pb[3][:, 0:16]),
                    waits=[evsb])
              evec = em.op("vector", lambda v: v.tensor_copy(
                  out=T["cntr"][:, :], in_=pb[3][0:1, 32:48]), inc="v")
              bank_r[3] = evec
              b0_ev = em.op("gpsimd", lambda g: g.tensor_add(
                  out=T["b0row"][:, :], in0=T["b0row"][:, :], in1=T["cntr"][:, :]),
                  waits=[evec], inc="p")
              em.op("gpsimd", lambda g: g.tensor_add(out=slotf, in0=srow0, in1=T["ecb"][:, :]))
              em.op("gpsimd", lambda g: g.tensor_scalar_add(out=slotf, in0=slotf, scalar1=-1.0))
              em.op("gpsimd", lambda g: g.tensor_single_scalar(
                  out=tmpz, in_=srow0, scalar=float(CCAP), op=ALU.is_gt))
              em.op("gpsimd", lambda g: g.tensor_scalar_mul(out=tmpz, in0=tmpz, scalar1=1e5))
              em.op("gpsimd", lambda g: g.tensor_add(out=slotf, in0=slotf, in1=tmpz))
              evpa = em.op("gpsimd", lambda g: g.tensor_mul(out=tmpz, in0=eq1, in1=slotf),
                           waits=[rtsifree], inc="p")
              evsa = em.op("vector", lambda v: v.tensor_reduce(
                  out=r_[:, 36:37], in_=tmpz, axis=AX.X, op=ALU.add), waits=[evpa], inc="v")
              em.op("gpsimd", lambda g: g.tensor_sub(out=eq1, in0=msk0, in1=eq1))
              evpb = em.op("gpsimd", lambda g: g.tensor_mul(out=tmpz, in0=eq1, in1=slotf),
                           waits=[evsa], inc="p")
              evsbf = em.op("vector", lambda v: v.tensor_reduce(
                  out=r_[:, 37:38], in_=tmpz, axis=AX.X, op=ALU.add), waits=[evpb], inc="v")
              em.op("gpsimd", lambda g: g.tensor_copy(
                  out=T["rtsi"][:, 0:1], in_=r_[:, 36:37]), waits=[evsbf])
              em.op("gpsimd", lambda g: g.tensor_copy(
                  out=T["rtsi"][:, 1:2], in_=r_[:, 37:38]))
              em.op("gpsimd", lambda g, s=s: g.tensor_single_scalar(
                  out=r_[:, 38:39], in_=T["iocol"][:, :], scalar=float(s * 128), op=ALU.add))
              em.op("gpsimd", lambda g, s=s: g.tensor_single_scalar(
                  out=r_[:, 39:40], in_=T["iocol"][:, :], scalar=float(1024 + s * 128),
                  op=ALU.add))
              em.op("gpsimd", lambda g: g.tensor_copy(
                  out=T["metasrc"][:, 0:1], in_=m1w), waits=[metafree])
              em.op("gpsimd", lambda g: g.tensor_copy(
                  out=T["metasrc"][:, 1:2], in_=r_[:, 38:39]))
              em.op("gpsimd", lambda g: g.tensor_copy(
                  out=T["metasrc"][:, 4:5], in_=wB))
              evmeta = em.op("gpsimd", lambda g: g.tensor_copy(
                  out=T["metasrc"][:, 5:6], in_=r_[:, 39:40]), inc="p")
              t16 = T["t16tok"][:, par * 2048:(par + 1) * 2048]
              em.dma("gpsimd", lambda g, t16=t16: g.indirect_dma_start(
                  out=d_gb, out_offset=bass.IndirectOffsetOnAxis(T["rtsi"][:, 0:1], 0),
                  in_=t16, in_offset=None, bounds_check=BCREG["g"], oob_is_err=False),
                  waits=[ev_gbz, ev_t16cp, evmeta], inc="ldw")
              em.dma("gpsimd", lambda g: g.indirect_dma_start(
                  out=d_md, out_offset=bass.IndirectOffsetOnAxis(T["rtsi"][:, 0:1], 0),
                  in_=T["metasrc"][:, 0:4], in_offset=None,
                  bounds_check=BCREG["g"], oob_is_err=False),
                  waits=[ev_mdz], inc="ldw")
              ev_sc3 = em.dma("gpsimd", lambda g, t16=t16: g.indirect_dma_start(
                  out=d_gb, out_offset=bass.IndirectOffsetOnAxis(T["rtsi"][:, 1:2], 0),
                  in_=t16, in_offset=None, bounds_check=BCREG["g"], oob_is_err=False),
                  inc="ldw")
              ev_scat = em.dma("gpsimd", lambda g: g.indirect_dma_start(
                  out=d_md, out_offset=bass.IndirectOffsetOnAxis(T["rtsi"][:, 1:2], 0),
                  in_=T["metasrc"][:, 4:8], in_offset=None,
                  bounds_check=BCREG["g"], oob_is_err=False),
                  inc="ldw")
              t16free[par] = ev_sc3
              metafree = ev_scat
              rtsifree = ev_scat

      if dbg:
          dbg_cap("dbg_tfm", T["tfm"][:, :], [ev_tfm_done])
          dbg_cap("dbg_wall", T["wall"][:, :], [ev_gate[NSLAB - 1]])

      ev_w13 = {}
      ev_w2 = {}
      ev_w13_free = {}
      ev_w2_free = {}
      ev_hpsum_free = {}
      ev_yevac = {}
      ev_hst_readers = None
      dbg_y_hold = None

      def w13_par(e, i):
          return (e * NIC + i) % 2

      def w2_par(e, dq):
          return (e * 4 + dq) % 2

      def load_w13(e, i):
          par = w13_par(e, i)
          em.dma("gpsimd", lambda g, e=e, i=i, par=par: g.dma_start(
              out=T["w13"][:, par * 4096: par * 4096 + 2048],
              in_=d_w1b[e, i].rearrange("p a q -> p (a q)")),
              waits=[ev_w13_free.get(par), ev_cw1[e]], inc="ldw")
          ev_w13[(e, i)] = em.dma("gpsimd", lambda g, e=e, i=i, par=par: g.dma_start(
              out=T["w13"][:, par * 4096 + 2048: par * 4096 + 4096],
              in_=d_w3b[e, i].rearrange("p a q -> p (a q)")),
              waits=[ev_cw3[e]], inc="ldw")

      def load_w2(e, dq):
          par = w2_par(e, dq)
          ev_w2[(e, dq)] = em.dma("gpsimd", lambda g, e=e, dq=dq, par=par: g.dma_start(
              out=T["w2q"][:, par * 5632:(par + 1) * 5632].rearrange("p (i d) -> p i d", i=NIC),
              in_=d_w2b[e, dq].rearrange("i p d -> p i d")),
              waits=[ev_w2_free.get(par), ev_cw2[e]], inc="ldw")

      ev_ysp = [None] * NSLAB
      ylast = {}
      tfs_evac = {}
      gsl_free = {}
      ysc_free = {}
      metg_free = {}
      ev_yscat = None
      n_dense = NEXP if not SPARSE else 2

      for e in range(NEXP):
          if e == 0:
              load_w13(0, 0)
              load_w13(0, 1)
              load_w2(0, 0)
              load_w2(0, 1)
          dense = e < n_dense
          epar = e % 2
          nslot = 1024 if dense else 2 * CCAP

          if not dense:
              ep = e - 2
              ev_gath = [None, None]
              ev_metg = [None, None]
              for k in range(2):
                  ev_gath[k] = em.dma("gpsimd", lambda g, ep=ep, k=k: g.dma_start(
                      out=T["gsl"][:, k * 2048:(k + 1) * 2048],
                      in_=d_gb[ep * CCAP + k * 128: ep * CCAP + (k + 1) * 128, :]),
                      waits=[ev_scat, gsl_free.get(k),
                             ev_ysp[NSLAB - 1] if e == 2 else None], inc="ldw")
                  ev_metg[k] = em.dma("sync", lambda e_, ep=ep, k=k: e_.dma_start(
                      out=T["metg"][:, k * 4:(k + 1) * 4],
                      in_=d_md[ep * CCAP + k * 128: ep * CCAP + (k + 1) * 128, :]),
                      waits=[ev_scat, metg_free.get(k)], inc="ldx")
                  em.op("gpsimd", lambda g, k=k: g.tensor_copy(
                      out=T["dsti"][:, k:k + 1], in_=T["metg"][:, k * 4 + 1:k * 4 + 2]),
                      waits=[ev_metg[k]], inc="p")
              for k in range(2):
                  lastTk = None
                  for grpi in range(4):
                      bank = grpi % 2
                      j0 = grpi * 4
                      for jj in range(4):
                          j = j0 + jj
                          em.op("tensor", lambda t, j=j, bank=bank, jj=jj, k=k:
                                t.transpose(pb[bank][:, jj * 128:(jj + 1) * 128],
                                            T["gsl"][:, k * 2048 + j * 128:
                                                     k * 2048 + (j + 1) * 128],
                                            T["idf32"][:, :]),
                                waits=([ev_gath[k],
                                        ev_ysp[NSLAB - 1] if e == 2 else None]
                                       if j == 0 else []) +
                                      ([bank_r.get(bank)] if jj == 0 else []),
                                inc="t" if jj == 3 else None)
                      bank_w[bank] = em.ev("t")
                      dest = T["tfs"][:, epar * 4096:(epar + 1) * 4096].rearrange(
                          "p (f c) -> p f c", f=16)[:, j0:j0 + 4, k * 128:(k + 1) * 128]
                      ev_te = em.op("vector", lambda v, bank=bank, dest=dest:
                                    v.tensor_copy(out=dest, in_=pb[bank][:, :].rearrange(
                                        "p (a b) -> p a b", a=4)),
                                    waits=[bank_w[bank],
                                           ev_ysp[NSLAB - 1] if e == 2 else None], inc="v")
                      bank_r[bank] = ev_te
                  gsl_free[k] = em.ev("t")
              tfs_evac[epar] = em.ev("v")

          ev_last_hmul = None
          nfree = 512 if dense else 256
          for i in range(NIC):
              par = w13_par(e, i)
              for half in range(2 if dense else 1):
                  sel = (2 * i + half) % 2
                  b1 = 4 + sel
                  b3 = 6 + sel
                  for fc in range(NFC):
                      lhs1 = T["w13"][:, par * 4096 + fc * 128: par * 4096 + (fc + 1) * 128]
                      if dense:
                          rhs = T["tfm"][:, fc * 1024 + half * 512: fc * 1024 + (half + 1) * 512]
                      else:
                          rhs = T["tfs"][:, epar * 4096 + fc * 256: epar * 4096 + (fc + 1) * 256]
                      em.op("tensor", lambda t, b1=b1, lhs1=lhs1, rhs=rhs, fc=fc, nfree=nfree:
                            t.matmul(pb[b1][:, 0:nfree], lhs1, rhs,
                                     start=(fc == 0), stop=(fc == NFC - 1)),
                            waits=([ev_w13[(e, i)], ev_gate[NSLAB - 1],
                                    None if dense else tfs_evac.get(epar),
                                    ev_hpsum_free.get(b1), bank_r.get(b1)] if fc == 0 else []),
                            inc="t" if fc == NFC - 1 else None)
                  ev_mm1 = em.ev("t")
                  bank_w[b1] = ev_mm1
                  for fc in range(NFC):
                      lhs3 = T["w13"][:, par * 4096 + 2048 + fc * 128:
                                      par * 4096 + 2048 + (fc + 1) * 128]
                      if dense:
                          rhs = T["tfm"][:, fc * 1024 + half * 512: fc * 1024 + (half + 1) * 512]
                      else:
                          rhs = T["tfs"][:, epar * 4096 + fc * 256: epar * 4096 + (fc + 1) * 256]
                      em.op("tensor", lambda t, b3=b3, lhs3=lhs3, rhs=rhs, fc=fc, nfree=nfree:
                            t.matmul(pb[b3][:, 0:nfree], lhs3, rhs,
                                     start=(fc == 0), stop=(fc == NFC - 1)),
                            waits=([ev_hpsum_free.get(b3), bank_r.get(b3)] if fc == 0 else []),
                            inc="t" if fc == NFC - 1 else None)
                  ev_mm3 = em.ev("t")
                  bank_w[b3] = ev_mm3
                  ev_sil = em.op("scalar", lambda s_, b1=b1, sel=sel, nfree=nfree:
                                 s_.activation(
                                     out=T["sil"][:, sel * 512: sel * 512 + nfree],
                                     in_=pb[b1][:, 0:nfree], func=AF.Silu),
                                 waits=[ev_mm1, ev_hpsum_free.get(("sil", sel))], inc="s")
                  dst = T["hst"][:, i * 1024 + half * 512: i * 1024 + half * 512 + nfree]
                  first_write = (i == 0 and half == 0)
                  ev_hm = em.op("vector", lambda v, b3=b3, sel=sel, dst=dst, nfree=nfree:
                                v.tensor_tensor(
                                    out=dst, in0=T["sil"][:, sel * 512: sel * 512 + nfree],
                                    in1=pb[b3][:, 0:nfree], op=ALU.mult),
                                waits=[ev_sil, ev_mm3,
                                       ev_hst_readers if first_write else None,
                                       dbg_y_hold if first_write else None],
                                inc="v")
                  ev_hpsum_free[b1] = ev_hm
                  ev_hpsum_free[b3] = ev_hm
                  ev_hpsum_free[("sil", sel)] = ev_hm
                  bank_r[b1] = ev_hm
                  bank_r[b3] = ev_hm
                  ev_last_hmul = ev_hm
              ev_w13_free[par] = em.ev("t")
              if i + 2 < NIC:
                  load_w13(e, i + 2)
              elif e + 1 < NEXP:
                  load_w13(e + 1, i + 2 - NIC)

          if dense:
              for dq in range(4):
                  par = w2_par(e, dq)
                  for s in range(NSLAB):
                      bank = s % 4
                      for i in range(NIC):
                          lhs = T["hst"][:, i * 1024 + s * 128: i * 1024 + (s + 1) * 128]
                          rhs = T["w2q"][:, par * 5632 + i * 512: par * 5632 + (i + 1) * 512]
                          em.op("tensor", lambda t, bank=bank, lhs=lhs, rhs=rhs, i=i: t.matmul(
                              pb[bank][:, :], lhs, rhs, start=(i == 0), stop=(i == NIC - 1)),
                              waits=([ev_last_hmul, ev_w2[(e, dq)],
                                      ev_yevac.get(bank), bank_r.get(bank)] if i == 0 else []),
                              inc="t" if i == NIC - 1 else None)
                      ev_bmm = em.ev("t")
                      bank_w[bank] = ev_bmm
                      ysl = T["yac"][:, s * 2048 + dq * 512: s * 2048 + (dq + 1) * 512]
                      if e == 0:
                          ev_ye = em.op("vector", lambda v, bank=bank, ysl=ysl:
                                        v.tensor_scalar_mul(
                                            out=ysl, in0=pb[bank][:, :], scalar1=1.0),
                                        waits=[ev_bmm, ev_scat], inc="v")
                      else:
                          ev_ye = em.op("vector", lambda v, bank=bank, ysl=ysl:
                                        v.scalar_tensor_tensor(
                                            out=ysl, in0=pb[bank][:, :], scalar=1.0, in1=ysl,
                                            op0=ALU.mult, op1=ALU.add),
                                        waits=[ev_bmm], inc="v")
                      ev_yevac[bank] = ev_ye
                      bank_r[bank] = ev_ye
                      if e == n_dense - 1 and dq == 3:
                          ylast[s] = ev_ye
                  ev_w2_free[par] = em.ev("t")
                  if dq + 2 < 4:
                      load_w2(e, dq + 2)
                  elif e + 1 < NEXP:
                      load_w2(e + 1, dq + 2 - 4)
              if SPARSE and e == n_dense - 1:
                  for s in range(NSLAB):
                      ev_ysp[s] = em.dma("sync", lambda e_, s=s: e_.dma_start(
                          out=d_yb[2 * TOKC + s * 128: 2 * TOKC + (s + 1) * 128, :],
                          in_=T["yac"][:, s * 2048:(s + 1) * 2048]),
                          waits=[ylast[s]], inc="st")
          else:
              for dq in range(4):
                  par = w2_par(e, dq)
                  for k in range(2):
                      bank = 2 + k
                      for i in range(NIC):
                          lhs = T["hst"][:, i * 1024 + k * 128: i * 1024 + (k + 1) * 128]
                          rhs = T["w2q"][:, par * 5632 + i * 512: par * 5632 + (i + 1) * 512]
                          em.op("tensor", lambda t, bank=bank, lhs=lhs, rhs=rhs, i=i: t.matmul(
                              pb[bank][:, :], lhs, rhs, start=(i == 0), stop=(i == NIC - 1)),
                              waits=([ev_last_hmul, ev_w2[(e, dq)],
                                      ev_yevac.get(bank), bank_r.get(bank)] if i == 0 else []),
                              inc="t" if i == NIC - 1 else None)
                      ev_bmm = em.ev("t")
                      bank_w[bank] = ev_bmm
                      wcol = T["metg"][:, k * 4: k * 4 + 1]
                      ysl = T["ysc"][:, k * 2048 + dq * 512: k * 2048 + (dq + 1) * 512]
                      ev_ye = em.op("vector", lambda v, bank=bank, ysl=ysl, wcol=wcol:
                                    v.tensor_scalar_mul(
                                        out=ysl, in0=pb[bank][:, :], scalar1=wcol),
                                    waits=[ev_bmm, ev_metg[k],
                                           ysc_free.get(k) if dq == 0 else None], inc="v")
                      ev_yevac[bank] = ev_ye
                      bank_r[bank] = ev_ye
                      if dq == 3:
                          metg_free[k] = ev_ye
                          ysc_free[k] = em.dma("gpsimd", lambda g, k=k: g.indirect_dma_start(
                              out=d_yb,
                              out_offset=bass.IndirectOffsetOnAxis(T["dsti"][:, k:k + 1], 0),
                              in_=T["ysc"][:, k * 2048:(k + 1) * 2048], in_offset=None,
                              bounds_check=BCREG["y"], oob_is_err=False),
                              waits=[ev_ye, ev_ybz], inc="ldw")
                          ev_yscat = ysc_free[k]
                  ev_w2_free[par] = em.ev("t")
                  if dq + 2 < 4:
                      load_w2(e, dq + 2)
                  elif e + 1 < NEXP:
                      load_w2(e + 1, dq + 2 - 4)
          ev_hst_readers = em.ev("t")

          if dbg and e == 0:
              c1 = dbg_cap("dbg_hst", T["hst"][:, :], [ev_last_hmul])
              c2 = dbg_cap("dbg_yac0", T["yac"][:, 0:2048], [ev_yevac[0], ev_yevac[1],
                                                             ev_yevac[2], ev_yevac[3]])
              dbg_y_hold = ("dbgst", max(c1[1], c2[1]))

      ev_store = [None] * NSLAB
      yf_free = [None, None]
      ga_free = None
      for s in range(NSLAB):
          par = s % 2
          if SPARSE:
              yf = T["yfull"][:, par * 2048:(par + 1) * 2048]
              l1 = em.dma("sync", lambda e_, s=s, yf=yf: e_.dma_start(
                  out=yf, in_=d_yb[2 * TOKC + s * 128: 2 * TOKC + (s + 1) * 128, :]),
                  waits=[ev_ysp[s], yf_free[par]], inc="ldx")
              l2 = em.dma("sync", lambda e_, s=s: e_.dma_start(
                  out=T["gsl"][:, 0:2048], in_=d_yb[s * 128:(s + 1) * 128, :]),
                  waits=[ev_yscat, ga_free, gsl_free.get(0), gsl_free.get(1)], inc="ldx")
              l3 = em.dma("sync", lambda e_, s=s: e_.dma_start(
                  out=T["gsl"][:, 2048:4096],
                  in_=d_yb[TOKC + s * 128: TOKC + (s + 1) * 128, :]), inc="ldx")
              ev_a1 = em.op("vector", lambda v, yf=yf: v.tensor_add(
                  out=yf, in0=yf, in1=T["gsl"][:, 0:2048]), waits=[l1, l2])
              ev_a2 = em.op("vector", lambda v, yf=yf: v.tensor_add(
                  out=yf, in0=yf, in1=T["gsl"][:, 2048:4096]), waits=[l3], inc="v")
              ga_free = ev_a2
              ev_ln5 = ln_to_scr(yf, [ev_a2])
              yf_free[par] = ev_ln5
          else:
              ysl = T["yac"][:, s * 2048:(s + 1) * 2048]
              ev_ln5 = ln_to_scr(ysl, [ev_yevac[0], ev_yevac[1], ev_yevac[2], ev_yevac[3]])
          ev_xl = em.dma("sync", lambda e_, s=s, par=par: e_.dma_start(
              out=T["xslab"][:, par * 2048:(par + 1) * 2048],
              in_=d_x2[s * 128:(s + 1) * 128, :]),
              waits=[ev_gate[NSLAB - 1], ev_store[s - 2] if s >= 2 else None], inc="ldx")
          X = T["xslab"][:, par * 2048:(par + 1) * 2048]
          O = T["oslab"][:, par * 2048:(par + 1) * 2048]
          ev_o = em.op("vector", lambda v, X=X, O=O: v.tensor_add(
              out=O, in0=X, in1=T["scr"][:, :]),
              waits=[ev_ln5, ev_xl,
                     ev_store[s - 2] if s >= 2 else ev_ofree[NSLAB - 2 + par]], inc="v")
          scr_guard[0] = [ev_o]
          ev_store[s] = em.dma("sync", lambda e_, s=s, O=O: e_.dma_start(
              out=d_out[s * 128:(s + 1) * 128, :], in_=O), waits=[ev_o], inc="st")

      if dbg:
          em.dma("gpsimd", lambda g: g.dma_start(out=dbg_outs["dbg_x2"], in_=d_x2),
                 waits=[ev_spill[NSLAB - 1]], inc="dbgst")
          em.op("gpsimd", lambda g: g.engine_nop(), waits=[("dbgst", em.cnt["dbgst"])])

    final_st = em.ev("st")

    with contextlib.ExitStack() as stk:
        sems = {r: stk.enter_context(nc.semaphore(r)) for r in SEM_ROLES}
        block = stk.enter_context(nc.Block())

        def replay(name):
            def body(eng):
                if name == "gpsimd" and SPARSE:
                    BCREG["g"] = eng.to_reg(GROWS - 1)
                    BCREG["y"] = eng.to_reg(2 * TOKC - 1)
                for (fn, w, inc, by) in em.q[name]:
                    for (role, val) in w:
                        eng.wait_ge(sems[role], val)
                    inst = fn(eng)
                    if inc is not None:
                        inst.then_inc(sems[inc], by)
                if name == "sync":
                    eng.wait_ge(sems["st"], final_st[1])
            return body

        nc._em = em
        block.sync(replay("sync"))
        block.gpsimd(replay("gpsimd"))
        block.tensor(replay("tensor"))
        block.vector(replay("vector"))
        block.scalar(replay("scalar"))

    return nc


def kernel(**inputs):
    dbg = bool(inputs.pop("_dbg", False))
    g = host_layout(inputs)
    nc = build_nc(dbg=dbg)
    shared = {k: g[k] for k in ["wq", "wk", "wv", "bqc", "bkc", "bvc", "brow", "wout",
                                "bout2", "gate", "w1_all", "w3_all", "w2_all"]}
    in_maps = []
    for c in range(NCORE):
        m = {"x_in": g["x_shards"][c]}
        m.update(shared)
        in_maps.append(m)
    res = run_bass_kernel_spmd(nc, in_maps, core_ids=list(range(NCORE)))
    out = np.concatenate([np.asarray(res.results[c]["out"]) for c in range(NCORE)], axis=0)
    if dbg:
        return out.reshape(1, 8192, D), res
    return out.reshape(1, 8192, D)
